# version 02
# baseline (speedup 1.0000x reference)
from contextlib import ExitStack
import os
import numpy as np
import ml_dtypes
import concourse.bass as bass
import concourse.mybir as mybir
from concourse.alu_op_type import AluOpType as ALU
from concourse.bass_utils import run_bass_kernel_spmd

F32 = mybir.dt.float32
BF16 = mybir.dt.bfloat16
AF = mybir.ActivationFunctionType

ENGINES = ("tensor", "vector", "scalar", "gpsimd", "sync")

L = 2048
D = 1024
DI = 2048
NST = 16
DTR = 64
CH = 512
NCH = L // CH
NKT = D // 128
NDT = DI // 128

UPTO = os.environ.get("MK_UPTO", "ffn")
N_MAMBA_CHUNKS = int(os.environ.get("MK_CHUNKS", str(NCH)))
SELFWAIT_ENGINES = ("vector", "scalar")
MK_SELFWAIT = os.environ.get("MK_SELFWAIT", "1") == "1"
MK_RACEDET = os.environ.get("MK_RACEDET", "0") == "1"


class MK:

    def __init__(self, nc, ctx):
        self.nc = nc
        self.ctx = ctx
        self.stream = {e: [] for e in ENGINES}
        self.ninstr = 0
        self.nsem = 0
        self.prog = {}
        self.progval = {}
        self.dmaval = {}
        for e in ENGINES:
            self.prog[e] = self._new_sem("prog_" + e)
            self.progval[e] = 0

    def _new_sem(self, name):
        self.nsem += 1
        return self.ctx.enter_context(self.nc.semaphore(name))

    def dma_sem(self, name):
        s = self._new_sem(name)
        self.dmaval[id(s)] = 0
        return s

    def op(self, eng, fn, tick=True):
        if MK_SELFWAIT and eng in SELFWAIT_ENGINES and self.progval[eng] > 0:
            self.stream[eng].append(
                ("wait", self.prog[eng], self.progval[eng]))
        if tick:
            self.progval[eng] += 1
            inc = (self.prog[eng], 1)
        else:
            inc = None
        self.stream[eng].append(("op", fn, inc))
        self.ninstr += 1
        return (self.prog[eng], self.progval[eng])

    def dma(self, eng, fn, sem):
        if MK_SELFWAIT and eng in SELFWAIT_ENGINES and self.progval[eng] > 0:
            self.stream[eng].append(
                ("wait", self.prog[eng], self.progval[eng]))
        self.dmaval[id(sem)] += 16
        self.stream[eng].append(("op", fn, (sem, 16)))
        self.ninstr += 1
        return (sem, self.dmaval[id(sem)])

    def mark(self, eng):
        return (self.prog[eng], self.progval[eng])

    def wait_mark(self, eng, mark):
        sem, v = mark
        if v <= 0:
            return
        self.stream[eng].append(("wait", sem, v))

    def replay(self, block):
        def mk_runner(eng_name):
            def run(eng):
                for item in self.stream[eng_name]:
                    if item[0] == "op":
                        _, fn, inc = item
                        inst = fn(eng)
                        if inc is not None:
                            inst.then_inc(*inc)
                    else:
                        _, sem, v = item
                        eng.wait_ge(sem, v)
            return run
        block.tensor(mk_runner("tensor"))
        block.vector(mk_runner("vector"))
        block.scalar(mk_runner("scalar"))
        block.gpsimd(mk_runner("gpsimd"))
        block.sync(mk_runner("sync"))

    def check_deadlock(self):
        pos = {e: 0 for e in ENGINES}
        cur = {}
        progress = True
        while progress:
            progress = False
            for e in ENGINES:
                while pos[e] < len(self.stream[e]):
                    item = self.stream[e][pos[e]]
                    if item[0] == "op":
                        if item[2] is not None:
                            s, n = item[2]
                            cur[id(s)] = cur.get(id(s), 0) + n
                        pos[e] += 1
                        progress = True
                    else:
                        _, sem, v = item
                        if cur.get(id(sem), 0) >= v:
                            pos[e] += 1
                            progress = True
                        else:
                            break
        stuck = [(e, pos[e], len(self.stream[e])) for e in ENGINES
                 if pos[e] < len(self.stream[e])]
        if stuck:
            det = "; ".join(f"{e}@{p}/{n}" for e, p, n in stuck)
            raise RuntimeError("DEADLOCK in recorded program: " + det)


class Ring:

    def __init__(self, mk, n):
        self.mk, self.n = mk, n
        self.p = 0
        self.rels = []
        self.ready = []

    def acq(self, eng):
        i = self.p % self.n
        if self.p >= self.n:
            self.mk.wait_mark(eng, self.rels[self.p - self.n])
        self.p += 1
        return i

    def fin(self, mark):
        self.ready.append(mark)

    def rdy(self, eng, k=None):
        k = self.p - 1 if k is None else k
        self.mk.wait_mark(eng, self.ready[k])

    def rel(self, mark):
        self.rels.append(mark)


class Arena:
    def __init__(self, nc, ctx, nbytes):
        self.nc = nc
        self.nbytes = nbytes
        self.base_handle = ctx.enter_context(
            nc.sbuf_tensor("arena", [128, nbytes], mybir.dt.uint8))
        self.base_addr = nc.lookup_mloc(self.base_handle).addr
        self.off = 0
        self.stack = []
        self.peak = 0
        self.cnt = 0

    def alloc(self, shape, dtype, name=None):
        per_part = int(np.prod(shape[1:])) * mybir.dt.size(dtype)
        off = (self.off + 31) // 32 * 32
        assert off + per_part <= self.nbytes, (
            f"SBUF arena overflow: {off}+{per_part} > {self.nbytes} ({name})")
        self.off = off + per_part
        self.peak = max(self.peak, self.off)
        self.cnt += 1
        nm = name or f"ar{self.cnt}"
        h = self.nc.virtual_sbuf_tensor(nm, list(shape), dtype)
        self.nc._tensor(h.name, list(shape), dtype, type="SB")
        mloc = self.nc.lookup_mloc(h)
        mloc.allocated = not self.nc.target_bir_lowering
        mloc.addr = self.base_addr + off
        return h

    def push(self):
        self.stack.append(self.off)

    def pop(self):
        self.off = self.stack.pop()

    def at(self, shape, dtype, byte_off, name=None):
        per_part = int(np.prod(shape[1:])) * mybir.dt.size(dtype)
        assert byte_off + per_part <= self.nbytes
        self.cnt += 1
        nm = name or f"ar{self.cnt}"
        h = self.nc.virtual_sbuf_tensor(nm, list(shape), dtype)
        self.nc._tensor(h.name, list(shape), dtype, type="SB")
        mloc = self.nc.lookup_mloc(h)
        mloc.allocated = not self.nc.target_bir_lowering
        mloc.addr = self.base_addr + byte_off
        return h


class PsumPool:
    def __init__(self, nc, ctx):
        self.nc = nc
        self.base = ctx.enter_context(
            nc.psum_tensor("psarena", [128, 16384], mybir.dt.uint8))
        self.base_addr = nc.lookup_mloc(self.base).addr
        self.cnt = 0
        self.cache = {}

    def bank(self, idx, dtype, cols, name=None, nbanks=1):
        assert 0 <= idx and idx + nbanks <= 8
        assert cols * mybir.dt.size(dtype) <= 2048 * nbanks
        key = (idx, dtype, cols)
        if key in self.cache:
            return self.cache[key]
        self.cnt += 1
        nm = name or f"ps{self.cnt}"
        h = self.nc.virtual_psum_tensor(nm, [128, cols], dtype)
        self.nc._tensor(h.name, [128, cols], dtype, type="PSUM")
        mloc = self.nc.lookup_mloc(h)
        mloc.allocated = not self.nc.target_bir_lowering
        mloc.addr = self.base_addr + idx * 2048
        self.cache[key] = h
        return h


def build_program(dbg_taps=(), upto=UPTO):
    nc = bass.Bass("TRN2", target_bir_lowering=False,
                   detect_race_conditions=MK_RACEDET)
    taps_requested = set(dbg_taps)
    taps_declared = []

    def din(name, shape, dtype=F32):
        return nc.dram_tensor(name, list(shape), dtype,
                              kind="ExternalInput").ap()

    x_in = din("x", [L, D])
    m_inproj = din("m_inproj", [2 * DI, D])
    m_convw = din("m_convw", [DI, 4])
    m_convb = din("m_convb", [DI, 1])
    m_xproj = din("m_xproj", [DTR + 2 * NST, DI])
    m_dtw = din("m_dtw", [DI, DTR])
    m_dtb = din("m_dtb", [DI, 1])
    m_alog = din("m_alog", [DI, NST])
    m_dvec = din("m_dvec", [DI, 1])
    m_outproj = din("m_outproj", [D, DI])
    c_ones_col = din("c_ones_col", [128, 1], BF16)
    c_ones_colf = din("c_ones_colf", [128, 1], F32)
    c_ones_row = din("c_ones_row", [1, 128], BF16)
    c_eye = din("c_eye", [128, 128], BF16)
    c_eye_f32 = din("c_eye_f32", [128, 128], F32)
    c_sel16 = din("c_sel16", [16, 16 * 128], BF16)
    c_scal = din("c_scal", [128, 8])
    c_ones512 = din("c_ones512", [1, 512], BF16)
    c_ones_rowf = din("c_ones_rowf", [1, 128], F32)
    e_wr = din("e_wr", [D, 8])
    e_br = din("e_br", [8, 1])
    e_w1 = din("e_w1", [8, D, 4096])
    e_b1 = din("e_b1", [8 * 4096, 1])
    e_w2 = din("e_w2", [8, 4096, D])
    e_b2 = din("e_b2", [8, D])
    a_wq = din("a_wq", [D, D])
    a_wk = din("a_wk", [D, D])
    a_wv = din("a_wv", [2 * D, D])
    a_lq1 = din("a_lq1", [64, 1])
    a_lq2 = din("a_lq2", [64, 1])
    a_lk1 = din("a_lk1", [64, 1])
    a_lk2 = din("a_lk2", [64, 1])
    a_wproj = din("a_wproj", [2 * D, 2 * D])
    a_alpha = din("a_alpha", [1, 1])
    c_cs1 = din("c_cs1", [128, L], BF16)
    c_cs2 = din("c_cs2", [128, L], BF16)
    c_hmask = din("c_hmask", [128, 4], BF16)
    f_w1 = din("f_w1", [4096, D])
    f_w1s = din("f_w1s", [4096, 1])
    f_w2 = din("f_w2", [4096, D])
    f_w2s = din("f_w2s", [4096, 1])
    f_h1 = din("f_h1", [4096, 4096])
    f_h1s = din("f_h1s", [4096, 1])
    f_h2 = din("f_h2", [4096, 4096])
    f_h2s = din("f_h2s", [4096, 1])
    f_w3 = din("f_w3", [D, 4096])
    f_w3s = din("f_w3s", [D, 1])

    y_out = nc.dram_tensor("y", [L, D], F32, kind="ExternalOutput").ap()

    def dscr(name, shape, dtype):
        return nc.dram_tensor(name, list(shape), dtype, kind="Internal").ap()

    pk_inproj = dscr("pk_inproj", [32, 128, 1024], BF16)
    pk_outproj = dscr("pk_outproj", [8, 128, 2048], BF16)
    pk_x = dscr("pk_x", [128, NKT * L], F32)
    pk_wq = dscr("pk_wq", [8, 128, 1024], BF16)
    pk_wk = dscr("pk_wk", [8, 128, 1024], BF16)
    pk_wv = dscr("pk_wv", [16, 128, 1024], BF16)
    pk_wproj = dscr("pk_wproj", [16, 128, 2048], BF16)
    pk_q1 = dscr("pk_q1", [512, L], BF16)
    pk_q2 = dscr("pk_q2", [512, L], BF16)
    pk_k1 = dscr("pk_k1", [512, L], BF16)
    pk_k2 = dscr("pk_k2", [512, L], BF16)
    pk_fw1 = dscr("pk_fw1", [32, 128, 1024], BF16)
    pk_fw2 = dscr("pk_fw2", [32, 128, 1024], BF16)
    pk_fh1 = dscr("pk_fh1", [32, 128, 4096], BF16)
    pk_fh2 = dscr("pk_fh2", [32, 128, 4096], BF16)
    pk_fw3 = dscr("pk_fw3", [8, 128, 4096], BF16)
    pk_out2 = dscr("pk_out2", [16, 128, 2048], BF16)
    pk_v = dscr("pk_v", [16, 128, 2048], BF16)

    ctx = ExitStack()
    with ctx:
        ar = Arena(nc, ctx, 207 * 1024)
        ps = PsumPool(nc, ctx)
        mk = MK(nc, ctx)

        def barrier():
            mk.op("sync", lambda s: s.nop())
            marks = {e: mk.mark(e) for e in ENGINES}
            for e in ENGINES:
                for o in ENGINES:
                    if o != e:
                        mk.wait_mark(e, marks[o])

        sem_store = mk.dma_sem("store")
        sem_dmac = mk.dma_sem("dmac")

        def tap(name, ap, mark):
            if name not in taps_requested:
                return
            t = nc.dram_tensor("dbg_" + name, list(ap.shape), ap.dtype,
                               kind="ExternalOutput").ap()
            taps_declared.append(name)
            mk.wait_mark("sync", mark)
            mk.dma("sync", lambda s, a=ap, t=t: s.dma_start(out=t, in_=a),
                   sem_store)

        x_fm = ar.alloc([128, NKT * L], F32, "x_fm")
        ones_col = ar.alloc([128, 1], BF16, "ones_col")
        ones_colf = ar.alloc([128, 1], F32, "ones_colf")
        ones_row = ar.alloc([1, 128], BF16, "ones_row")
        eye_bf = ar.alloc([128, 128], BF16, "eye_bf")
        eye_f32 = ar.alloc([128, 128], F32, "eye_f32")
        sel16 = ar.alloc([16, 16 * 128], BF16, "sel16")
        ones512 = ar.alloc([1, 512], BF16, "ones512")
        ones_rowf = ar.alloc([1, 128], F32, "ones_rowf")
        scal_sb = ar.alloc([128, 8], F32, "scal")
        convw_sb = ar.alloc([128, NDT * 4], F32, "convw")
        convb_sb = ar.alloc([128, NDT], F32, "convb")
        dtb_sb = ar.alloc([128, NDT], F32, "dtb")
        dvec_sb = ar.alloc([128, NDT], F32, "dvec")
        alog_sb = ar.alloc([128, NDT * NST], F32, "alog")
        xprojT = ar.alloc([128, 16 * 96], BF16, "xprojT")
        dtwT = ar.alloc([64, DI], BF16, "dtwT")
        rows = ar.alloc([128, 1536], F32, "rows")
        rows_bf = ar.alloc([128, 1024], BF16, "rows_bf")

        def load_small(dst_ap, src_ap):
            mk.dma("sync", lambda s, d=dst_ap, i=src_ap:
                   s.dma_start(out=d, in_=i), sem_dmac)

        load_small(ones_col[:, :], c_ones_col)
        load_small(ones_colf[:, :], c_ones_colf)
        load_small(ones_row[:, :], c_ones_row)
        load_small(eye_bf[:, :], c_eye)
        load_small(eye_f32[:, :], c_eye_f32)
        load_small(sel16[:, :], c_sel16)
        load_small(scal_sb[:, :], c_scal)
        load_small(ones512[:, :], c_ones512)
        load_small(ones_rowf[:, :], c_ones_rowf)
        for t_ in range(NDT):
            load_small(convw_sb[:, t_ * 4:(t_ + 1) * 4],
                       m_convw[t_ * 128:(t_ + 1) * 128, :])
            load_small(convb_sb[:, t_:t_ + 1],
                       m_convb[t_ * 128:(t_ + 1) * 128, :])
            load_small(dtb_sb[:, t_:t_ + 1],
                       m_dtb[t_ * 128:(t_ + 1) * 128, :])
            load_small(dvec_sb[:, t_:t_ + 1],
                       m_dvec[t_ * 128:(t_ + 1) * 128, :])
            load_small(alog_sb[:, t_ * NST:(t_ + 1) * NST],
                       m_alog[t_ * 128:(t_ + 1) * 128, :])
        v_small = (sem_dmac, mk.dmaval[id(sem_dmac)])

        mk.wait_mark("scalar", v_small)
        mA = mk.op("scalar", lambda s:
                   s.activation(alog_sb[:, :], alog_sb[:, :], AF.Exp))
        mk.wait_mark("vector", mA)
        mk.op("vector", lambda v:
              v.tensor_scalar(alog_sb[:, :], alog_sb[:, :],
                              -1.0, None, ALU.mult))

        ar.push()
        xtok = ar.alloc([128, 2 * D], F32, "xtok")
        TPF = [ps.bank(0, F32, 512, "TPF0"), ps.bank(1, F32, 512, "TPF1")]
        sem_xtok = [mk.dma_sem(f"xtok{i}") for i in range(2)]
        rg_xtok = Ring(mk, 2)
        rg_tpf = Ring(mk, 2)

        for lt in range(16):
            si = rg_xtok.acq("sync")
            buf = xtok[:, si * D:si * D + D]
            mdma = mk.dma("sync", lambda s, b=buf, lt=lt:
                          s.dma_start(out=b,
                                      in_=x_in[lt * 128:(lt + 1) * 128, :]),
                          sem_xtok[si])
            rg_xtok.fin(mdma)
            rg_xtok.rdy("tensor")
            if lt == 0:
                mk.wait_mark("tensor", v_small)
            for kt in range(NKT):
                bi = rg_tpf.acq("tensor")
                pb = TPF[bi]
                mt = mk.op("tensor", lambda t, b=buf, kt=kt, pb=pb:
                           t.transpose(pb[:, 0:128],
                                       b[:, kt * 128:(kt + 1) * 128],
                                       eye_f32[:, :]))
                rg_tpf.fin(mt)
                rg_tpf.rdy("vector")
                mv = mk.op("vector", lambda v, kt=kt, lt=lt, pb=pb:
                           v.tensor_copy(
                               x_fm[:, kt * L + lt * 128:
                                    kt * L + (lt + 1) * 128],
                               pb[:, 0:128]))
                rg_tpf.rel(mv)
            rg_xtok.rel(mt)
        m_xfm = mv

        wnat = ar.alloc([128, 2 * 1024], BF16, "wnat")
        wstg = ar.alloc([128, 2 * 1024], BF16, "wstg")
        TPS = ps.bank(6, BF16, 2048, "TPS", nbanks=2)
        sem_wnat = [mk.dma_sem(f"wnat{i}") for i in range(2)]
        sem_park = [mk.dma_sem(f"park{i}") for i in range(2)]
        sem_park2 = mk.dma_sem("park2")
        rg_wnat = Ring(mk, 2)
        rg_tps = Ring(mk, 1)
        rg_wstg = Ring(mk, 2)

        def park_matrix(w_hbm, nrows, pk, ncols):
            kchunks = ncols // 128
            for rt in range(nrows // 128):
                ni = rg_wnat.acq("gpsimd")
                nat = wnat[:, ni * 1024:ni * 1024 + ncols]
                md = mk.dma("gpsimd", lambda g, nat=nat, rt=rt:
                            g.dma_start(
                                out=nat,
                                in_=w_hbm[rt * 128:(rt + 1) * 128, :]),
                            sem_wnat[ni])
                rg_wnat.fin(md)
                rg_wnat.rdy("tensor")
                rg_tps.acq("tensor")
                for kc in range(kchunks):
                    mt = mk.op("tensor", lambda t, kc=kc, nat=nat:
                               t.transpose(TPS[:, kc * 128:(kc + 1) * 128],
                                           nat[:, kc * 128:(kc + 1) * 128],
                                           eye_bf[:, :]))
                rg_tps.fin(mt)
                rg_wnat.rel(mt)
                sgi = rg_wstg.acq("scalar")
                stg = wstg[:, sgi * 1024:sgi * 1024 + ncols]
                rg_tps.rdy("scalar")
                ms = mk.op("scalar", lambda s, stg=stg, kchunks=kchunks:
                           s.copy(stg, TPS[:, 0:kchunks * 128]))
                rg_tps.rel(ms)
                rg_wstg.fin(ms)
                rg_wstg.rdy("sync")
                mp = mk.dma("sync", lambda s, stg=stg, rt=rt, pk=pk:
                            s.dma_start(out=pk[rt], in_=stg), sem_park[sgi])
                rg_wstg.rel(mp)

        mk.wait_mark("gpsimd", v_small)
        park_matrix(m_inproj, 2 * DI, pk_inproj, D)

        wnat2 = ar.alloc([128, 2048], BF16, "wnat2")
        wstg2 = ar.alloc([128, 2048], BF16, "wstg2")
        sem_wnat2 = mk.dma_sem("wnat2")
        rg_wnat2 = Ring(mk, 1)
        rg_wstg2 = Ring(mk, 1)

        def park_matrix2(w_hbm, nrows, pk):
            for rt in range(nrows // 128):
                rg_wnat2.acq("gpsimd")
                md = mk.dma("gpsimd", lambda g, rt=rt, w_hbm=w_hbm:
                            g.dma_start(
                                out=wnat2[:, :],
                                in_=w_hbm[rt * 128:(rt + 1) * 128, :]),
                            sem_wnat2)
                rg_wnat2.fin(md)
                rg_wnat2.rdy("tensor")
                rg_tps.acq("tensor")
                for kc in range(16):
                    mt = mk.op("tensor", lambda t, kc=kc:
                               t.transpose(TPS[:, kc * 128:(kc + 1) * 128],
                                           wnat2[:, kc * 128:(kc + 1) * 128],
                                           eye_bf[:, :]))
                rg_tps.fin(mt)
                rg_wnat2.rel(mt)
                rg_wstg2.acq("scalar")
                rg_tps.rdy("scalar")
                ms = mk.op("scalar", lambda s: s.copy(wstg2[:, :], TPS[:, :]))
                rg_tps.rel(ms)
                rg_wstg2.fin(ms)
                rg_wstg2.rdy("sync")
                mp = mk.dma("sync", lambda s, rt=rt, pk=pk:
                            s.dma_start(out=pk[rt], in_=wstg2[:, :]),
                            sem_park2)
                rg_wstg2.rel(mp)

        park_matrix2(m_outproj, D, pk_outproj)

        wnat4 = ar.alloc([128, 4096], BF16, "wnat4")
        wstg4 = ar.alloc([128, 4096], BF16, "wstg4")
        sem_wnat4 = mk.dma_sem("wnat4")
        rg_wnat4 = Ring(mk, 1)
        rg_wstg4 = Ring(mk, 1)

        def park_matrix4(w_hbm, nrows, pk):
            for rt in range(nrows // 128):
                rg_wnat4.acq("gpsimd")
                md = mk.dma("gpsimd", lambda g, rt=rt, w_hbm=w_hbm:
                            g.dma_start(
                                out=wnat4[:, :],
                                in_=w_hbm[rt * 128:(rt + 1) * 128, :]),
                            sem_wnat4)
                rg_wnat4.fin(md)
                rg_wnat4.rdy("tensor")
                rg_wstg4.acq("scalar")
                for grp in range(2):
                    rg_tps.acq("tensor")
                    for kc in range(16):
                        mt = mk.op("tensor", lambda t, kc=kc, grp=grp:
                                   t.transpose(
                                       TPS[:, kc * 128:(kc + 1) * 128],
                                       wnat4[:, grp * 2048 + kc * 128:
                                             grp * 2048 + (kc + 1) * 128],
                                       eye_bf[:, :]))
                    rg_tps.fin(mt)
                    if grp == 1:
                        rg_wnat4.rel(mt)
                    rg_tps.rdy("scalar")
                    ms = mk.op("scalar", lambda s, grp=grp:
                               s.copy(wstg4[:, grp * 2048:(grp + 1) * 2048],
                                      TPS[:, :]))
                    rg_tps.rel(ms)
                rg_wstg4.fin(ms)
                rg_wstg4.rdy("sync")
                mp = mk.dma("sync", lambda s, rt=rt, pk=pk:
                            s.dma_start(out=pk[rt], in_=wstg4[:, :]),
                            sem_park2)
                rg_wstg4.rel(mp)

        if upto in ("attn", "ffn"):
            park_matrix2(a_wproj, 2 * D, pk_wproj)
            park_matrix(a_wq, D, pk_wq, D)
            park_matrix(a_wk, D, pk_wk, D)
            park_matrix(a_wv, 2 * D, pk_wv, D)
        if upto == "ffn":
            park_matrix(f_w1, 4096, pk_fw1, D)
            park_matrix(f_w2, 4096, pk_fw2, D)
            park_matrix4(f_h1, 4096, pk_fh1)
            park_matrix4(f_h2, 4096, pk_fh2)
            park_matrix4(f_w3, D, pk_fw3)

        xpn = ar.alloc([96, DI], BF16, "xpn")
        sem_xpn = mk.dma_sem("xpn")
        mk.dma("gpsimd", lambda g: g.dma_start(out=xpn[:, :], in_=m_xproj),
               sem_xpn)
        mk.wait_mark("tensor", (sem_xpn, 16))
        for et in range(16):
            rg_tps.acq("tensor")
            mt = mk.op("tensor", lambda t, et=et:
                       t.transpose(TPS[:, 0:96],
                                   xpn[:, et * 128:(et + 1) * 128],
                                   eye_bf[0:96, 0:96]))
            rg_tps.fin(mt)
            rg_tps.rdy("scalar")
            ms = mk.op("scalar", lambda s, et=et:
                       s.copy(xprojT[:, et * 96:(et + 1) * 96], TPS[:, 0:96]))
            rg_tps.rel(ms)
        dtn = ar.alloc([128, 16 * 64], BF16, "dtn")
        sem_dtn = mk.dma_sem("dtn")
        for t_ in range(16):
            mk.dma("gpsimd", lambda g, t_=t_:
                   g.dma_start(out=dtn[:, t_ * 64:(t_ + 1) * 64],
                               in_=m_dtw[t_ * 128:(t_ + 1) * 128, :]),
                   sem_dtn)
        mk.wait_mark("tensor", (sem_dtn, 256))
        for et in range(16):
            rg_tps.acq("tensor")
            mt = mk.op("tensor", lambda t, et=et:
                       t.transpose(TPS[0:64, 0:128],
                                   dtn[:, et * 64:(et + 1) * 64],
                                   eye_bf[:, :]))
            rg_tps.fin(mt)
            rg_tps.rdy("scalar")
            ms = mk.op("scalar", lambda s, et=et:
                       s.copy(dtwT[0:64, et * 128:(et + 1) * 128],
                              TPS[0:64, 0:128]))
            rg_tps.rel(ms)

        for sp_ in sem_park + [sem_park2]:
            mk.wait_mark("sync", (sp_, mk.dmaval[id(sp_)]))
        barrier()
        ar.pop()

        P_LNS = ps.bank(5, F32, 512, "lnsum")
        P_LNQ = ps.bank(2, F32, 512, "lnsq")

        def emit_ln(col0, out_bf, sq_tmp, aaB, bbB, rg_sq,
                    out_off=0, out_stride=512):
            for kt in range(NKT):
                xs = x_fm[:, kt * L + col0:kt * L + col0 + 512]
                qi = rg_sq.acq("scalar")
                sq = sq_tmp[:, qi * 512:qi * 512 + 512]
                msq = mk.op("scalar", lambda s, xs=xs, sq=sq:
                            s.activation(sq, xs, AF.Square))
                rg_sq.fin(msq)
                mk.op("tensor", lambda t, xs=xs, kt=kt:
                      t.matmul(P_LNS[0:1, 0:512], ones_colf[:, :], xs,
                               start=(kt == 0), stop=(kt == NKT - 1)))
                rg_sq.rdy("tensor")
                mq = mk.op("tensor", lambda t, sq=sq, kt=kt:
                           t.matmul(P_LNQ[0:1, 0:512], ones_col[:, :], sq,
                                    start=(kt == 0), stop=(kt == NKT - 1)))
                rg_sq.rel(mq)
            m_stats = mq
            mk.wait_mark("vector", m_stats)
            r_mu = rows[0:1, 0:512]
            r_var = rows[0:1, 512:1024]
            r_rstd = rows[0:1, 1024:1536]
            r_aa, r_bb = rows_bf[0:1, 0:512], rows_bf[0:1, 512:1024]
            mk.op("vector", lambda v: v.tensor_scalar(
                r_mu, P_LNS[0:1, 0:512], 1.0 / D, None, ALU.mult))
            mk.op("vector", lambda v: v.tensor_scalar(
                r_var, P_LNQ[0:1, 0:512], 1.0 / D, None, ALU.mult))
            mk.op("vector", lambda v: v.tensor_tensor(
                r_rstd, r_mu, r_mu, ALU.mult))
            mv = mk.op("vector", lambda v: v.tensor_tensor(
                r_var, r_var, r_rstd, ALU.subtract))
            mk.wait_mark("scalar", mv)
            mk.op("scalar", lambda s: s.activation(
                r_var, r_var, AF.Ln, bias=scal_sb[0:1, 0:1]))
            msc = mk.op("scalar", lambda s: s.activation(
                r_rstd, r_var, AF.Exp, scale=scal_sb[0:1, 1:2]))
            mk.wait_mark("vector", msc)
            mk.op("vector", lambda v: v.tensor_copy(r_aa, r_rstd))
            mv2 = mk.op("vector", lambda v: v.scalar_tensor_tensor(
                r_bb, r_mu, -1.0, r_rstd, ALU.mult, ALU.mult))
            mk.wait_mark("tensor", mv2)
            mt = mk.op("tensor", lambda t: t.matmul(
                P_LNS[:, 0:512], ones_row[:, :], r_aa,
                start=True, stop=True))
            mk.wait_mark("scalar", mt)
            ms = mk.op("scalar", lambda s: s.copy(aaB[:, :], P_LNS[:, 0:512]))
            mk.wait_mark("tensor", ms)
            mt = mk.op("tensor", lambda t: t.matmul(
                P_LNQ[:, 0:512], ones_row[:, :], r_bb,
                start=True, stop=True))
            mk.wait_mark("scalar", mt)
            ms = mk.op("scalar", lambda s: s.copy(bbB[:, :], P_LNQ[:, 0:512]))
            mk.wait_mark("vector", ms)
            for kt in range(NKT):
                xs = x_fm[:, kt * L + col0:kt * L + col0 + 512]
                ob = out_bf[:, kt * out_stride + out_off:
                            kt * out_stride + out_off + 512]
                mk.op("vector", lambda v, xs=xs, ob=ob:
                      v.tensor_tensor(ob, xs, aaB[:, :], ALU.mult))
                mln = mk.op("vector", lambda v, ob=ob:
                            v.tensor_tensor(ob, ob, bbB[:, :], ALU.add))
            return mln

        ar.push()
        xln_c = ar.alloc([128, NKT * 512], BF16, "xln_c")
        sq_tmp = ar.alloc([128, 2 * 512], BF16, "sq_tmp")
        aaB = ar.alloc([128, 512], F32, "aaB")
        bbB = ar.alloc([128, 512], F32, "bbB")
        xin_sil = ar.alloc([128, NDT * 512], BF16, "xin_sil")
        tails = ar.alloc([128, NDT * 4], F32, "tails")
        bbuf = ar.alloc([128, 520], F32, "bbuf")
        xc_t = ar.alloc([128, 2 * 512], BF16, "xc_t")
        dt_bf = ar.alloc([64, 512], BF16, "dt_bf")
        bc_bf = ar.alloc([16, 2 * 512], BF16, "bc_bf")
        bball = ar.alloc([128, 16 * 512], BF16, "bball")
        cball = ar.alloc([128, 16 * 512], BF16, "cball")
        u_t = ar.alloc([128, 512], F32, "u_t")
        sp_t = ar.alloc([128, 512], F32, "sp_t")
        delta_t = ar.alloc([128, 2 * 512], BF16, "delta_t")
        du_t = ar.alloc([128, 2 * 512], BF16, "du_t")
        da_t = ar.alloc([128, 3 * 512], BF16, "da_t")
        dbu_t = ar.alloc([128, 3 * 512], BF16, "dbu_t")
        h_t = ar.alloc([128, 3 * 512], BF16, "h_t")
        cm_t = ar.alloc([128, 3 * 512], BF16, "cm_t")
        hcarry = ar.alloc([128, 512], F32, "hcarry")
        y_sb = ar.alloc([128, NDT * 512], BF16, "y_sb")
        yt_t = ar.alloc([128, 512], F32, "yt_t")
        z_t = ar.alloc([128, 2 * 512], BF16, "z_t")
        wslA = ar.alloc([128, 2 * 1024], BF16, "wslA")
        wslO = ar.alloc([128, 2048], BF16, "wslO")

        MB = [ps.bank(0, F32, 512, "MB0"), ps.bank(1, F32, 512, "MB1")]
        MC = [ps.bank(3, F32, 512, "MC0"), ps.bank(4, F32, 512, "MC1")]
        MY = [ps.bank(6, F32, 512, "MY0"), ps.bank(7, F32, 512, "MY1")]
        rg_mb = Ring(mk, 2)
        rg_mc = Ring(mk, 2)
        rg_my = Ring(mk, 2)
        rg_sq = Ring(mk, 2)
        rg_wslA = Ring(mk, 2)
        sem_wslA = [mk.dma_sem(f"wslA{i}") for i in range(2)]
        rg_wslO = Ring(mk, 1)
        sem_wslO = [mk.dma_sem("wslO0")]
        rg_usp = Ring(mk, 1)
        rg_delta = Ring(mk, 2)
        rg_da = Ring(mk, 3)
        rg_cm = Ring(mk, 3)
        rg_z = Ring(mk, 2)
        rg_xc = Ring(mk, 2)

        mk.op("vector", lambda v: v.memset(hcarry[:, :], 0.0))
        mk.op("vector", lambda v: v.memset(tails[:, :], 0.0))
        mk.wait_mark("tensor", m_xfm)
        mk.wait_mark("scalar", m_xfm)

        for c in range(N_MAMBA_CHUNKS):
            col0 = c * CH
            mln = emit_ln(col0, xln_c, sq_tmp, aaB, bbB, rg_sq)
            mk.wait_mark("tensor", mln)
            tap(f"xln{c}", xln_c[:, :], mln)

            for et in range(16):
                si = rg_wslA.acq("sync")
                dst = wslA[:, si * 1024:si * 1024 + 1024]
                md = mk.dma("sync", lambda s, et=et, dst=dst:
                            s.dma_start(out=dst, in_=pk_inproj[et]),
                            sem_wslA[si])
                rg_wslA.fin(md)
                bi = rg_mb.acq("tensor")
                pb = MB[bi]
                rg_wslA.rdy("tensor")
                for kc in range(NKT):
                    mm = mk.op("tensor", lambda t, pb=pb, kc=kc, dst=dst:
                               t.matmul(pb[:, 0:512],
                                        dst[:, kc * 128:(kc + 1) * 128],
                                        xln_c[:, kc * 512:(kc + 1) * 512],
                                        start=(kc == 0), stop=(kc == NKT - 1)))
                rg_mb.fin(mm)
                rg_wslA.rel(mm)
                dt = et
                bb = bbuf[:, 0:515]
                rg_mb.rdy("vector")
                mk.op("vector", lambda v, bb=bb, dt=dt:
                      v.tensor_copy(bb[:, 0:3], tails[:, dt * 4:dt * 4 + 3]))
                mv = mk.op("vector", lambda v, bb=bb, pb=pb:
                           v.tensor_copy(bb[:, 3:515], pb[:, 0:512]))
                rg_mb.rel(mv)
                mk.op("vector", lambda v, bb=bb, dt=dt:
                      v.tensor_copy(tails[:, dt * 4:dt * 4 + 3],
                                    bb[:, 512:515]))
                xi = rg_xc.acq("vector")
                xs = xc_t[:, xi * 512:xi * 512 + 512]
                mk.op("vector", lambda v, bb=bb, dt=dt, xs=xs:
                      v.tensor_scalar(xs, bb[:, 0:512],
                                      convw_sb[:, dt * 4:dt * 4 + 1], None,
                                      ALU.mult))
                for k in range(1, 4):
                    mxc = mk.op("vector",
                                lambda v, bb=bb, dt=dt, xs=xs, k=k:
                                v.scalar_tensor_tensor(
                                    xs, bb[:, k:k + 512],
                                    convw_sb[:, dt * 4 + k:dt * 4 + k + 1],
                                    xs, ALU.mult, ALU.add))
                rg_xc.fin(mxc)
                rg_xc.rdy("scalar")
                msil = mk.op("scalar", lambda s, dt=dt, xs=xs:
                             s.activation(xin_sil[:, dt * 512:(dt + 1) * 512],
                                          xs, AF.Silu,
                                          bias=convb_sb[:, dt:dt + 1]))
                rg_xc.rel(msil)
            m_xin = msil
            tap(f"xin{c}", xin_sil[:, :], m_xin)
            tap(f"tails{c}", tails[:, :], m_xin)

            mk.wait_mark("tensor", m_xin)
            groups = [(0, 64, dt_bf[:, :]), (64, 16, bc_bf[:, 0:512]),
                      (80, 16, bc_bf[:, 512:1024])]
            for g0, gw, gdst in groups:
                bi = rg_mc.acq("tensor")
                pxd = MC[bi]
                for et in range(16):
                    mm = mk.op("tensor",
                               lambda t, et=et, pxd=pxd, g0=g0, gw=gw:
                               t.matmul(pxd[0:gw, 0:512],
                                        xprojT[:, et * 96 + g0:
                                               et * 96 + g0 + gw],
                                        xin_sil[:, et * 512:(et + 1) * 512],
                                        start=(et == 0), stop=(et == 15)))
                rg_mc.fin(mm)
                rg_mc.rdy("vector")
                mv = mk.op("vector", lambda v, gdst=gdst, pxd=pxd, gw=gw:
                           v.tensor_copy(gdst, pxd[0:gw, 0:512]))
                rg_mc.rel(mv)
            m_bcrows = mv
            tap(f"xdbl{c}", bc_bf[:, :], m_bcrows)

            mk.wait_mark("tensor", m_bcrows)
            for n in range(NST):
                for which, bcdst in ((0, bball), (1, cball)):
                    bi = rg_mc.acq("tensor")
                    pb = MC[bi]
                    mm = mk.op("tensor", lambda t, n=n, pb=pb, which=which:
                               t.matmul(pb[:, 0:512],
                                        sel16[:, n * 128:(n + 1) * 128],
                                        bc_bf[:, which * 512:
                                              which * 512 + 512],
                                        start=True, stop=True))
                    rg_mc.fin(mm)
                    rg_mc.rdy("scalar")
                    ms = mk.op("scalar", lambda s, n=n, pb=pb, bcdst=bcdst:
                               s.copy(bcdst[:, n * 512:(n + 1) * 512],
                                      pb[:, 0:512]))
                    rg_mc.rel(ms)
            m_bcall = ms
            mk.wait_mark("vector", m_bcall)

            for dt in range(NDT):
                bi = rg_mc.acq("tensor")
                pdt = MC[bi]
                mm = mk.op("tensor", lambda t, dt=dt, pdt=pdt:
                           t.matmul(pdt[:, 0:512],
                                    dtwT[:, dt * 128:(dt + 1) * 128],
                                    dt_bf[:, :], start=True, stop=True))
                rg_mc.fin(mm)
                rg_usp.acq("scalar")
                u = u_t[:, 0:512]
                sp = sp_t[:, 0:512]
                rg_mc.rdy("scalar")
                msp0 = mk.op("scalar", lambda s, u=u, pdt=pdt, dt=dt:
                             s.activation(u, pdt[:, 0:512], AF.Identity,
                                          bias=dtb_sb[:, dt:dt + 1]))
                rg_mc.rel(msp0)
                mk.op("scalar", lambda s, u=u, sp=sp:
                      s.activation(sp, u, AF.Abs))
                mk.op("scalar", lambda s, sp=sp:
                      s.activation(sp, sp, AF.Exp, scale=-1.0))
                msp = mk.op("scalar", lambda s, sp=sp:
                            s.activation(sp, sp, AF.Ln, bias=1.0))
                rg_usp.fin(msp)
                di = rg_delta.acq("vector")
                dl = delta_t[:, di * 512:di * 512 + 512]
                du = du_t[:, di * 512:di * 512 + 512]
                rg_usp.rdy("vector")
                mdl0 = mk.op("vector", lambda v, u=u, sp=sp, dl=dl:
                             v.scalar_tensor_tensor(dl, u, 0.0, sp, ALU.max,
                                                    ALU.add))
                rg_usp.rel(mdl0)
                mdl = mk.op("vector", lambda v, dl=dl, du=du, dt=dt:
                            v.tensor_tensor(
                                du, dl,
                                xin_sil[:, dt * 512:(dt + 1) * 512],
                                ALU.mult))
                rg_delta.fin(mdl)
                rg_delta.rdy("scalar")
                yi = rg_my.acq("tensor")
                py = MY[yi]
                for n in range(NST):
                    idx = dt * NST + n
                    ai = rg_da.acq("scalar")
                    da = da_t[:, ai * 512:ai * 512 + 512]
                    mda = mk.op("scalar", lambda s, da=da, dl=dl, idx=idx:
                                s.activation(da, dl, AF.Exp,
                                             scale=alog_sb[:, idx:idx + 1]))
                    rg_da.fin(mda)
                    if n == NST - 1:
                        rg_delta.rel(mda)
                    ci = rg_cm.acq("vector")
                    dbu = dbu_t[:, ci * 512:ci * 512 + 512]
                    hh = h_t[:, ci * 512:ci * 512 + 512]
                    cm = cm_t[:, ci * 512:ci * 512 + 512]
                    mk.op("vector", lambda v, dbu=dbu, du=du, n=n:
                          v.tensor_tensor(dbu, du,
                                          bball[:, n * 512:(n + 1) * 512],
                                          ALU.mult))
                    rg_da.rdy("vector")
                    msc = mk.op("vector",
                                lambda v, hh=hh, da=da, dbu=dbu, idx=idx:
                                v.tensor_tensor_scan(
                                    hh, da, dbu,
                                    hcarry[:, 2 * idx + 1:2 * idx + 2],
                                    ALU.mult, ALU.add))
                    rg_da.rel(msc)
                    mk.op("vector", lambda v, hh=hh, idx=idx:
                          v.tensor_copy(hcarry[:, 2 * idx:2 * idx + 2],
                                        hh[:, 510:512]))
                    mcm = mk.op("vector", lambda v, cm=cm, hh=hh, n=n:
                                v.tensor_tensor(
                                    cm, hh,
                                    cball[:, n * 512:(n + 1) * 512],
                                    ALU.mult))
                    rg_cm.fin(mcm)
                    rg_cm.rdy("tensor")
                    mns = mk.op("tensor", lambda t, cm=cm, py=py, n=n:
                                t.matmul(py[:, 0:512], eye_bf[:, :], cm,
                                         start=(n == 0), stop=(n == NST - 1)))
                    rg_cm.rel(mns)
                rg_my.fin(mns)
                si = rg_wslA.acq("sync")
                dstz = wslA[:, si * 1024:si * 1024 + 1024]
                md = mk.dma("sync", lambda s, dt=dt, dstz=dstz:
                            s.dma_start(out=dstz, in_=pk_inproj[16 + dt]),
                            sem_wslA[si])
                rg_wslA.fin(md)
                bi = rg_mb.acq("tensor")
                pz = MB[bi]
                rg_wslA.rdy("tensor")
                for kc in range(NKT):
                    mm = mk.op("tensor", lambda t, pz=pz, kc=kc, dstz=dstz:
                               t.matmul(pz[:, 0:512],
                                        dstz[:, kc * 128:(kc + 1) * 128],
                                        xln_c[:, kc * 512:(kc + 1) * 512],
                                        start=(kc == 0), stop=(kc == NKT - 1)))
                rg_mb.fin(mm)
                rg_wslA.rel(mm)
                zi = rg_z.acq("scalar")
                zz = z_t[:, zi * 512:zi * 512 + 512]
                rg_mb.rdy("scalar")
                mz = mk.op("scalar", lambda s, zz=zz, pz=pz:
                           s.activation(zz, pz[:, 0:512], AF.Silu))
                rg_mb.rel(mz)
                rg_z.fin(mz)
                yt = yt_t[:, 0:512]
                rg_z.rdy("vector")
                rg_my.rdy("vector")
                mk.op("vector", lambda v, yt=yt, dt=dt:
                      v.tensor_scalar(yt,
                                      xin_sil[:, dt * 512:(dt + 1) * 512],
                                      dvec_sb[:, dt:dt + 1], None, ALU.mult))
                mv = mk.op("vector", lambda v, yt=yt, py=py:
                           v.tensor_tensor(yt, py[:, 0:512], yt, ALU.add))
                rg_my.rel(mv)
                my5 = mk.op("vector", lambda v, yt=yt, zz=zz, dt=dt:
                            v.tensor_tensor(y_sb[:, dt * 512:(dt + 1) * 512],
                                            yt, zz, ALU.mult))
                rg_z.rel(my5)
            m_y = my5
            tap(f"my{c}", y_sb[:, :], m_y)

            mk.wait_mark("tensor", m_y)
            for ot in range(NKT):
                rg_wslO.acq("sync")
                dst = wslO[:, 0:2048]
                md = mk.dma("sync", lambda s, ot=ot, dst=dst:
                            s.dma_start(out=dst, in_=pk_outproj[ot]),
                            sem_wslO[0])
                rg_wslO.fin(md)
                bi = rg_mb.acq("tensor")
                pb = MB[bi]
                rg_wslO.rdy("tensor")
                for kc in range(NDT):
                    mm = mk.op("tensor", lambda t, pb=pb, kc=kc, dst=dst:
                               t.matmul(pb[:, 0:512],
                                        dst[:, kc * 128:(kc + 1) * 128],
                                        y_sb[:, kc * 512:(kc + 1) * 512],
                                        start=(kc == 0),
                                        stop=(kc == NDT - 1)))
                rg_mb.fin(mm)
                rg_wslO.rel(mm)
                xs = x_fm[:, ot * L + col0:ot * L + col0 + CH]
                rg_mb.rdy("vector")
                mv = mk.op("vector", lambda v, xs=xs, pb=pb:
                           v.tensor_tensor(xs, xs, pb[:, 0:512], ALU.add))
                rg_mb.rel(mv)
        ar.pop()
        barrier()

        if upto in ("moe", "attn", "ffn"):
            ar.push()
            xln_m = ar.alloc([128, NKT * 1024], BF16, "xln_m")
            sq_m = ar.alloc([128, 2 * 512], BF16, "sq_m")
            aa_m = ar.alloc([128, 512], F32, "aa_m")
            bb_m = ar.alloc([128, 512], F32, "bb_m")
            wr_sb = ar.alloc([128, 64], BF16, "wr_sb")
            wrf = ar.alloc([128, 64], F32, "wrf")
            srep = ar.alloc([128, 16], F32, "srep")
            rowt = ar.alloc([1, 16], F32, "rowt")
            acolm = ar.alloc([128, 8], F32, "acolm")
            bcolm = ar.alloc([128, 8], F32, "bcolm")
            br_sb = ar.alloc([8, 1], F32, "br_sb")
            logit_sb = ar.alloc([16, 2048], F32, "logit_sb")
            ltok = ar.alloc([128, 16 * 8], F32, "ltok")
            gtok = ar.alloc([128, 16 * 8], F32, "gtok")
            sel_sbuf = ar.alloc([128, 16], F32, "sel_sbuf")
            grow = ar.alloc([16, 2048], BF16, "grow")
            gcol = ar.alloc([128, 8], F32, "gcol")
            gb_sb = ar.alloc([128, 1024], BF16, "gb_sb")
            b1_sb = ar.alloc([128, 256], F32, "b1_sb")
            b2row = ar.alloc([1, 1024], BF16, "b2row")
            h_sb = ar.alloc([128, 32 * 1024], BF16, "h_sb")
            wsl1 = ar.alloc([128, 2 * 1024], BF16, "wsl1")
            wsl2 = ar.alloc([128, 2 * 1024], BF16, "wsl2")
            ybf = ar.alloc([128, 2 * 512], BF16, "ybf")
            mtm = ar.alloc([128, 2 * 512], BF16, "mtm")

            sem_moec = mk.dma_sem("moec")
            sem_moeg = mk.dma_sem("moeg")

            def g_dma_wrf(s, kc):
                return s.dma_start(out=wrf[:, kc * 8:(kc + 1) * 8],
                                   in_=e_wr[kc * 128:(kc + 1) * 128, :])
            sem_w1sl = [mk.dma_sem(f"w1sl{i}") for i in range(2)]
            sem_w2sl = [mk.dma_sem(f"w2sl{i}") for i in range(2)]
            sem_b2r = mk.dma_sem("b2r")
            rg_w1 = Ring(mk, 2)
            rg_w2 = Ring(mk, 2)
            rg_hp = Ring(mk, 2)
            rg_ybf = Ring(mk, 2)
            rg_b2 = Ring(mk, 1)

            for kc in range(NKT):
                mk.dma("gpsimd", lambda g, kc=kc:
                       g.dma_start(out=wr_sb[:, kc * 8:(kc + 1) * 8],
                                   in_=e_wr[kc * 128:(kc + 1) * 128, :]),
                       sem_moeg)
            mk.dma("sync", lambda s: s.dma_start(out=br_sb[:, :], in_=e_br),
                   sem_moec)
            for kc in range(NKT):
                mk.dma("sync", lambda s, kc=kc:
                       g_dma_wrf(s, kc), sem_moec)
            for e_ in range(8):
                for t_ in range(32):
                    mk.dma("sync", lambda s, e_=e_, t_=t_:
                           s.dma_start(
                               out=b1_sb[:, e_ * 32 + t_:e_ * 32 + t_ + 1],
                               in_=e_b1[e_ * 4096 + t_ * 128:
                                        e_ * 4096 + (t_ + 1) * 128, :]),
                           sem_moec)
            v_moec = (sem_moec, mk.dmaval[id(sem_moec)])
            v_moeg = (sem_moeg, mk.dmaval[id(sem_moeg)])
            m_gz = mk.op("vector", lambda v: v.memset(grow[:, :], 0.0))

            HP = [ps.bank(0, F32, 512, "hp0"), ps.bank(1, F32, 512, "hp1")]
            RT = ps.bank(2, F32, 512, "rt")
            GT = ps.bank(3, F32, 512, "gt")
            YP = [ps.bank(i, F32, 512, f"yp{i}") for i in range(8)]

            mk.wait_mark("tensor", v_moec)
            for kc in range(NKT):
                mm = mk.op("tensor", lambda t, kc=kc:
                           t.matmul(RT[0:1, 0:8], ones_colf[:, :],
                                    wrf[:, kc * 8:(kc + 1) * 8],
                                    start=(kc == 0), stop=(kc == NKT - 1)))
            mt = mk.op("tensor", lambda t:
                       t.transpose(RT[0:1, 8:16], br_sb[:, :],
                                   eye_f32[0:8, 0:8]))
            mk.wait_mark("vector", mt)
            mk.op("vector", lambda v: v.tensor_copy(rowt[0:1, 0:8],
                                                    RT[0:1, 0:8]))
            mvr = mk.op("vector", lambda v: v.tensor_copy(rowt[0:1, 8:16],
                                                          RT[0:1, 8:16]))
            mk.wait_mark("tensor", mvr)
            mm = mk.op("tensor", lambda t:
                       t.matmul(RT[:, 0:16], ones_rowf[:, :],
                                rowt[0:1, 0:16], start=True, stop=True))
            mk.wait_mark("vector", mm)
            msr = mk.op("vector", lambda v: v.tensor_copy(srep[:, 0:16],
                                                          RT[:, 0:16]))

            m_prev_y = (mk.prog["vector"], 0)
            mk.wait_mark("tensor", msr)
            for half in range(2):
                hb = half * 1024
                mk.wait_mark("tensor", m_prev_y)
                if half == 0:
                    mk.wait_mark("tensor", v_moec)
                    mk.wait_mark("tensor", v_moeg)
                mln1 = emit_ln(hb, xln_m, sq_m, aa_m, bb_m, rg_sq,
                               out_off=0, out_stride=1024)
                mln2 = mln1
                m_ab = (mk.prog["vector"], 0)
                for lc in range(2):
                    mk.wait_mark("tensor",
                                 mln1 if lc == 0 else mln2)
                    mk.wait_mark("tensor", m_ab)
                    for blk in range(4):
                        mta = mk.op("tensor", lambda t, blk=blk:
                                    t.transpose(
                                        GT[:, blk:blk + 1],
                                        rows[0:1, 1024 + blk * 128:
                                             1024 + (blk + 1) * 128],
                                        eye_f32[0:1, 0:1]))
                        mtb = mk.op("tensor", lambda t, blk=blk:
                                    t.transpose(
                                        TPS[:, 2 * blk:2 * blk + 1],
                                        rows_bf[0:1, 512 + blk * 128:
                                                512 + (blk + 1) * 128],
                                        eye_bf[0:1, 0:1]))
                    mk.wait_mark("vector", mtb)
                    for blk in range(4):
                        mva = mk.op("vector",
                                    lambda v, lc=lc, blk=blk:
                                    v.tensor_copy(
                                        acolm[:, lc * 4 + blk:
                                              lc * 4 + blk + 1],
                                        GT[:, blk:blk + 1]))
                        mvb = mk.op("vector",
                                    lambda v, lc=lc, blk=blk:
                                    v.tensor_copy(
                                        bcolm[:, lc * 4 + blk:
                                              lc * 4 + blk + 1],
                                        TPS[:, 2 * blk:2 * blk + 1]))
                    m_ab = mvb
                    if lc == 0:
                        mk.wait_mark("vector", mvb)
                        mln2 = emit_ln(hb + 512, xln_m, sq_m, aa_m, bb_m,
                                       rg_sq, out_off=512, out_stride=1024)
                m_gt = (mk.prog["vector"], 0)
                mlast_raw = (mk.prog["vector"], 0)
                for lc in range(2):
                    mk.wait_mark("tensor", m_prev_y)
                    mk.wait_mark("tensor", mlast_raw)
                    tok0 = hb + lc * 512
                    for kc in range(NKT):
                        mm = mk.op("tensor", lambda t, kc=kc, tok0=tok0:
                                   t.matmul(RT[0:8, 0:512],
                                            wrf[:, kc * 8:(kc + 1) * 8],
                                            x_fm[:, kc * L + tok0:
                                                 kc * L + tok0 + 512],
                                            start=(kc == 0),
                                            stop=(kc == NKT - 1)))
                    mk.wait_mark("vector", mm)
                    mlg = mk.op("vector", lambda v, lc=lc:
                                v.tensor_copy(
                                    logit_sb[0:8, hb + lc * 512:
                                             hb + (lc + 1) * 512],
                                    RT[0:8, 0:512]))
                    mlast_raw = mlg
                mk.wait_mark("tensor", mlast_raw)
                for tt in range(8):
                    mk.wait_mark("tensor", m_gt)
                    mt = mk.op("tensor", lambda t, tt=tt:
                               t.transpose(GT[:, 0:8],
                                           logit_sb[0:8, hb + tt * 128:
                                                    hb + (tt + 1) * 128],
                                           eye_f32[0:8, 0:8]))
                    mk.wait_mark("vector", mt)
                    lt_ = ltok[:, tt * 8:(tt + 1) * 8]
                    gt_ = gtok[:, tt * 8:(tt + 1) * 8]
                    mv = mk.op("vector", lambda v, lt_=lt_, tt=tt:
                               v.tensor_scalar(
                                   lt_, GT[:, 0:8],
                                   acolm[:, tt:tt + 1], None, ALU.mult))
                    mv = mk.op("vector", lambda v, lt_=lt_, tt=tt:
                               v.scalar_tensor_tensor(
                                   lt_, srep[:, 0:8],
                                   bcolm[:, tt:tt + 1], lt_,
                                   ALU.mult, ALU.add))
                    mv = mk.op("vector", lambda v, lt_=lt_:
                               v.tensor_tensor(lt_, lt_, srep[:, 8:16],
                                               ALU.add))
                    mk.op("vector", lambda v, lt_=lt_: v.tensor_reduce(
                        gcol[:, 0:1], lt_, mybir.AxisListType.X, ALU.max))
                    mk.op("vector", lambda v: v.tensor_scalar(
                        gcol[:, 1:2], gcol[:, 0:1], -1.0, None, ALU.mult))
                    mk.wait_mark("scalar", mk.mark("vector"))
                    msc = mk.op("scalar", lambda s, lt_=lt_, gt_=gt_:
                                s.activation(gt_, lt_, AF.Exp,
                                             bias=gcol[:, 1:2]))
                    mk.wait_mark("vector", msc)
                    mk.op("vector", lambda v, lt_=lt_: v.tensor_scalar(
                        sel_sbuf[:, 0:8], lt_, gcol[:, 0:1], None, ALU.is_ge))
                    mk.op("vector", lambda v, lt_=lt_:
                          v.scalar_tensor_tensor(
                              sel_sbuf[:, 8:16], sel_sbuf[:, 0:8], -1e30,
                              lt_, ALU.mult, ALU.add))
                    mk.op("vector", lambda v: v.tensor_reduce(
                        gcol[:, 2:3], sel_sbuf[:, 8:16],
                        mybir.AxisListType.X, ALU.max))
                    mk.op("vector", lambda v: v.tensor_tensor(
                        gcol[:, 3:4], gcol[:, 2:3], gcol[:, 1:2], ALU.add))
                    mk.wait_mark("scalar", mk.mark("vector"))
                    msc = mk.op("scalar", lambda s: s.activation(
                        gcol[:, 4:5], gcol[:, 3:4], AF.Exp))
                    mk.wait_mark("vector", msc)
                    mk.op("vector", lambda v: v.tensor_scalar(
                        gcol[:, 5:6], gcol[:, 4:5], 1.0, None, ALU.add))
                    mk.op("vector", lambda v: v.reciprocal(
                        gcol[:, 6:7], gcol[:, 5:6]))
                    mk.op("vector", lambda v, lt_=lt_: v.tensor_scalar(
                        sel_sbuf[:, 0:8], lt_, gcol[:, 2:3], None, ALU.is_ge))
                    mk.op("vector", lambda v, gt_=gt_: v.tensor_tensor(
                        gt_, gt_, sel_sbuf[:, 0:8], ALU.mult))
                    mg = mk.op("vector", lambda v, gt_=gt_: v.tensor_scalar(
                        gt_, gt_, gcol[:, 6:7], None, ALU.mult))
                    mk.wait_mark("tensor", mg)
                    mt2 = mk.op("tensor", lambda t, gt_=gt_:
                                t.transpose(GT[0:8, 8:136], gt_,
                                            eye_f32[:, :]))
                    mk.wait_mark("vector", mt2)
                    mgr = mk.op("vector", lambda v, tt=tt:
                                v.tensor_copy(
                                    grow[0:8, hb + tt * 128:
                                         hb + (tt + 1) * 128],
                                    GT[0:8, 8:136]))
                    m_gt = mgr
                mk.wait_mark("tensor", mln2)
                for e_ in range(8):
                    mk.wait_mark("tensor", mgr if e_ == 0 else m_prev_y)
                    if e_ == 0:
                        mk.wait_mark("tensor", m_gz)
                    for lc in range(2):
                        mm = mk.op("tensor", lambda t, e_=e_, lc=lc:
                                   t.matmul(GT[:, 0:512],
                                            sel16[:, e_ * 128:(e_ + 1) * 128],
                                            grow[0:16, hb + lc * 512:
                                                 hb + (lc + 1) * 512],
                                            start=True, stop=True))
                        mk.wait_mark("scalar", mm)
                        msg = mk.op("scalar", lambda s, lc=lc:
                                    s.copy(gb_sb[:, lc * 512:(lc + 1) * 512],
                                           GT[:, 0:512]))
                        mk.wait_mark("tensor", msg)
                    rg_b2.acq("sync")
                    mb2 = mk.dma("gpsimd", lambda g, e_=e_:
                                 g.dma_start(out=b2row[:, :],
                                             in_=e_b2[e_:e_ + 1, :]),
                                 sem_b2r)
                    rg_b2.fin(mb2)
                    for ht in range(32):
                        wi = rg_w1.acq("gpsimd")
                        dst1 = wsl1[:, wi * 1024:(wi + 1) * 1024]
                        md = mk.dma(
                            "gpsimd", lambda g, e_=e_, ht=ht, dst1=dst1:
                            g.dma_start(
                                out=dst1.rearrange(
                                    "p (kc hh) -> p kc hh", hh=128),
                                in_=e_w1[e_][:, ht * 128:(ht + 1) * 128]
                                .rearrange("(kc p) hh -> p kc hh", p=128)),
                            sem_w1sl[wi])
                        rg_w1.fin(md)
                        rg_w1.rdy("tensor")
                        for lc in range(2):
                            hi = rg_hp.acq("tensor")
                            hp = HP[hi]
                            if half == 0 and e_ == 0 and ht == 0:
                                mk.wait_mark("tensor", m_prev_y)
                            for kc in range(NKT):
                                mm = mk.op(
                                    "tensor",
                                    lambda t, hp=hp, kc=kc, lc=lc, dst1=dst1:
                                    t.matmul(
                                        hp[:, 0:512],
                                        dst1[:, kc * 128:(kc + 1) * 128],
                                        xln_m[:, kc * 1024 + lc * 512:
                                              kc * 1024 + (lc + 1) * 512],
                                        start=(kc == 0),
                                        stop=(kc == NKT - 1)))
                            rg_hp.fin(mm)
                            if lc == 1:
                                rg_w1.rel(mm)
                            rg_hp.rdy("vector")
                            mh = mk.op(
                                "vector",
                                lambda v, hp=hp, e_=e_, ht=ht, lc=lc:
                                v.tensor_scalar(
                                    h_sb[:, ht * 1024 + lc * 512:
                                         ht * 1024 + (lc + 1) * 512],
                                    hp[:, 0:512],
                                    b1_sb[:, e_ * 32 + ht:e_ * 32 + ht + 1],
                                    0.0, ALU.add, ALU.max))
                            rg_hp.rel(mh)
                    m_h_done = mh
                    rg_b2.rdy("tensor")
                    for lc in range(2):
                        mk.wait_mark("tensor", m_h_done)
                        mk.wait_mark("tensor", m_prev_y)
                        for dt in range(8):
                            mm = mk.op("tensor", lambda t, dt=dt:
                                       t.matmul(YP[dt][:, 0:512],
                                                b2row[0:1,
                                                      dt * 128:(dt + 1) * 128],
                                                ones512[0:1, 0:512],
                                                start=True, stop=False))
                        for hc in range(32):
                            wi = rg_w2.acq("gpsimd")
                            dst2 = wsl2[:, wi * 1024:(wi + 1) * 1024]
                            md = mk.dma(
                                "gpsimd", lambda g, e_=e_, hc=hc, dst2=dst2:
                                g.dma_start(
                                    out=dst2,
                                    in_=e_w2[e_][hc * 128:(hc + 1) * 128, :]),
                                sem_w2sl[wi])
                            rg_w2.fin(md)
                            rg_w2.rdy("tensor")
                            for dt in range(8):
                                mm = mk.op(
                                    "tensor",
                                    lambda t, dt=dt, hc=hc, lc=lc, dst2=dst2:
                                    t.matmul(
                                        YP[dt][:, 0:512],
                                        dst2[:, dt * 128:(dt + 1) * 128],
                                        h_sb[:, hc * 1024 + lc * 512:
                                             hc * 1024 + (lc + 1) * 512],
                                        start=False, stop=(hc == 31)))
                                if dt == 7:
                                    rg_w2.rel(mm)
                        m_ymm = mm
                        mk.wait_mark("scalar", m_ymm)
                        for dt in range(8):
                            yi = rg_ybf.acq("scalar")
                            yb = ybf[:, yi * 512:(yi + 1) * 512]
                            msy = mk.op("scalar", lambda s, yb=yb, dt=dt:
                                        s.copy(yb, YP[dt][:, 0:512]))
                            rg_ybf.fin(msy)
                            rg_ybf.rdy("vector")
                            mt_ = mtm[:, (dt % 2) * 512:
                                      (dt % 2) * 512 + 512]
                            mk.op("vector", lambda v, yb=yb, mt_=mt_, lc=lc:
                                  v.tensor_tensor(
                                      mt_, yb,
                                      gb_sb[:, lc * 512:(lc + 1) * 512],
                                      ALU.mult))
                            xs = x_fm[:, dt * L + hb + lc * 512:
                                      dt * L + hb + (lc + 1) * 512]
                            mv = mk.op("vector", lambda v, xs=xs, mt_=mt_:
                                       v.tensor_tensor(xs, xs, mt_, ALU.add))
                            rg_ybf.rel(mv)
                        m_prev_y = mv
                    rg_b2.rel(m_prev_y)
            ar.pop()
            barrier()

        if upto in ("attn", "ffn"):
            ar.push()
            cs1 = ar.alloc([128, L], BF16, "cs1")
            cs2 = ar.alloc([128, L], BF16, "cs2")
            lamv = ar.alloc([64, 4], F32, "lamv")
            alph = ar.alloc([1, 4], F32, "alph")
            srow = ar.alloc([1, 1024], F32, "srow")
            srow_bf = ar.alloc([1, 1024], BF16, "srow_bf")
            ccol = ar.alloc([128, 16], F32, "ccol")
            hmask_sb = ar.alloc([128, 4], BF16, "hmask")

            sem_ac = mk.dma_sem("ac")
            mk.dma("sync", lambda s: s.dma_start(out=hmask_sb[:, :],
                                                 in_=c_hmask), sem_ac)
            mk.dma("sync", lambda s: s.dma_start(out=cs1[:, :], in_=c_cs1),
                   sem_ac)
            mk.dma("sync", lambda s: s.dma_start(out=cs2[:, :], in_=c_cs2),
                   sem_ac)
            for i, src in enumerate((a_lq1, a_lk1, a_lq2, a_lk2)):
                mk.dma("sync", lambda s, i=i, src=src:
                       s.dma_start(out=lamv[:, i:i + 1], in_=src), sem_ac)
            mk.dma("sync", lambda s: s.dma_start(out=alph[0:1, 0:1],
                                                 in_=a_alpha), sem_ac)
            v_ac = (sem_ac, mk.dmaval[id(sem_ac)])

            ar.push()
            sq_a = ar.alloc([128, 2 * 512], BF16, "sq_a")
            aa_a = ar.alloc([128, 512], F32, "aa_a")
            bb_a = ar.alloc([128, 512], F32, "bb_a")
            xln_a = ar.alloc([128, NKT * 512], BF16, "xln_a")
            wslq = ar.alloc([128, 2 * 1024], BF16, "wslq")
            qkev = ar.alloc([128, 4 * 512], BF16, "qkev")
            shf_t = ar.alloc([128, 512], BF16, "shf_t")
            t1_t = ar.alloc([128, 512], BF16, "t1_t")
            sqb_a = ar.alloc([128, 512], BF16, "sqb_a")

            AP0 = ps.bank(0, F32, 512, "ap0")
            AP1 = ps.bank(1, F32, 512, "ap1")
            sem_aw = [mk.dma_sem(f"aw{i}") for i in range(2)]
            sem_qv = [mk.dma_sem(f"qv{i}") for i in range(4)]
            rg_aw = Ring(mk, 2)
            rg_qe = Ring(mk, 4)
            PERM32 = [(i + 16) % 32 for i in range(32)]

            def park_dst(ot, lc):
                c0, c1 = lc * 512, (lc + 1) * 512
                if ot < 4:
                    return pk_q1[ot * 128:(ot + 1) * 128, c0:c1]
                if ot < 8:
                    return pk_q2[(ot - 4) * 128:(ot - 3) * 128, c0:c1]
                if ot < 12:
                    return pk_k1[(ot - 8) * 128:(ot - 7) * 128, c0:c1]
                if ot < 16:
                    return pk_k2[(ot - 12) * 128:(ot - 11) * 128, c0:c1]
                return pk_v[ot - 16][:, c0:c1]

            mk.wait_mark("vector", v_ac)
            mk.op("vector", lambda v: v.memset(ccol[:, 0:4], 0.0))
            m_ev = [(mk.prog["vector"], 0)] * 2
            m_nrmm = (mk.prog["tensor"], 0)
            m_nrd = (mk.prog["vector"], 0)
            for lc in range(4):
                mln = emit_ln(lc * 512, xln_a, sq_a, aa_a, bb_a, rg_sq)
                mk.wait_mark("tensor", mln)
                for ot in range(32):
                    wi = rg_aw.acq("sync")
                    dsl = wslq[:, wi * 1024:(wi + 1) * 1024]
                    pkw = pk_wq[ot] if ot < 8 else (
                        pk_wk[ot - 8] if ot < 16 else pk_wv[ot - 16])
                    md = mk.dma("sync", lambda s, pkw=pkw, dsl=dsl:
                                s.dma_start(out=dsl, in_=pkw), sem_aw[wi])
                    rg_aw.fin(md)
                    rg_aw.rdy("tensor")
                    pb = AP0 if ot % 2 == 0 else AP1
                    mk.wait_mark("tensor", m_ev[ot % 2])
                    for kc in range(NKT):
                        mm = mk.op(
                            "tensor", lambda t, pb=pb, kc=kc, dsl=dsl:
                            t.matmul(pb[:, 0:512],
                                     dsl[:, kc * 128:(kc + 1) * 128],
                                     xln_a[:, kc * 512:(kc + 1) * 512],
                                     start=(kc == 0), stop=(kc == NKT - 1)))
                    rg_aw.rel(mm)
                    qi = rg_qe.acq("vector")
                    buf = qkev[:, qi * 512:(qi + 1) * 512]
                    mk.wait_mark("vector", mm)
                    mcv = mk.op("vector", lambda v, buf=buf, pb=pb:
                                v.tensor_copy(buf, pb[:, 0:512]))
                    m_ev[ot % 2] = mcv
                    mfin = mcv
                    if ot < 16:
                        c1s = cs1[:, lc * 512:(lc + 1) * 512]
                        c2s = cs2[:, lc * 512:(lc + 1) * 512]
                        mk.op("vector", lambda v, buf=buf:
                              v.stream_shuffle(shf_t[:, :], buf, PERM32))
                        mk.op("vector", lambda v, buf=buf, c1s=c1s:
                              v.tensor_tensor(t1_t[:, :], buf, c1s,
                                              ALU.mult))
                        mk.op("vector", lambda v, c2s=c2s:
                              v.tensor_tensor(shf_t[:, :], shf_t[:, :],
                                              c2s, ALU.mult))
                        m_rp = mk.op("vector", lambda v, buf=buf:
                                     v.tensor_tensor(buf, t1_t[:, :],
                                                     shf_t[:, :], ALU.add))
                        vi = ot // 4
                        mk.wait_mark("scalar", m_nrmm)
                        mk.wait_mark("scalar", m_rp)
                        m_sq = mk.op("scalar", lambda s, buf=buf:
                                     s.activation(sqb_a[:, :], buf,
                                                  AF.Square))
                        mk.wait_mark("tensor", m_sq)
                        mk.wait_mark("tensor", m_nrd)
                        m_nrmm = mk.op("tensor", lambda t: t.matmul(
                            P_LNS[0:4, 0:512], hmask_sb[:, :],
                            sqb_a[:, :], start=True, stop=True))
                        mk.wait_mark("vector", m_nrmm)
                        mk.op("vector", lambda v: v.tensor_reduce(
                            ccol[0:4, 8:9], P_LNS[0:4, 0:512],
                            mybir.AxisListType.X, ALU.max))
                        m_nrd = mfin = mk.op(
                            "vector", lambda v, vi=vi:
                            v.tensor_tensor(ccol[0:4, vi:vi + 1],
                                            ccol[0:4, vi:vi + 1],
                                            ccol[0:4, 8:9], ALU.max))
                    rg_qe.fin(mfin)
                    rg_qe.rdy("sync")
                    mpk = mk.dma("sync",
                                 lambda s, buf=buf, d=park_dst(ot, lc):
                                 s.dma_start(out=d, in_=buf), sem_qv[qi])
                    rg_qe.rel(mpk)
            ar.pop()
            barrier()

            mk.op("vector", lambda v: v.tensor_tensor(
                lamv[:, 0:1], lamv[:, 0:1], lamv[:, 1:2], ALU.mult))
            mvl = mk.op("vector", lambda v: v.tensor_tensor(
                lamv[:, 2:3], lamv[:, 2:3], lamv[:, 3:4], ALU.mult))
            mk.wait_mark("tensor", mvl)
            mk.op("tensor", lambda t: t.matmul(
                AP0[0:1, 256:257], ones_colf[0:64, :], lamv[:, 0:1],
                start=True, stop=True))
            mtl = mk.op("tensor", lambda t: t.matmul(
                AP0[0:1, 257:258], ones_colf[0:64, :], lamv[:, 2:3],
                start=True, stop=True))
            mk.wait_mark("scalar", mtl)
            msl = mk.op("scalar", lambda s: s.activation(
                alph[0:1, 1:3], AP0[0:1, 256:258], AF.Exp))
            mk.wait_mark("vector", msl)
            mk.op("vector", lambda v: v.tensor_tensor(
                alph[0:1, 1:2], alph[0:1, 1:2], alph[0:1, 2:3],
                ALU.subtract))
            mk.op("vector", lambda v: v.tensor_scalar(
                alph[0:1, 1:2], alph[0:1, 1:2], 0.2, None, ALU.add))
            mk.op("vector", lambda v: v.tensor_scalar(
                alph[0:1, 3:4], alph[0:1, 0:1], 2.0, 1.0, ALU.mult,
                ALU.add))
            mk.op("vector", lambda v: v.tensor_scalar(
                alph[0:1, 3:4], alph[0:1, 3:4], 1.0 / 3.14159265358979,
                None, ALU.mult))
            mtb = mk.op("tensor", lambda t: t.transpose(
                AP0[0:4, 0:4], ccol[0:4, 0:4], eye_f32[0:4, 0:4]))
            mk.wait_mark("vector", mtb)
            mvb = mk.op("vector", lambda v: v.tensor_reduce(
                ccol[0:4, 8:9], AP0[0:4, 0:4], mybir.AxisListType.X,
                ALU.max))
            mk.wait_mark("tensor", mvb)
            mtb2 = mk.op("tensor", lambda t: t.transpose(
                AP0[0:1, 128:132], ccol[0:4, 8:9], eye_f32[0:4, 0:4]))
            mk.wait_mark("vector", mtb2)
            mk.op("vector", lambda v: v.tensor_copy(ccol[0:1, 12:16],
                                                    AP0[0:1, 128:132]))
            mk.op("vector", lambda v: v.tensor_tensor(
                ccol[0:1, 9:10], ccol[0:1, 12:13], ccol[0:1, 14:15],
                ALU.mult))
            mvq = mk.op("vector", lambda v: v.tensor_tensor(
                ccol[0:1, 10:11], ccol[0:1, 13:14], ccol[0:1, 15:16],
                ALU.mult))
            mk.wait_mark("scalar", mvq)
            mk.op("scalar", lambda s: s.activation(
                ccol[0:1, 9:11], ccol[0:1, 9:11], AF.Ln))
            msq3 = mk.op("scalar", lambda s: s.activation(
                ccol[0:1, 9:11], ccol[0:1, 9:11], AF.Exp,
                scale=scal_sb[0:1, 3:4]))
            mk.wait_mark("vector", msq3)
            mvb2 = mk.op("vector", lambda v: v.tensor_scalar(
                ccol[0:1, 9:11], ccol[0:1, 9:11], -0.125, None, ALU.mult))
            mk.wait_mark("tensor", mvb2)
            mtb3 = mk.op("tensor", lambda t: t.matmul(
                AP0[:, 0:2], ones_rowf[:, :], ccol[0:1, 9:11],
                start=True, stop=True))
            mk.wait_mark("vector", mtb3)
            m_setup = mk.op("vector", lambda v: v.tensor_copy(
                ccol[:, 2:4], AP0[:, 0:2]))

            ar.push()
            qs1 = ar.alloc([32, L], BF16, "qs1")
            qs2 = ar.alloc([32, L], BF16, "qs2")
            ks1 = ar.alloc([32, L], BF16, "ks1")
            ks2 = ar.alloc([32, L], BF16, "ks2")
            vh = ar.alloc([128, L], BF16, "vh")
            vtok = ar.alloc([128, L], BF16, "vtok")
            e1b = ar.alloc([128, 2 * 512], BF16, "e1b")
            e2b = ar.alloc([128, 2 * 512], BF16, "e2b")
            rbc = ar.alloc([128, 1024], BF16, "rbc")
            cb1 = ar.alloc([128, 512], BF16, "cb1")
            cb2 = ar.alloc([128, 512], BF16, "cb2")
            oT = ar.alloc([128, L], BF16, "oT")
            o2h = ar.alloc([128, L], BF16, "o2h")
            sqa2 = ar.alloc([128, 512], BF16, "sqa2")
            gnc = ar.alloc([128, 16], F32, "gnc")

            SB1 = ps.bank(0, F32, 512, "ap0")
            SB2 = ps.bank(1, F32, 512, "ap1")
            AV1 = ps.bank(3, F32, 512, "av1")
            AV2 = ps.bank(4, F32, 512, "av2")
            SS1 = ps.bank(2, F32, 512, "ss1")
            SS2 = ps.bank(5, F32, 512, "ss2")
            sem_stg = mk.dma_sem("stg")
            sem_vh = mk.dma_sem("vh")
            sem_o2 = mk.dma_sem("o2")
            rg_stg = Ring(mk, 1)
            rg_vh = Ring(mk, 1)
            rg_o2h = Ring(mk, 1)
            rg_e1 = Ring(mk, 2)
            rg_e2 = Ring(mk, 2)

            for sp_ in sem_qv:
                mk.wait_mark("sync", (sp_, mk.dmaval[id(sp_)]))
            m_comb = (mk.prog["vector"], 0)
            m_avlast = (mk.prog["tensor"], 0)
            m_stT = (mk.prog["tensor"], 0)
            m_o2dma = (sem_o2, 0)
            for h in range(16):
                rg_stg.acq("sync")
                for stg, src in ((qs1, pk_q1), (qs2, pk_q2),
                                 (ks1, pk_k1), (ks2, pk_k2)):
                    md = mk.dma("sync", lambda s, stg=stg, src=src, h=h:
                                s.dma_start(
                                    out=stg[:, :],
                                    in_=src[h * 32:(h + 1) * 32, :]),
                                sem_stg)
                rg_stg.fin(md)
                rg_vh.acq("sync")
                mvhd = mk.dma("sync", lambda s, h=h:
                              s.dma_start(out=vh[:, :], in_=pk_v[h]),
                              sem_vh)
                rg_vh.fin(mvhd)
                rg_vh.rdy("tensor")
                mk.wait_mark("vector", m_avlast)
                for b_ in range(16):
                    mtv = mk.op("tensor", lambda t, b_=b_:
                                t.transpose(
                                    TPS[:, (b_ % 8) * 128:
                                        (b_ % 8 + 1) * 128],
                                    vh[:, b_ * 128:(b_ + 1) * 128],
                                    eye_bf[:, :]))
                    if b_ % 8 == 7:
                        mk.wait_mark("vector", mtv)
                        mve = mk.op("vector", lambda v, b_=b_:
                                    v.tensor_copy(
                                        vtok[:, (b_ - 7) * 128:
                                             (b_ + 1) * 128],
                                        TPS[:, 0:1024]))
                        mk.wait_mark("tensor", mve)
                m_vtok = mve
                mvhlast = None
                rg_stg.rdy("tensor")
                if h == 0:
                    mk.wait_mark("tensor", m_setup)
                mk.wait_mark("vector", m_stT)
                for qc in range(4):
                    for kc in range(16):
                        if kc == 0:
                            mk.wait_mark("tensor", m_comb)
                        mm1 = mk.op("tensor", lambda t, qc=qc, kc=kc:
                                    t.matmul(
                                        SB1[:, 0:512],
                                        ks1[:, kc * 128:(kc + 1) * 128],
                                        qs1[:, qc * 512:(qc + 1) * 512],
                                        start=True, stop=True))
                        mm2 = mk.op("tensor", lambda t, qc=qc, kc=kc:
                                    t.matmul(
                                        SB2[:, 0:512],
                                        ks2[:, kc * 128:(kc + 1) * 128],
                                        qs2[:, qc * 512:(qc + 1) * 512],
                                        start=True, stop=True))
                        mk.wait_mark("scalar", mm2)
                        i1 = rg_e1.acq("scalar")
                        ee1 = e1b[:, i1 * 512:(i1 + 1) * 512]
                        ms1 = mk.op("scalar", lambda s, ee1=ee1:
                                    s.activation(ee1, SB1[:, 0:512],
                                                 AF.Exp,
                                                 bias=ccol[:, 2:3],
                                                 scale=scal_sb[:, 2:3]))
                        rg_e1.fin(ms1)
                        i2 = rg_e2.acq("scalar")
                        ee2 = e2b[:, i2 * 512:(i2 + 1) * 512]
                        ms2 = mk.op("scalar", lambda s, ee2=ee2:
                                    s.activation(ee2, SB2[:, 0:512],
                                                 AF.Exp,
                                                 bias=ccol[:, 3:4],
                                                 scale=scal_sb[:, 2:3]))
                        rg_e2.fin(ms2)
                        rg_e1.rdy("tensor")
                        rg_e2.rdy("tensor")
                        mk.wait_mark("tensor", m_vtok)
                        ma1 = mk.op("tensor", lambda t, kc=kc, ee1=ee1:
                                    t.matmul(
                                        AV1[:, 0:512],
                                        vtok[:, kc * 128:(kc + 1) * 128],
                                        ee1, start=(kc == 0),
                                        stop=(kc == 15)))
                        ma2 = mk.op("tensor", lambda t, kc=kc, ee2=ee2:
                                    t.matmul(
                                        AV2[:, 0:512],
                                        vtok[:, kc * 128:(kc + 1) * 128],
                                        ee2, start=(kc == 0),
                                        stop=(kc == 15)))
                        mvhlast = ma2
                        mss1 = mk.op("tensor", lambda t, kc=kc, ee1=ee1:
                                     t.matmul(SS1[0:1, 0:512],
                                              ones_col[:, :], ee1,
                                              start=(kc == 0),
                                              stop=(kc == 15)))
                        mss2 = mk.op("tensor", lambda t, kc=kc, ee2=ee2:
                                     t.matmul(SS2[0:1, 0:512],
                                              ones_col[:, :], ee2,
                                              start=(kc == 0),
                                              stop=(kc == 15)))
                        rg_e1.rel(mss1)
                        rg_e2.rel(mss2)
                    m_avlast = ma2
                    mk.wait_mark("vector", mss2)
                    mk.op("vector", lambda v: v.reciprocal(
                        srow[0:1, 0:512], SS1[0:1, 0:512]))
                    mk.op("vector", lambda v: v.reciprocal(
                        srow[0:1, 512:1024], SS2[0:1, 0:512]))
                    mk.op("vector", lambda v: v.tensor_copy(
                        srow_bf[0:1, 0:512], srow[0:1, 0:512]))
                    mvr = mk.op("vector", lambda v: v.tensor_scalar(
                        srow_bf[0:1, 512:1024], srow[0:1, 512:1024],
                        alph[0:1, 1:2], None, ALU.mult))
                    mk.wait_mark("tensor", mvr)
                    mk.wait_mark("tensor", ms2)
                    mtr1 = mk.op("tensor", lambda t: t.matmul(
                        SB1[:, 0:512], ones_row[:, :],
                        srow_bf[0:1, 0:512], start=True, stop=True))
                    mtr2 = mk.op("tensor", lambda t: t.matmul(
                        SB2[:, 0:512], ones_row[:, :],
                        srow_bf[0:1, 512:1024], start=True, stop=True))
                    mk.wait_mark("vector", mtr2)
                    mk.op("vector", lambda v: v.tensor_copy(
                        rbc[:, 0:512], SB1[:, 0:512]))
                    mk.op("vector", lambda v: v.tensor_copy(
                        rbc[:, 512:1024], SB2[:, 0:512]))
                    mk.op("vector", lambda v: v.tensor_tensor(
                        cb1[:, :], AV1[:, 0:512], rbc[:, 0:512],
                        ALU.mult))
                    mk.op("vector", lambda v: v.tensor_tensor(
                        cb2[:, :], AV2[:, 0:512], rbc[:, 512:1024],
                        ALU.mult))
                    m_comb = mk.op("vector", lambda v, qc=qc:
                                   v.tensor_tensor(
                                       oT[:, qc * 512:(qc + 1) * 512],
                                       cb1[:, :], cb2[:, :],
                                       ALU.subtract))
                    mk.op("vector", lambda v, qc=qc: v.tensor_reduce(
                        gnc[:, qc:qc + 1],
                        oT[:, qc * 512:(qc + 1) * 512],
                        mybir.AxisListType.X, ALU.add))
                    mk.wait_mark("scalar", m_comb)
                    msq2 = mk.op("scalar", lambda s, qc=qc: s.activation(
                        sqa2[:, :], oT[:, qc * 512:(qc + 1) * 512],
                        AF.Square))
                    mk.wait_mark("vector", msq2)
                    mk.op("vector", lambda v, qc=qc: v.tensor_reduce(
                        gnc[:, 4 + qc:5 + qc], sqa2[:, :],
                        mybir.AxisListType.X, ALU.add))
                for a_, b_ in ((0, 1), (2, 3), (0, 2),
                               (4, 5), (6, 7), (4, 6)):
                    mgn = mk.op("vector", lambda v, a_=a_, b_=b_:
                                v.tensor_tensor(gnc[:, a_:a_ + 1],
                                                gnc[:, a_:a_ + 1],
                                                gnc[:, b_:b_ + 1],
                                                ALU.add))
                mk.wait_mark("tensor", mgn)
                mgt = mk.op("tensor", lambda t: t.transpose(
                    SB1[0:1, 0:128], gnc[:, 0:1], eye_f32[:, :]))
                mgt = mk.op("tensor", lambda t: t.transpose(
                    SB1[0:1, 128:256], gnc[:, 4:5], eye_f32[:, :]))
                mk.wait_mark("vector", mgt)
                mk.op("vector", lambda v: v.tensor_reduce(
                    srow[0:1, 0:1], SB1[0:1, 0:128],
                    mybir.AxisListType.X, ALU.add))
                mk.op("vector", lambda v: v.tensor_reduce(
                    srow[0:1, 1:2], SB1[0:1, 128:256],
                    mybir.AxisListType.X, ALU.add))
                mk.op("vector", lambda v: v.tensor_scalar(
                    srow[0:1, 0:2], srow[0:1, 0:2], 1.0 / 262144.0, None,
                    ALU.mult))
                mk.op("vector", lambda v: v.tensor_tensor(
                    srow[0:1, 2:3], srow[0:1, 0:1], srow[0:1, 0:1],
                    ALU.mult))
                mvv = mk.op("vector", lambda v: v.tensor_tensor(
                    srow[0:1, 1:2], srow[0:1, 1:2], srow[0:1, 2:3],
                    ALU.subtract))
                mk.wait_mark("scalar", mvv)
                mk.op("scalar", lambda s: s.activation(
                    srow[0:1, 1:2], srow[0:1, 1:2], AF.Ln,
                    bias=scal_sb[0:1, 0:1]))
                msv = mk.op("scalar", lambda s: s.activation(
                    srow[0:1, 1:2], srow[0:1, 1:2], AF.Exp,
                    scale=scal_sb[0:1, 1:2]))
                mk.wait_mark("vector", msv)
                mk.op("vector", lambda v: v.tensor_scalar(
                    srow[0:1, 1:2], srow[0:1, 1:2], 0.8, None, ALU.mult))
                mgn2 = mk.op("vector", lambda v: v.scalar_tensor_tensor(
                    srow[0:1, 2:3], srow[0:1, 0:1], -1.0, srow[0:1, 1:2],
                    ALU.mult, ALU.mult))
                mk.wait_mark("tensor", mgn2)
                mgt2 = mk.op("tensor", lambda t: t.matmul(
                    SB1[:, 0:2], ones_rowf[:, :], srow[0:1, 1:3],
                    start=True, stop=True))
                mk.wait_mark("vector", mgt2)
                mk.op("vector", lambda v: v.tensor_copy(ccol[:, 8:10],
                                                        SB1[:, 0:2]))
                mnrm = mk.op("vector", lambda v: v.tensor_scalar(
                    oT[:, :], oT[:, :], ccol[:, 8:9], ccol[:, 9:10],
                    ALU.mult, ALU.add))
                mk.wait_mark("tensor", mnrm)
                rg_o2h.acq("tensor")
                oT3 = oT.rearrange("p (x s) -> p x s", s=16)
                for qq in range(16):
                    mtq = mk.op("tensor", lambda t, qq=qq:
                                t.transpose(
                                    TPS[:, (qq % 8) * 128:
                                        (qq % 8 + 1) * 128],
                                    oT3[:, :, qq:qq + 1].rearrange(
                                        "p x s -> p (x s)"),
                                    eye_bf[:, :]))
                    if qq % 8 == 7:
                        mk.wait_mark("vector", mtq)
                        mo2 = mk.op("vector", lambda v, qq=qq:
                                    v.tensor_copy(
                                        o2h[:, (qq - 7) * 128:
                                            (qq + 1) * 128],
                                        TPS[:, 0:1024]))
                        mk.wait_mark("tensor", mo2)
                m_stT = mtq
                rg_o2h.fin(mo2)
                rg_o2h.rdy("sync")
                mo2d = mk.dma("sync", lambda s, h=h:
                              s.dma_start(out=pk_out2[h], in_=o2h[:, :]),
                              sem_o2)
                rg_o2h.rel(mo2d)
                m_o2dma = mo2d
                rg_stg.rel(mm2)
                rg_vh.rel(mvhlast if mvhlast else m_vtok)
            ar.pop()
            barrier()

            ar.push()
            olc = ar.alloc([128, 16 * 512], BF16, "olc")
            wslp = ar.alloc([128, 2 * 2048], BF16, "wslp")
            vpart = ar.alloc([128, 512], BF16, "vpart")
            abuf = ar.alloc([128, 512], F32, "abuf")
            rbufC = ar.alloc([128, 512], F32, "rbufC")
            tbufC = ar.alloc([128, 512], F32, "tbufC")
            gbuf = ar.alloc([128, 512], BF16, "gbuf")
            mbuf = ar.alloc([128, 512], BF16, "mbuf")
            c1col = ar.alloc([128, 1], F32, "c1col")
            sem_olc = mk.dma_sem("olc")
            sem_wp = [mk.dma_sem(f"wp{i}") for i in range(2)]
            rg_wp = Ring(mk, 2)
            rg_olc = Ring(mk, 1)
            BP0 = ps.bank(0, F32, 512, "ap0")
            BP1 = ps.bank(1, F32, 512, "ap1")

            mc1 = mk.op("tensor", lambda t: t.matmul(
                BP0[:, 0:1], ones_rowf[:, :], alph[0:1, 3:4],
                start=True, stop=True))
            mk.wait_mark("vector", mc1)
            mc1v = mk.op("vector", lambda v: v.tensor_copy(c1col[:, :],
                                                           BP0[:, 0:1]))
            m_ev0 = mc1v
            m_ev1 = mc1v
            m_gate = (mk.prog["vector"], 0)
            mk.wait_mark("sync", (sem_o2, mk.dmaval[id(sem_o2)]))
            for lc in range(4):
                rg_olc.acq("sync")
                for cc in range(16):
                    mol = mk.dma("sync", lambda s, cc=cc, lc=lc:
                                 s.dma_start(
                                     out=olc[:, cc * 512:(cc + 1) * 512],
                                     in_=pk_out2[cc][:, lc * 512:
                                                     (lc + 1) * 512]),
                                 sem_olc)
                rg_olc.fin(mol)
                rg_olc.rdy("tensor")
                for op_ in range(8):
                    for which in range(2):
                        ot = (8 + op_) if which == 0 else op_
                        wi = rg_wp.acq("sync")
                        dsl = wslp[:, wi * 2048:(wi + 1) * 2048]
                        md = mk.dma("sync", lambda s, ot=ot, dsl=dsl:
                                    s.dma_start(out=dsl,
                                                in_=pk_wproj[ot]),
                                    sem_wp[wi])
                        rg_wp.fin(md)
                        rg_wp.rdy("tensor")
                        pb = BP0 if which == 0 else BP1
                        mk.wait_mark("tensor",
                                     m_ev0 if which == 0 else m_ev1)
                        for cc in range(16):
                            mm = mk.op(
                                "tensor",
                                lambda t, pb=pb, cc=cc, dsl=dsl:
                                t.matmul(
                                    pb[:, 0:512],
                                    dsl[:, cc * 128:(cc + 1) * 128],
                                    olc[:, cc * 512:(cc + 1) * 512],
                                    start=(cc == 0), stop=(cc == 15)))
                        rg_wp.rel(mm)
                        if which == 0:
                            mk.wait_mark("scalar", mm)
                            mk.wait_mark("scalar", m_gate)
                            m_ev0 = mk.op("scalar", lambda s:
                                          s.copy(vpart[:, :],
                                                 BP0[:, 0:512]))
                        else:
                            mk.wait_mark("scalar", mm)
                            mar = mk.op("scalar", lambda s:
                                        s.activation(abuf[:, :],
                                                     BP1[:, 0:512],
                                                     AF.Abs))
                            mk.wait_mark("vector", mar)
                            mk.op("vector", lambda v: v.reciprocal(
                                rbufC[:, :], abuf[:, :]))
                            m_u = mk.op("vector", lambda v:
                                        v.tensor_tensor(
                                            rbufC[:, :], abuf[:, :],
                                            rbufC[:, :], ALU.min))
                            mk.wait_mark("scalar", m_u)
                            m_t = mk.op("scalar", lambda s:
                                        s.activation(tbufC[:, :],
                                                     rbufC[:, :],
                                                     AF.Arctan))
                            mk.op("vector", lambda v: v.tensor_scalar(
                                abuf[:, :], abuf[:, :], 1.0, None,
                                ALU.is_ge))
                            mk.wait_mark("vector", m_t)
                            mk.op("vector", lambda v: v.tensor_scalar(
                                rbufC[:, :], abuf[:, :], -2.0, 1.0,
                                ALU.mult, ALU.add))
                            mk.op("vector", lambda v: v.tensor_tensor(
                                rbufC[:, :], tbufC[:, :], rbufC[:, :],
                                ALU.mult))
                            mk.op("vector", lambda v:
                                  v.scalar_tensor_tensor(
                                      rbufC[:, :], abuf[:, :],
                                      1.5707963267948966, rbufC[:, :],
                                      ALU.mult, ALU.add))
                            m_ev1 = mk.op("vector", lambda v:
                                          v.tensor_scalar(
                                              abuf[:, :], BP1[:, 0:512],
                                              0.0, None, ALU.is_ge))
                            mk.op("vector", lambda v: v.tensor_scalar(
                                abuf[:, :], abuf[:, :], 2.0, -1.0,
                                ALU.mult, ALU.add))
                            mk.op("vector", lambda v: v.tensor_tensor(
                                abuf[:, :], rbufC[:, :], abuf[:, :],
                                ALU.mult))
                            mk.wait_mark("vector", m_ev0)
                            mk.op("vector", lambda v: v.tensor_scalar(
                                gbuf[:, :], abuf[:, :], c1col[:, :],
                                0.5, ALU.mult, ALU.add))
                            m_gate = mk.op("vector", lambda v:
                                           v.tensor_tensor(
                                               mbuf[:, :], gbuf[:, :],
                                               vpart[:, :], ALU.mult))
                            xs = x_fm[:, op_ * L + lc * 512:
                                      op_ * L + (lc + 1) * 512]
                            mk.op("vector", lambda v, xs=xs:
                                  v.tensor_tensor(xs, xs, mbuf[:, :],
                                                  ALU.add))
                rg_olc.rel(mm)
            ar.pop()
            barrier()

        if upto == "ffn":
            ar.push()
            sq_f = ar.alloc([128, 2 * 512], BF16, "sq_f")
            aa_f = ar.alloc([128, 512], F32, "aa_f")
            bb_f = ar.alloc([128, 512], F32, "bb_f")
            xln_f = ar.alloc([128, NKT * 512], BF16, "xln_f")
            hA = ar.alloc([128, 32 * 512], BF16, "hA")
            hB = ar.alloc([128, 32 * 512], BF16, "hB")
            wsh = ar.alloc([128, 2 * 4096], BF16, "wsh")
            ft1 = ar.alloc([128, 2 * 512], BF16, "ft1")
            ft2 = ar.alloc([128, 2 * 512], BF16, "ft2")
            w1s_sb = ar.alloc([128, 32], F32, "w1s")
            w2s_sb = ar.alloc([128, 32], F32, "w2s")
            h1s_sb = ar.alloc([128, 32], F32, "h1s")
            h2s_sb = ar.alloc([128, 32], F32, "h2s")
            w3s_sb = ar.alloc([128, 8], F32, "w3s")

            sem_ffc = mk.dma_sem("ffc")
            for dst, src, n in ((w1s_sb, f_w1s, 32), (w2s_sb, f_w2s, 32),
                                (h1s_sb, f_h1s, 32), (h2s_sb, f_h2s, 32),
                                (w3s_sb, f_w3s, 8)):
                for t_ in range(n):
                    mk.dma("sync", lambda s, dst=dst, src=src, t_=t_:
                           s.dma_start(out=dst[:, t_:t_ + 1],
                                       in_=src[t_ * 128:(t_ + 1) * 128,
                                               0:1]),
                           sem_ffc)
            v_ffc = (sem_ffc, mk.dmaval[id(sem_ffc)])
            mk.wait_mark("scalar", v_ffc)
            mk.wait_mark("vector", v_ffc)

            PB = [ps.bank(0, F32, 512, "ap0"), ps.bank(1, F32, 512, "ap1"),
                  ps.bank(3, F32, 512, "av1"), ps.bank(4, F32, 512, "av2")]
            m_pb = [(mk.prog["vector"], 0)] * 4
            sem_fh = [mk.dma_sem(f"fh{i}") for i in range(2)]
            rg_fh = Ring(mk, 2)
            m_s4mm = (mk.prog["tensor"], 0)
            m_s3mm = (mk.prog["tensor"], 0)

            for q_ in range(4):
                mln = emit_ln(q_ * 512, xln_f, sq_f, aa_f, bb_f, rg_sq)
                mk.wait_mark("tensor", mln)
                mk.wait_mark("vector", m_s4mm)
                for ot in range(32):
                    wi = rg_fh.acq("sync")
                    dsl = wsh[:, wi * 4096:wi * 4096 + 2048]
                    mk.dma("sync", lambda s, ot=ot, dsl=dsl:
                           s.dma_start(out=dsl[:, 0:1024],
                                       in_=pk_fw1[ot]), sem_fh[wi])
                    md = mk.dma("sync", lambda s, ot=ot, dsl=dsl:
                                s.dma_start(out=dsl[:, 1024:2048],
                                            in_=pk_fw2[ot]), sem_fh[wi])
                    rg_fh.fin(md)
                    rg_fh.rdy("tensor")
                    b1, b2 = (0, 1) if ot % 2 == 0 else (2, 3)
                    mk.wait_mark("tensor", m_pb[b1])
                    mk.wait_mark("tensor", m_pb[b2])
                    for kc in range(NKT):
                        mk.op("tensor", lambda t, b1=b1, kc=kc, dsl=dsl:
                              t.matmul(PB[b1][:, 0:512],
                                       dsl[:, kc * 128:(kc + 1) * 128],
                                       xln_f[:, kc * 512:(kc + 1) * 512],
                                       start=(kc == 0),
                                       stop=(kc == NKT - 1)))
                    for kc in range(NKT):
                        mm = mk.op(
                            "tensor", lambda t, b2=b2, kc=kc, dsl=dsl:
                            t.matmul(PB[b2][:, 0:512],
                                     dsl[:, 1024 + kc * 128:
                                         1024 + (kc + 1) * 128],
                                     xln_f[:, kc * 512:(kc + 1) * 512],
                                     start=(kc == 0),
                                     stop=(kc == NKT - 1)))
                    rg_fh.rel(mm)
                    mk.wait_mark("scalar", mm)
                    si = ot % 2
                    f1 = ft1[:, si * 512:(si + 1) * 512]
                    f2 = ft2[:, si * 512:(si + 1) * 512]
                    mk.op("scalar", lambda s, f1=f1, b1=b1, ot=ot:
                          s.activation(f1, PB[b1][:, 0:512], AF.Silu,
                                       scale=w1s_sb[:, ot:ot + 1]))
                    msc = mk.op("scalar", lambda s, f2=f2, b2=b2, ot=ot:
                                s.activation(f2, PB[b2][:, 0:512],
                                             AF.Identity,
                                             scale=w2s_sb[:, ot:ot + 1]))
                    mk.wait_mark("vector", msc)
                    mtt = mk.op("vector", lambda v, f1=f1, f2=f2, ot=ot:
                                v.tensor_tensor(
                                    hA[:, ot * 512:(ot + 1) * 512],
                                    f1, f2, ALU.mult))
                    m_pb[b1] = mtt
                    m_pb[b2] = mtt
                m_hA = mtt
                mk.wait_mark("tensor", m_hA)
                mk.wait_mark("scalar", m_s3mm)
                for ot in range(32):
                    wi = rg_fh.acq("sync")
                    dsl = wsh[:, wi * 4096:(wi + 1) * 4096]
                    md = mk.dma("sync", lambda s, ot=ot, dsl=dsl:
                                s.dma_start(out=dsl, in_=pk_fh1[ot]),
                                sem_fh[wi])
                    rg_fh.fin(md)
                    rg_fh.rdy("tensor")
                    bi = ot % 4
                    mk.wait_mark("tensor", m_pb[bi])
                    for kc in range(32):
                        mm = mk.op(
                            "tensor", lambda t, bi=bi, kc=kc, dsl=dsl:
                            t.matmul(PB[bi][:, 0:512],
                                     dsl[:, kc * 128:(kc + 1) * 128],
                                     hA[:, kc * 512:(kc + 1) * 512],
                                     start=(kc == 0), stop=(kc == 31)))
                    rg_fh.rel(mm)
                    mk.wait_mark("scalar", mm)
                    msg = mk.op("scalar", lambda s, bi=bi, ot=ot:
                                s.activation(
                                    hB[:, ot * 512:(ot + 1) * 512],
                                    PB[bi][:, 0:512], AF.Gelu,
                                    scale=h1s_sb[:, ot:ot + 1]))
                    m_pb[bi] = msg
                m_hB = msg
                mk.wait_mark("tensor", m_hB)
                for ot in range(32):
                    wi = rg_fh.acq("sync")
                    dsl = wsh[:, wi * 4096:(wi + 1) * 4096]
                    md = mk.dma("sync", lambda s, ot=ot, dsl=dsl:
                                s.dma_start(out=dsl, in_=pk_fh2[ot]),
                                sem_fh[wi])
                    rg_fh.fin(md)
                    rg_fh.rdy("tensor")
                    bi = ot % 4
                    mk.wait_mark("tensor", m_pb[bi])
                    for kc in range(32):
                        mm = mk.op(
                            "tensor", lambda t, bi=bi, kc=kc, dsl=dsl:
                            t.matmul(PB[bi][:, 0:512],
                                     dsl[:, kc * 128:(kc + 1) * 128],
                                     hB[:, kc * 512:(kc + 1) * 512],
                                     start=(kc == 0), stop=(kc == 31)))
                    m_s3mm = mm
                    rg_fh.rel(mm)
                    mk.wait_mark("scalar", mm)
                    msg = mk.op("scalar", lambda s, bi=bi, ot=ot:
                                s.activation(
                                    hA[:, ot * 512:(ot + 1) * 512],
                                    PB[bi][:, 0:512], AF.Gelu,
                                    scale=h2s_sb[:, ot:ot + 1]))
                    m_pb[bi] = msg
                m_hA2 = msg
                mk.wait_mark("tensor", m_hA2)
                for ot in range(NKT):
                    wi = rg_fh.acq("sync")
                    dsl = wsh[:, wi * 4096:(wi + 1) * 4096]
                    md = mk.dma("sync", lambda s, ot=ot, dsl=dsl:
                                s.dma_start(out=dsl, in_=pk_fw3[ot]),
                                sem_fh[wi])
                    rg_fh.fin(md)
                    rg_fh.rdy("tensor")
                    bi = ot % 4
                    mk.wait_mark("tensor", m_pb[bi])
                    for kc in range(32):
                        mm = mk.op(
                            "tensor", lambda t, bi=bi, kc=kc, dsl=dsl:
                            t.matmul(PB[bi][:, 0:512],
                                     dsl[:, kc * 128:(kc + 1) * 128],
                                     hA[:, kc * 512:(kc + 1) * 512],
                                     start=(kc == 0), stop=(kc == 31)))
                    m_s4mm = mm
                    rg_fh.rel(mm)
                    mk.wait_mark("vector", mm)
                    xs = x_fm[:, ot * L + q_ * 512:
                              ot * L + (q_ + 1) * 512]
                    mfr = mk.op("vector", lambda v, bi=bi, ot=ot, xs=xs:
                                v.scalar_tensor_tensor(
                                    xs, PB[bi][:, 0:512],
                                    w3s_sb[:, ot:ot + 1], xs,
                                    ALU.mult, ALU.add))
                    m_pb[bi] = mfr
            ar.pop()
            barrier()

        ar.push()
        otok = ar.alloc([128, 2 * D], F32, "otok")
        EP = [ps.bank(0, F32, 512, "EP0"), ps.bank(1, F32, 512, "EP1")]
        rg_ep = Ring(mk, 2)
        rg_otok = Ring(mk, 2)
        sem_ost = [mk.dma_sem(f"ost{i}") for i in range(2)]
        for lt in range(16):
            oi = rg_otok.acq("tensor")
            buf = otok[:, oi * D:oi * D + D]
            for kt in range(NKT):
                bi = rg_ep.acq("tensor")
                pb = EP[bi]
                mt = mk.op("tensor", lambda t, lt=lt, kt=kt, pb=pb:
                           t.transpose(pb[:, 0:128],
                                       x_fm[:, kt * L + lt * 128:
                                            kt * L + (lt + 1) * 128],
                                       eye_f32[:, :]))
                rg_ep.fin(mt)
                rg_ep.rdy("vector")
                mv = mk.op("vector", lambda v, buf=buf, kt=kt, pb=pb:
                           v.tensor_copy(buf[:, kt * 128:(kt + 1) * 128],
                                         pb[:, 0:128]))
                rg_ep.rel(mv)
            rg_otok.fin(mv)
            rg_otok.rdy("sync")
            mst = mk.dma("sync", lambda s, buf=buf, lt=lt:
                         s.dma_start(out=y_out[lt * 128:(lt + 1) * 128, :],
                                     in_=buf),
                         sem_ost[oi])
            rg_otok.rel(mst)
        ar.pop()

        for sp_ in sem_ost + [sem_store]:
            mk.wait_mark("sync", (sp_, mk.dmaval[id(sp_)]))
        mk.check_deadlock()
        nins = mk.ninstr
        peak = ar.peak
        with nc.Block() as block:
            mk.replay(block)

    meta = dict(ninstr=nins, sbuf_peak=peak, taps=taps_declared,
                nsem=mk.nsem)
    return nc, meta


def _constants():
    bf = ml_dtypes.bfloat16
    c = {}
    c["c_ones_col"] = np.ones((128, 1), dtype=bf)
    c["c_ones_colf"] = np.ones((128, 1), dtype=np.float32)
    c["c_ones_row"] = np.ones((1, 128), dtype=bf)
    c["c_eye"] = np.eye(128, dtype=bf)
    c["c_eye_f32"] = np.eye(128, dtype=np.float32)
    sel = np.zeros((16, 16 * 128), dtype=bf)
    for n in range(16):
        sel[n, n * 128:(n + 1) * 128] = 1
    c["c_sel16"] = sel
    scal = np.zeros((128, 8), dtype=np.float32)
    scal[:, 0] = 1e-5
    scal[:, 1] = -0.5
    scal[:, 2] = 0.125
    scal[:, 3] = 0.5
    c["c_scal"] = scal
    c["c_ones512"] = np.ones((1, 512), dtype=bf)
    c["c_ones_rowf"] = np.ones((1, 128), dtype=np.float32)
    i = np.arange(16, dtype=np.float64)
    inv_freq = 1.0 / 10000.0 ** (i / 256.0)
    t = np.arange(L, dtype=np.float64)
    freqs = np.outer(t, inv_freq)
    cosb = np.cos(freqs).astype(np.float32).astype(bf).astype(np.float32)
    sinb = np.sin(freqs).astype(np.float32).astype(bf).astype(np.float32)
    p = np.arange(128)
    cs1 = cosb[:, p % 16].T
    sgn = np.where((p % 32) < 16, -1.0, 1.0)[:, None]
    cs2 = sinb[:, p % 16].T * sgn
    c["c_cs1"] = np.ascontiguousarray(cs1.astype(bf))
    c["c_cs2"] = np.ascontiguousarray(cs2.astype(bf))
    hm = np.zeros((128, 4), dtype=bf)
    for j in range(4):
        hm[j * 32:(j + 1) * 32, j] = 1
    c["c_hmask"] = hm
    return c


def make_in_maps(inputs):
    consts = _constants()
    x_full = np.ascontiguousarray(inputs["L0_attn_x"])
    base = dict(
        m_inproj=inputs["L0_attn_in_proj_w"],
        m_convw=inputs["L0_attn_conv_w"].reshape(DI, 4),
        m_convb=inputs["L0_attn_conv_b"].reshape(DI, 1),
        m_xproj=inputs["L0_attn_x_proj_w"],
        m_dtw=inputs["L0_attn_dt_proj_w"],
        m_dtb=inputs["L0_attn_dt_proj_b"].reshape(DI, 1),
        m_alog=inputs["L0_attn_A_log"],
        m_dvec=inputs["L0_attn_D"].reshape(DI, 1),
        m_outproj=inputs["L0_attn_out_proj_w"],
        e_wr=inputs["L0_ffn_Wr"],
        e_br=inputs["L0_ffn_br"].reshape(8, 1),
        e_w1=inputs["L0_ffn_W1"],
        e_b1=inputs["L0_ffn_b1"].reshape(8 * 4096, 1),
        e_w2=inputs["L0_ffn_W2"],
        e_b2=inputs["L0_ffn_b2"],
        a_wq=inputs["L1_attn_Wq"],
        a_wk=inputs["L1_attn_Wk"],
        a_wv=inputs["L1_attn_Wv"],
        a_lq1=inputs["L1_attn_lambda_q1"].reshape(64, 1),
        a_lq2=inputs["L1_attn_lambda_q2"].reshape(64, 1),
        a_lk1=inputs["L1_attn_lambda_k1"].reshape(64, 1),
        a_lk2=inputs["L1_attn_lambda_k2"].reshape(64, 1),
        a_wproj=inputs["L1_attn_Wproj"],
        a_alpha=inputs["L1_attn_alpha"].reshape(1, 1),
        f_w1=inputs["L1_ffn_w1_w"],
        f_w1s=inputs["L1_ffn_w1_s"].reshape(4096, 1),
        f_w2=inputs["L1_ffn_w2_w"],
        f_w2s=inputs["L1_ffn_w2_s"].reshape(4096, 1),
        f_h1=inputs["L1_ffn_h1_w"],
        f_h1s=inputs["L1_ffn_h1_s"].reshape(4096, 1),
        f_h2=inputs["L1_ffn_h2_w"],
        f_h2s=inputs["L1_ffn_h2_s"].reshape(4096, 1),
        f_w3=inputs["L1_ffn_w3_w"],
        f_w3s=inputs["L1_ffn_w3_s"].reshape(D, 1),
        **consts,
    )
    base = {k: np.ascontiguousarray(v) for k, v in base.items()}
    in_maps = []
    for core in range(8):
        m = dict(base)
        m["x"] = np.ascontiguousarray(x_full[core])
        in_maps.append(m)
    return in_maps


_PROG_CACHE = {}


def _get_prog(key=()):
    if key not in _PROG_CACHE:
        _PROG_CACHE[key] = build_program(dbg_taps=key)
    return _PROG_CACHE[key]


def kernel(**inputs):
    nc, meta = _get_prog()
    in_maps = make_in_maps(inputs)
    res = run_bass_kernel_spmd(nc, in_maps, core_ids=list(range(8)))
    out = np.stack([res.results[i]["y"] for i in range(8)], axis=0)
    return out.astype(np.float32)


if __name__ == "__main__":
    import time
    t0 = time.time()
    nc, meta = build_program()
    print("built:", meta, f"{time.time() - t0:.1f}s")


# version 03 (final)
# speedup vs baseline: 1.0927x; 1.0927x over previous
from contextlib import ExitStack
import os
import numpy as np
import ml_dtypes
import concourse.bass as bass
import concourse.mybir as mybir
from concourse.alu_op_type import AluOpType as ALU
from concourse.bass_utils import run_bass_kernel_spmd

F32 = mybir.dt.float32
BF16 = mybir.dt.bfloat16
AF = mybir.ActivationFunctionType

ENGINES = ("tensor", "vector", "scalar", "gpsimd", "sync")

L = 2048
D = 1024
DI = 2048
NST = 16
DTR = 64
CH = 512
NCH = L // CH
NKT = D // 128
NDT = DI // 128

UPTO = os.environ.get("MK_UPTO", "ffn")
N_MAMBA_CHUNKS = int(os.environ.get("MK_CHUNKS", str(NCH)))
SELFWAIT_ENGINES = ("vector", "scalar")
MK_SELFWAIT = os.environ.get("MK_SELFWAIT", "1") == "1"
MK_RACEDET = os.environ.get("MK_RACEDET", "0") == "1"


class MK:

    def __init__(self, nc, ctx):
        self.nc = nc
        self.ctx = ctx
        self.stream = {e: [] for e in ENGINES}
        self.ninstr = 0
        self.nsem = 0
        self.prog = {}
        self.progval = {}
        self.dmaval = {}
        for e in ENGINES:
            self.prog[e] = self._new_sem("prog_" + e)
            self.progval[e] = 0

    def _new_sem(self, name):
        self.nsem += 1
        return self.ctx.enter_context(self.nc.semaphore(name))

    def dma_sem(self, name):
        s = self._new_sem(name)
        self.dmaval[id(s)] = 0
        return s

    def op(self, eng, fn, tick=True):
        if MK_SELFWAIT and eng in SELFWAIT_ENGINES and self.progval[eng] > 0:
            self.stream[eng].append(
                ("wait", self.prog[eng], self.progval[eng]))
        if tick:
            self.progval[eng] += 1
            inc = (self.prog[eng], 1)
        else:
            inc = None
        self.stream[eng].append(("op", fn, inc))
        self.ninstr += 1
        return (self.prog[eng], self.progval[eng])

    def dma(self, eng, fn, sem):
        if MK_SELFWAIT and eng in SELFWAIT_ENGINES and self.progval[eng] > 0:
            self.stream[eng].append(
                ("wait", self.prog[eng], self.progval[eng]))
        self.dmaval[id(sem)] += 16
        self.stream[eng].append(("op", fn, (sem, 16)))
        self.ninstr += 1
        return (sem, self.dmaval[id(sem)])

    def mark(self, eng):
        return (self.prog[eng], self.progval[eng])

    def wait_mark(self, eng, mark):
        sem, v = mark
        if v <= 0:
            return
        self.stream[eng].append(("wait", sem, v))

    def replay(self, block):
        def mk_runner(eng_name):
            def run(eng):
                for item in self.stream[eng_name]:
                    if item[0] == "op":
                        _, fn, inc = item
                        inst = fn(eng)
                        if inc is not None:
                            inst.then_inc(*inc)
                    else:
                        _, sem, v = item
                        eng.wait_ge(sem, v)
            return run
        block.tensor(mk_runner("tensor"))
        block.vector(mk_runner("vector"))
        block.scalar(mk_runner("scalar"))
        block.gpsimd(mk_runner("gpsimd"))
        block.sync(mk_runner("sync"))

    def check_deadlock(self):
        pos = {e: 0 for e in ENGINES}
        cur = {}
        progress = True
        while progress:
            progress = False
            for e in ENGINES:
                while pos[e] < len(self.stream[e]):
                    item = self.stream[e][pos[e]]
                    if item[0] == "op":
                        if item[2] is not None:
                            s, n = item[2]
                            cur[id(s)] = cur.get(id(s), 0) + n
                        pos[e] += 1
                        progress = True
                    else:
                        _, sem, v = item
                        if cur.get(id(sem), 0) >= v:
                            pos[e] += 1
                            progress = True
                        else:
                            break
        stuck = [(e, pos[e], len(self.stream[e])) for e in ENGINES
                 if pos[e] < len(self.stream[e])]
        if stuck:
            det = "; ".join(f"{e}@{p}/{n}" for e, p, n in stuck)
            raise RuntimeError("DEADLOCK in recorded program: " + det)


class Ring:

    def __init__(self, mk, n):
        self.mk, self.n = mk, n
        self.p = 0
        self.rels = []
        self.ready = []

    def acq(self, eng):
        i = self.p % self.n
        if self.p >= self.n:
            self.mk.wait_mark(eng, self.rels[self.p - self.n])
        self.p += 1
        return i

    def fin(self, mark):
        self.ready.append(mark)

    def rdy(self, eng, k=None):
        k = self.p - 1 if k is None else k
        self.mk.wait_mark(eng, self.ready[k])

    def rel(self, mark):
        self.rels.append(mark)


class Arena:
    def __init__(self, nc, ctx, nbytes):
        self.nc = nc
        self.nbytes = nbytes
        self.base_handle = ctx.enter_context(
            nc.sbuf_tensor("arena", [128, nbytes], mybir.dt.uint8))
        self.base_addr = nc.lookup_mloc(self.base_handle).addr
        self.off = 0
        self.stack = []
        self.peak = 0
        self.cnt = 0

    def alloc(self, shape, dtype, name=None):
        per_part = int(np.prod(shape[1:])) * mybir.dt.size(dtype)
        off = (self.off + 31) // 32 * 32
        assert off + per_part <= self.nbytes, (
            f"SBUF arena overflow: {off}+{per_part} > {self.nbytes} ({name})")
        self.off = off + per_part
        self.peak = max(self.peak, self.off)
        self.cnt += 1
        nm = name or f"ar{self.cnt}"
        h = self.nc.virtual_sbuf_tensor(nm, list(shape), dtype)
        self.nc._tensor(h.name, list(shape), dtype, type="SB")
        mloc = self.nc.lookup_mloc(h)
        mloc.allocated = not self.nc.target_bir_lowering
        mloc.addr = self.base_addr + off
        return h

    def push(self):
        self.stack.append(self.off)

    def pop(self):
        self.off = self.stack.pop()

    def at(self, shape, dtype, byte_off, name=None):
        per_part = int(np.prod(shape[1:])) * mybir.dt.size(dtype)
        assert byte_off + per_part <= self.nbytes
        self.cnt += 1
        nm = name or f"ar{self.cnt}"
        h = self.nc.virtual_sbuf_tensor(nm, list(shape), dtype)
        self.nc._tensor(h.name, list(shape), dtype, type="SB")
        mloc = self.nc.lookup_mloc(h)
        mloc.allocated = not self.nc.target_bir_lowering
        mloc.addr = self.base_addr + byte_off
        return h


class PsumPool:
    def __init__(self, nc, ctx):
        self.nc = nc
        self.base = ctx.enter_context(
            nc.psum_tensor("psarena", [128, 16384], mybir.dt.uint8))
        self.base_addr = nc.lookup_mloc(self.base).addr
        self.cnt = 0
        self.cache = {}

    def bank(self, idx, dtype, cols, name=None, nbanks=1):
        assert 0 <= idx and idx + nbanks <= 8
        assert cols * mybir.dt.size(dtype) <= 2048 * nbanks
        key = (idx, dtype, cols)
        if key in self.cache:
            return self.cache[key]
        self.cnt += 1
        nm = name or f"ps{self.cnt}"
        h = self.nc.virtual_psum_tensor(nm, [128, cols], dtype)
        self.nc._tensor(h.name, [128, cols], dtype, type="PSUM")
        mloc = self.nc.lookup_mloc(h)
        mloc.allocated = not self.nc.target_bir_lowering
        mloc.addr = self.base_addr + idx * 2048
        self.cache[key] = h
        return h


def build_program(dbg_taps=(), upto=UPTO):
    nc = bass.Bass("TRN2", target_bir_lowering=False,
                   detect_race_conditions=MK_RACEDET)
    taps_requested = set(dbg_taps)
    taps_declared = []

    def din(name, shape, dtype=F32):
        return nc.dram_tensor(name, list(shape), dtype,
                              kind="ExternalInput").ap()

    x_in = din("x", [L, D])
    m_inproj = din("m_inproj", [2 * DI, D])
    m_convw = din("m_convw", [DI, 4])
    m_convb = din("m_convb", [DI, 1])
    m_xproj = din("m_xproj", [DTR + 2 * NST, DI])
    m_dtw = din("m_dtw", [DI, DTR])
    m_dtb = din("m_dtb", [DI, 1])
    m_alog = din("m_alog", [DI, NST])
    m_dvec = din("m_dvec", [DI, 1])
    m_outproj = din("m_outproj", [D, DI])
    c_ones_col = din("c_ones_col", [128, 1], BF16)
    c_ones_colf = din("c_ones_colf", [128, 1], F32)
    c_ones_row = din("c_ones_row", [1, 128], BF16)
    c_eye = din("c_eye", [128, 128], BF16)
    c_eye_f32 = din("c_eye_f32", [128, 128], F32)
    c_sel16 = din("c_sel16", [16, 16 * 128], BF16)
    c_scal = din("c_scal", [128, 8])
    c_ones512 = din("c_ones512", [1, 512], BF16)
    c_ones_rowf = din("c_ones_rowf", [1, 128], F32)
    e_wr = din("e_wr", [D, 8])
    e_br = din("e_br", [8, 1])
    e_w1 = din("e_w1", [8, D, 4096])
    e_b1 = din("e_b1", [8 * 4096, 1])
    e_w2 = din("e_w2", [8, 4096, D])
    e_b2 = din("e_b2", [8, D])
    a_wq = din("a_wq", [D, D])
    a_wk = din("a_wk", [D, D])
    a_wv = din("a_wv", [2 * D, D])
    a_lq1 = din("a_lq1", [64, 1])
    a_lq2 = din("a_lq2", [64, 1])
    a_lk1 = din("a_lk1", [64, 1])
    a_lk2 = din("a_lk2", [64, 1])
    a_wproj = din("a_wproj", [2 * D, 2 * D])
    a_alpha = din("a_alpha", [1, 1])
    c_cs1 = din("c_cs1", [128, L], BF16)
    c_cs2 = din("c_cs2", [128, L], BF16)
    c_hmask = din("c_hmask", [128, 4], BF16)
    f_w1 = din("f_w1", [4096, D])
    f_w1s = din("f_w1s", [4096, 1])
    f_w2 = din("f_w2", [4096, D])
    f_w2s = din("f_w2s", [4096, 1])
    f_h1 = din("f_h1", [4096, 4096])
    f_h1s = din("f_h1s", [4096, 1])
    f_h2 = din("f_h2", [4096, 4096])
    f_h2s = din("f_h2s", [4096, 1])
    f_w3 = din("f_w3", [D, 4096])
    f_w3s = din("f_w3s", [D, 1])

    y_out = nc.dram_tensor("y", [L, D], F32, kind="ExternalOutput").ap()

    def dscr(name, shape, dtype):
        return nc.dram_tensor(name, list(shape), dtype, kind="Internal").ap()

    pk_inproj = dscr("pk_inproj", [32, 128, 1024], BF16)
    pk_outproj = dscr("pk_outproj", [8, 128, 2048], BF16)
    pk_x = dscr("pk_x", [128, NKT * L], F32)
    pk_wq = dscr("pk_wq", [8, 128, 1024], BF16)
    pk_wk = dscr("pk_wk", [8, 128, 1024], BF16)
    pk_wv = dscr("pk_wv", [16, 128, 1024], BF16)
    pk_wproj = dscr("pk_wproj", [16, 128, 2048], BF16)
    pk_q1 = dscr("pk_q1", [512, L], BF16)
    pk_q2 = dscr("pk_q2", [512, L], BF16)
    pk_k1 = dscr("pk_k1", [512, L], BF16)
    pk_k2 = dscr("pk_k2", [512, L], BF16)
    pk_fw1 = dscr("pk_fw1", [32, 128, 1024], BF16)
    pk_fw2 = dscr("pk_fw2", [32, 128, 1024], BF16)
    pk_fh1 = dscr("pk_fh1", [32, 128, 4096], BF16)
    pk_fh2 = dscr("pk_fh2", [32, 128, 4096], BF16)
    pk_fw3 = dscr("pk_fw3", [8, 128, 4096], BF16)
    pk_out2 = dscr("pk_out2", [16, 128, 2048], BF16)
    pk_v = dscr("pk_v", [16, 128, 2048], BF16)

    ctx = ExitStack()
    with ctx:
        ar = Arena(nc, ctx, 207 * 1024)
        ps = PsumPool(nc, ctx)
        mk = MK(nc, ctx)

        def barrier():
            mk.op("sync", lambda s: s.nop())
            marks = {e: mk.mark(e) for e in ENGINES}
            for e in ENGINES:
                for o in ENGINES:
                    if o != e:
                        mk.wait_mark(e, marks[o])

        sem_store = mk.dma_sem("store")
        sem_dmac = mk.dma_sem("dmac")

        def tap(name, ap, mark):
            if name not in taps_requested:
                return
            t = nc.dram_tensor("dbg_" + name, list(ap.shape), ap.dtype,
                               kind="ExternalOutput").ap()
            taps_declared.append(name)
            mk.wait_mark("sync", mark)
            mk.dma("sync", lambda s, a=ap, t=t: s.dma_start(out=t, in_=a),
                   sem_store)

        x_fm = ar.alloc([128, NKT * L], F32, "x_fm")
        ones_col = ar.alloc([128, 1], BF16, "ones_col")
        ones_colf = ar.alloc([128, 1], F32, "ones_colf")
        ones_row = ar.alloc([1, 128], BF16, "ones_row")
        eye_bf = ar.alloc([128, 128], BF16, "eye_bf")
        eye_f32 = ar.alloc([128, 128], F32, "eye_f32")
        sel16 = ar.alloc([16, 16 * 128], BF16, "sel16")
        ones512 = ar.alloc([1, 512], BF16, "ones512")
        ones_rowf = ar.alloc([1, 128], F32, "ones_rowf")
        scal_sb = ar.alloc([128, 8], F32, "scal")
        convw_sb = ar.alloc([128, NDT * 4], F32, "convw")
        convb_sb = ar.alloc([128, NDT], F32, "convb")
        dtb_sb = ar.alloc([128, NDT], F32, "dtb")
        dvec_sb = ar.alloc([128, NDT], F32, "dvec")
        alog_sb = ar.alloc([128, NDT * NST], F32, "alog")
        xprojT = ar.alloc([128, 16 * 96], BF16, "xprojT")
        dtwT = ar.alloc([64, DI], BF16, "dtwT")
        rows = ar.alloc([128, 1536], F32, "rows")
        rows_bf = ar.alloc([128, 1024], BF16, "rows_bf")

        def load_small(dst_ap, src_ap):
            mk.dma("sync", lambda s, d=dst_ap, i=src_ap:
                   s.dma_start(out=d, in_=i), sem_dmac)

        load_small(ones_col[:, :], c_ones_col)
        load_small(ones_colf[:, :], c_ones_colf)
        load_small(ones_row[:, :], c_ones_row)
        load_small(eye_bf[:, :], c_eye)
        load_small(eye_f32[:, :], c_eye_f32)
        load_small(sel16[:, :], c_sel16)
        load_small(scal_sb[:, :], c_scal)
        load_small(ones512[:, :], c_ones512)
        load_small(ones_rowf[:, :], c_ones_rowf)
        for t_ in range(NDT):
            load_small(convw_sb[:, t_ * 4:(t_ + 1) * 4],
                       m_convw[t_ * 128:(t_ + 1) * 128, :])
            load_small(convb_sb[:, t_:t_ + 1],
                       m_convb[t_ * 128:(t_ + 1) * 128, :])
            load_small(dtb_sb[:, t_:t_ + 1],
                       m_dtb[t_ * 128:(t_ + 1) * 128, :])
            load_small(dvec_sb[:, t_:t_ + 1],
                       m_dvec[t_ * 128:(t_ + 1) * 128, :])
            load_small(alog_sb[:, t_ * NST:(t_ + 1) * NST],
                       m_alog[t_ * 128:(t_ + 1) * 128, :])
        v_small = (sem_dmac, mk.dmaval[id(sem_dmac)])

        mk.wait_mark("scalar", v_small)
        mA = mk.op("scalar", lambda s:
                   s.activation(alog_sb[:, :], alog_sb[:, :], AF.Exp))
        mk.wait_mark("vector", mA)
        mk.op("vector", lambda v:
              v.tensor_scalar(alog_sb[:, :], alog_sb[:, :],
                              -1.0, None, ALU.mult))

        ar.push()
        xtok = ar.alloc([128, 2 * D], F32, "xtok")
        TPF = [ps.bank(0, F32, 512, "TPF0"), ps.bank(1, F32, 512, "TPF1")]
        sem_xtok = [mk.dma_sem(f"xtok{i}") for i in range(2)]
        rg_xtok = Ring(mk, 2)
        rg_tpf = Ring(mk, 2)

        for lt in range(16):
            si = rg_xtok.acq("sync")
            buf = xtok[:, si * D:si * D + D]
            mdma = mk.dma("sync", lambda s, b=buf, lt=lt:
                          s.dma_start(out=b,
                                      in_=x_in[lt * 128:(lt + 1) * 128, :]),
                          sem_xtok[si])
            rg_xtok.fin(mdma)
            rg_xtok.rdy("tensor")
            if lt == 0:
                mk.wait_mark("tensor", v_small)
            for kt in range(NKT):
                bi = rg_tpf.acq("tensor")
                pb = TPF[bi]
                mt = mk.op("tensor", lambda t, b=buf, kt=kt, pb=pb:
                           t.transpose(pb[:, 0:128],
                                       b[:, kt * 128:(kt + 1) * 128],
                                       eye_f32[:, :]))
                rg_tpf.fin(mt)
                rg_tpf.rdy("vector")
                mv = mk.op("vector", lambda v, kt=kt, lt=lt, pb=pb:
                           v.tensor_copy(
                               x_fm[:, kt * L + lt * 128:
                                    kt * L + (lt + 1) * 128],
                               pb[:, 0:128]))
                rg_tpf.rel(mv)
            rg_xtok.rel(mt)
        m_xfm = mv

        wnat = ar.alloc([128, 2 * 1024], BF16, "wnat")
        wstg = ar.alloc([128, 2 * 1024], BF16, "wstg")
        TPS = ps.bank(6, BF16, 2048, "TPS", nbanks=2)
        sem_wnat = [mk.dma_sem(f"wnat{i}") for i in range(2)]
        sem_park = [mk.dma_sem(f"park{i}") for i in range(2)]
        sem_park2 = mk.dma_sem("park2")
        rg_wnat = Ring(mk, 2)
        rg_tps = Ring(mk, 1)
        rg_wstg = Ring(mk, 2)

        def park_matrix(w_hbm, nrows, pk, ncols):
            kchunks = ncols // 128
            for rt in range(nrows // 128):
                ni = rg_wnat.acq("gpsimd")
                nat = wnat[:, ni * 1024:ni * 1024 + ncols]
                md = mk.dma("gpsimd", lambda g, nat=nat, rt=rt:
                            g.dma_start(
                                out=nat,
                                in_=w_hbm[rt * 128:(rt + 1) * 128, :]),
                            sem_wnat[ni])
                rg_wnat.fin(md)
                rg_wnat.rdy("tensor")
                rg_tps.acq("tensor")
                for kc in range(kchunks):
                    mt = mk.op("tensor", lambda t, kc=kc, nat=nat:
                               t.transpose(TPS[:, kc * 128:(kc + 1) * 128],
                                           nat[:, kc * 128:(kc + 1) * 128],
                                           eye_bf[:, :]))
                rg_tps.fin(mt)
                rg_wnat.rel(mt)
                sgi = rg_wstg.acq("scalar")
                stg = wstg[:, sgi * 1024:sgi * 1024 + ncols]
                rg_tps.rdy("scalar")
                ms = mk.op("scalar", lambda s, stg=stg, kchunks=kchunks:
                           s.copy(stg, TPS[:, 0:kchunks * 128]))
                rg_tps.rel(ms)
                rg_wstg.fin(ms)
                rg_wstg.rdy("sync")
                mp = mk.dma("sync", lambda s, stg=stg, rt=rt, pk=pk:
                            s.dma_start(out=pk[rt], in_=stg), sem_park[sgi])
                rg_wstg.rel(mp)

        mk.wait_mark("gpsimd", v_small)
        park_matrix(m_inproj, 2 * DI, pk_inproj, D)

        wnat2 = ar.alloc([128, 2048], BF16, "wnat2")
        wstg2 = ar.alloc([128, 2048], BF16, "wstg2")
        sem_wnat2 = mk.dma_sem("wnat2")
        rg_wnat2 = Ring(mk, 1)
        rg_wstg2 = Ring(mk, 1)

        def park_matrix2(w_hbm, nrows, pk):
            for rt in range(nrows // 128):
                rg_wnat2.acq("gpsimd")
                md = mk.dma("gpsimd", lambda g, rt=rt, w_hbm=w_hbm:
                            g.dma_start(
                                out=wnat2[:, :],
                                in_=w_hbm[rt * 128:(rt + 1) * 128, :]),
                            sem_wnat2)
                rg_wnat2.fin(md)
                rg_wnat2.rdy("tensor")
                rg_tps.acq("tensor")
                for kc in range(16):
                    mt = mk.op("tensor", lambda t, kc=kc:
                               t.transpose(TPS[:, kc * 128:(kc + 1) * 128],
                                           wnat2[:, kc * 128:(kc + 1) * 128],
                                           eye_bf[:, :]))
                rg_tps.fin(mt)
                rg_wnat2.rel(mt)
                rg_wstg2.acq("scalar")
                rg_tps.rdy("scalar")
                ms = mk.op("scalar", lambda s: s.copy(wstg2[:, :], TPS[:, :]))
                rg_tps.rel(ms)
                rg_wstg2.fin(ms)
                rg_wstg2.rdy("sync")
                mp = mk.dma("sync", lambda s, rt=rt, pk=pk:
                            s.dma_start(out=pk[rt], in_=wstg2[:, :]),
                            sem_park2)
                rg_wstg2.rel(mp)

        park_matrix2(m_outproj, D, pk_outproj)

        wnat4 = ar.alloc([128, 4096], BF16, "wnat4")
        wstg4 = ar.alloc([128, 4096], BF16, "wstg4")
        sem_wnat4 = mk.dma_sem("wnat4")
        rg_wnat4 = Ring(mk, 1)
        rg_wstg4 = Ring(mk, 1)

        def park_matrix4(w_hbm, nrows, pk):
            for rt in range(nrows // 128):
                rg_wnat4.acq("gpsimd")
                md = mk.dma("gpsimd", lambda g, rt=rt, w_hbm=w_hbm:
                            g.dma_start(
                                out=wnat4[:, :],
                                in_=w_hbm[rt * 128:(rt + 1) * 128, :]),
                            sem_wnat4)
                rg_wnat4.fin(md)
                rg_wnat4.rdy("tensor")
                rg_wstg4.acq("scalar")
                for grp in range(2):
                    rg_tps.acq("tensor")
                    for kc in range(16):
                        mt = mk.op("tensor", lambda t, kc=kc, grp=grp:
                                   t.transpose(
                                       TPS[:, kc * 128:(kc + 1) * 128],
                                       wnat4[:, grp * 2048 + kc * 128:
                                             grp * 2048 + (kc + 1) * 128],
                                       eye_bf[:, :]))
                    rg_tps.fin(mt)
                    if grp == 1:
                        rg_wnat4.rel(mt)
                    rg_tps.rdy("scalar")
                    ms = mk.op("scalar", lambda s, grp=grp:
                               s.copy(wstg4[:, grp * 2048:(grp + 1) * 2048],
                                      TPS[:, :]))
                    rg_tps.rel(ms)
                rg_wstg4.fin(ms)
                rg_wstg4.rdy("sync")
                mp = mk.dma("sync", lambda s, rt=rt, pk=pk:
                            s.dma_start(out=pk[rt], in_=wstg4[:, :]),
                            sem_park2)
                rg_wstg4.rel(mp)

        if upto in ("attn", "ffn"):
            park_matrix2(a_wproj, 2 * D, pk_wproj)
            park_matrix(a_wq, D, pk_wq, D)
            park_matrix(a_wk, D, pk_wk, D)
            park_matrix(a_wv, 2 * D, pk_wv, D)
        if upto == "ffn":
            park_matrix(f_w1, 4096, pk_fw1, D)
            park_matrix(f_w2, 4096, pk_fw2, D)
            park_matrix4(f_h1, 4096, pk_fh1)
            park_matrix4(f_h2, 4096, pk_fh2)
            park_matrix4(f_w3, D, pk_fw3)

        xpn = ar.alloc([96, DI], BF16, "xpn")
        sem_xpn = mk.dma_sem("xpn")
        mk.dma("gpsimd", lambda g: g.dma_start(out=xpn[:, :], in_=m_xproj),
               sem_xpn)
        mk.wait_mark("tensor", (sem_xpn, 16))
        for et in range(16):
            rg_tps.acq("tensor")
            mt = mk.op("tensor", lambda t, et=et:
                       t.transpose(TPS[:, 0:96],
                                   xpn[:, et * 128:(et + 1) * 128],
                                   eye_bf[0:96, 0:96]))
            rg_tps.fin(mt)
            rg_tps.rdy("scalar")
            ms = mk.op("scalar", lambda s, et=et:
                       s.copy(xprojT[:, et * 96:(et + 1) * 96], TPS[:, 0:96]))
            rg_tps.rel(ms)
        dtn = ar.alloc([128, 16 * 64], BF16, "dtn")
        sem_dtn = mk.dma_sem("dtn")
        for t_ in range(16):
            mk.dma("gpsimd", lambda g, t_=t_:
                   g.dma_start(out=dtn[:, t_ * 64:(t_ + 1) * 64],
                               in_=m_dtw[t_ * 128:(t_ + 1) * 128, :]),
                   sem_dtn)
        mk.wait_mark("tensor", (sem_dtn, 256))
        for et in range(16):
            rg_tps.acq("tensor")
            mt = mk.op("tensor", lambda t, et=et:
                       t.transpose(TPS[0:64, 0:128],
                                   dtn[:, et * 64:(et + 1) * 64],
                                   eye_bf[:, :]))
            rg_tps.fin(mt)
            rg_tps.rdy("scalar")
            ms = mk.op("scalar", lambda s, et=et:
                       s.copy(dtwT[0:64, et * 128:(et + 1) * 128],
                              TPS[0:64, 0:128]))
            rg_tps.rel(ms)

        for sp_ in sem_park + [sem_park2]:
            mk.wait_mark("sync", (sp_, mk.dmaval[id(sp_)]))
        barrier()
        ar.pop()

        P_LNS = ps.bank(5, F32, 512, "lnsum")
        P_LNQ = ps.bank(2, F32, 512, "lnsq")

        def emit_ln(col0, out_bf, sq_tmp, aaB, bbB, rg_sq,
                    out_off=0, out_stride=512):
            for kt in range(NKT):
                xs = x_fm[:, kt * L + col0:kt * L + col0 + 512]
                qi = rg_sq.acq("scalar")
                sq = sq_tmp[:, qi * 512:qi * 512 + 512]
                msq = mk.op("scalar", lambda s, xs=xs, sq=sq:
                            s.activation(sq, xs, AF.Square))
                rg_sq.fin(msq)
                mk.op("tensor", lambda t, xs=xs, kt=kt:
                      t.matmul(P_LNS[0:1, 0:512], ones_colf[:, :], xs,
                               start=(kt == 0), stop=(kt == NKT - 1)))
                rg_sq.rdy("tensor")
                mq = mk.op("tensor", lambda t, sq=sq, kt=kt:
                           t.matmul(P_LNQ[0:1, 0:512], ones_col[:, :], sq,
                                    start=(kt == 0), stop=(kt == NKT - 1)))
                rg_sq.rel(mq)
            m_stats = mq
            mk.wait_mark("vector", m_stats)
            r_mu = rows[0:1, 0:512]
            r_var = rows[0:1, 512:1024]
            r_rstd = rows[0:1, 1024:1536]
            r_aa, r_bb = rows_bf[0:1, 0:512], rows_bf[0:1, 512:1024]
            mk.op("vector", lambda v: v.tensor_scalar(
                r_mu, P_LNS[0:1, 0:512], 1.0 / D, None, ALU.mult))
            mk.op("vector", lambda v: v.tensor_scalar(
                r_var, P_LNQ[0:1, 0:512], 1.0 / D, None, ALU.mult))
            mk.op("vector", lambda v: v.tensor_tensor(
                r_rstd, r_mu, r_mu, ALU.mult))
            mv = mk.op("vector", lambda v: v.tensor_tensor(
                r_var, r_var, r_rstd, ALU.subtract))
            mk.wait_mark("scalar", mv)
            mk.op("scalar", lambda s: s.activation(
                r_var, r_var, AF.Ln, bias=scal_sb[0:1, 0:1]))
            msc = mk.op("scalar", lambda s: s.activation(
                r_rstd, r_var, AF.Exp, scale=scal_sb[0:1, 1:2]))
            mk.wait_mark("vector", msc)
            mk.op("vector", lambda v: v.tensor_copy(r_aa, r_rstd))
            mv2 = mk.op("vector", lambda v: v.scalar_tensor_tensor(
                r_bb, r_mu, -1.0, r_rstd, ALU.mult, ALU.mult))
            mk.wait_mark("tensor", mv2)
            mt = mk.op("tensor", lambda t: t.matmul(
                P_LNS[:, 0:512], ones_row[:, :], r_aa,
                start=True, stop=True))
            mk.wait_mark("scalar", mt)
            ms = mk.op("scalar", lambda s: s.copy(aaB[:, :], P_LNS[:, 0:512]))
            mk.wait_mark("tensor", ms)
            mt = mk.op("tensor", lambda t: t.matmul(
                P_LNQ[:, 0:512], ones_row[:, :], r_bb,
                start=True, stop=True))
            mk.wait_mark("scalar", mt)
            ms = mk.op("scalar", lambda s: s.copy(bbB[:, :], P_LNQ[:, 0:512]))
            mk.wait_mark("vector", ms)
            for kt in range(NKT):
                xs = x_fm[:, kt * L + col0:kt * L + col0 + 512]
                ob = out_bf[:, kt * out_stride + out_off:
                            kt * out_stride + out_off + 512]
                mk.op("vector", lambda v, xs=xs, ob=ob:
                      v.tensor_tensor(ob, xs, aaB[:, :], ALU.mult))
                mln = mk.op("vector", lambda v, ob=ob:
                            v.tensor_tensor(ob, ob, bbB[:, :], ALU.add))
            return mln

        ar.push()
        xln_c = ar.alloc([128, NKT * 512], BF16, "xln_c")
        sq_tmp = ar.alloc([128, 2 * 512], BF16, "sq_tmp")
        aaB = ar.alloc([128, 512], F32, "aaB")
        bbB = ar.alloc([128, 512], F32, "bbB")
        xin_sil = ar.alloc([128, NDT * 512], BF16, "xin_sil")
        tails = ar.alloc([128, NDT * 4], F32, "tails")
        bbuf = ar.alloc([128, 520], F32, "bbuf")
        xc_t = ar.alloc([128, 2 * 512], BF16, "xc_t")
        dt_bf = ar.alloc([64, 512], BF16, "dt_bf")
        bc_bf = ar.alloc([16, 2 * 512], BF16, "bc_bf")
        bball = ar.alloc([128, 16 * 512], BF16, "bball")
        cball = ar.alloc([128, 16 * 512], BF16, "cball")
        u_t = ar.alloc([128, 512], F32, "u_t")
        sp_t = ar.alloc([128, 512], F32, "sp_t")
        delta_t = ar.alloc([128, 2 * 512], BF16, "delta_t")
        du_t = ar.alloc([128, 2 * 512], BF16, "du_t")
        da_t = ar.alloc([128, 3 * 512], BF16, "da_t")
        dbu_t = ar.alloc([128, 3 * 512], BF16, "dbu_t")
        h_t = ar.alloc([128, 3 * 512], BF16, "h_t")
        cm_t = ar.alloc([128, 3 * 512], BF16, "cm_t")
        hcarry = ar.alloc([128, 512], F32, "hcarry")
        y_sb = ar.alloc([128, NDT * 512], BF16, "y_sb")
        yt_t = ar.alloc([128, 512], F32, "yt_t")
        z_t = ar.alloc([128, 2 * 512], BF16, "z_t")
        wslA = ar.alloc([128, 2 * 1024], BF16, "wslA")
        wslO = ar.alloc([128, 2048], BF16, "wslO")

        MB = [ps.bank(0, F32, 512, "MB0"), ps.bank(1, F32, 512, "MB1")]
        MC = [ps.bank(3, F32, 512, "MC0"), ps.bank(4, F32, 512, "MC1")]
        MY = [ps.bank(6, F32, 512, "MY0"), ps.bank(7, F32, 512, "MY1")]
        rg_mb = Ring(mk, 2)
        rg_mc = Ring(mk, 2)
        rg_my = Ring(mk, 2)
        rg_sq = Ring(mk, 2)
        rg_wslA = Ring(mk, 2)
        sem_wslA = [mk.dma_sem(f"wslA{i}") for i in range(2)]
        rg_wslO = Ring(mk, 1)
        sem_wslO = [mk.dma_sem("wslO0")]
        rg_usp = Ring(mk, 1)
        rg_delta = Ring(mk, 2)
        rg_da = Ring(mk, 3)
        rg_cm = Ring(mk, 3)
        rg_z = Ring(mk, 2)
        rg_xc = Ring(mk, 2)

        mk.op("vector", lambda v: v.memset(hcarry[:, :], 0.0))
        mk.op("vector", lambda v: v.memset(tails[:, :], 0.0))
        mk.wait_mark("tensor", m_xfm)
        mk.wait_mark("scalar", m_xfm)

        for c in range(N_MAMBA_CHUNKS):
            col0 = c * CH
            mln = emit_ln(col0, xln_c, sq_tmp, aaB, bbB, rg_sq)
            mk.wait_mark("tensor", mln)
            tap(f"xln{c}", xln_c[:, :], mln)

            for et in range(16):
                si = rg_wslA.acq("sync")
                dst = wslA[:, si * 1024:si * 1024 + 1024]
                md = mk.dma("sync", lambda s, et=et, dst=dst:
                            s.dma_start(out=dst, in_=pk_inproj[et]),
                            sem_wslA[si])
                rg_wslA.fin(md)
                bi = rg_mb.acq("tensor")
                pb = MB[bi]
                rg_wslA.rdy("tensor")
                for kc in range(NKT):
                    mm = mk.op("tensor", lambda t, pb=pb, kc=kc, dst=dst:
                               t.matmul(pb[:, 0:512],
                                        dst[:, kc * 128:(kc + 1) * 128],
                                        xln_c[:, kc * 512:(kc + 1) * 512],
                                        start=(kc == 0), stop=(kc == NKT - 1)))
                rg_mb.fin(mm)
                rg_wslA.rel(mm)
                dt = et
                bb = bbuf[:, 0:515]
                rg_mb.rdy("vector")
                mk.op("vector", lambda v, bb=bb, dt=dt:
                      v.tensor_copy(bb[:, 0:3], tails[:, dt * 4:dt * 4 + 3]))
                mv = mk.op("vector", lambda v, bb=bb, pb=pb:
                           v.tensor_copy(bb[:, 3:515], pb[:, 0:512]))
                rg_mb.rel(mv)
                mk.op("vector", lambda v, bb=bb, dt=dt:
                      v.tensor_copy(tails[:, dt * 4:dt * 4 + 3],
                                    bb[:, 512:515]))
                xi = rg_xc.acq("vector")
                xs = xc_t[:, xi * 512:xi * 512 + 512]
                mk.op("vector", lambda v, bb=bb, dt=dt, xs=xs:
                      v.tensor_scalar(xs, bb[:, 0:512],
                                      convw_sb[:, dt * 4:dt * 4 + 1], None,
                                      ALU.mult))
                for k in range(1, 4):
                    mxc = mk.op("vector",
                                lambda v, bb=bb, dt=dt, xs=xs, k=k:
                                v.scalar_tensor_tensor(
                                    xs, bb[:, k:k + 512],
                                    convw_sb[:, dt * 4 + k:dt * 4 + k + 1],
                                    xs, ALU.mult, ALU.add))
                rg_xc.fin(mxc)
                rg_xc.rdy("scalar")
                msil = mk.op("scalar", lambda s, dt=dt, xs=xs:
                             s.activation(xin_sil[:, dt * 512:(dt + 1) * 512],
                                          xs, AF.Silu,
                                          bias=convb_sb[:, dt:dt + 1]))
                rg_xc.rel(msil)
            m_xin = msil
            tap(f"xin{c}", xin_sil[:, :], m_xin)
            tap(f"tails{c}", tails[:, :], m_xin)

            mk.wait_mark("tensor", m_xin)
            groups = [(0, 64, dt_bf[:, :]), (64, 16, bc_bf[:, 0:512]),
                      (80, 16, bc_bf[:, 512:1024])]
            for g0, gw, gdst in groups:
                bi = rg_mc.acq("tensor")
                pxd = MC[bi]
                for et in range(16):
                    mm = mk.op("tensor",
                               lambda t, et=et, pxd=pxd, g0=g0, gw=gw:
                               t.matmul(pxd[0:gw, 0:512],
                                        xprojT[:, et * 96 + g0:
                                               et * 96 + g0 + gw],
                                        xin_sil[:, et * 512:(et + 1) * 512],
                                        start=(et == 0), stop=(et == 15)))
                rg_mc.fin(mm)
                rg_mc.rdy("vector")
                mv = mk.op("vector", lambda v, gdst=gdst, pxd=pxd, gw=gw:
                           v.tensor_copy(gdst, pxd[0:gw, 0:512]))
                rg_mc.rel(mv)
            m_bcrows = mv
            tap(f"xdbl{c}", bc_bf[:, :], m_bcrows)

            mk.wait_mark("tensor", m_bcrows)
            for n in range(NST):
                for which, bcdst in ((0, bball), (1, cball)):
                    bi = rg_mc.acq("tensor")
                    pb = MC[bi]
                    mm = mk.op("tensor", lambda t, n=n, pb=pb, which=which:
                               t.matmul(pb[:, 0:512],
                                        sel16[:, n * 128:(n + 1) * 128],
                                        bc_bf[:, which * 512:
                                              which * 512 + 512],
                                        start=True, stop=True))
                    rg_mc.fin(mm)
                    rg_mc.rdy("scalar")
                    ms = mk.op("scalar", lambda s, n=n, pb=pb, bcdst=bcdst:
                               s.copy(bcdst[:, n * 512:(n + 1) * 512],
                                      pb[:, 0:512]))
                    rg_mc.rel(ms)
            m_bcall = ms
            mk.wait_mark("vector", m_bcall)

            for dt in range(NDT):
                bi = rg_mc.acq("tensor")
                pdt = MC[bi]
                mm = mk.op("tensor", lambda t, dt=dt, pdt=pdt:
                           t.matmul(pdt[:, 0:512],
                                    dtwT[:, dt * 128:(dt + 1) * 128],
                                    dt_bf[:, :], start=True, stop=True))
                rg_mc.fin(mm)
                rg_usp.acq("scalar")
                u = u_t[:, 0:512]
                sp = sp_t[:, 0:512]
                rg_mc.rdy("scalar")
                msp0 = mk.op("scalar", lambda s, u=u, pdt=pdt, dt=dt:
                             s.activation(u, pdt[:, 0:512], AF.Identity,
                                          bias=dtb_sb[:, dt:dt + 1]))
                rg_mc.rel(msp0)
                mk.op("scalar", lambda s, u=u, sp=sp:
                      s.activation(sp, u, AF.Abs))
                mk.op("scalar", lambda s, sp=sp:
                      s.activation(sp, sp, AF.Exp, scale=-1.0))
                msp = mk.op("scalar", lambda s, sp=sp:
                            s.activation(sp, sp, AF.Ln, bias=1.0))
                rg_usp.fin(msp)
                di = rg_delta.acq("vector")
                dl = delta_t[:, di * 512:di * 512 + 512]
                du = du_t[:, di * 512:di * 512 + 512]
                rg_usp.rdy("vector")
                mdl0 = mk.op("vector", lambda v, u=u, sp=sp, dl=dl:
                             v.scalar_tensor_tensor(dl, u, 0.0, sp, ALU.max,
                                                    ALU.add))
                rg_usp.rel(mdl0)
                mdl = mk.op("vector", lambda v, dl=dl, du=du, dt=dt:
                            v.tensor_tensor(
                                du, dl,
                                xin_sil[:, dt * 512:(dt + 1) * 512],
                                ALU.mult))
                rg_delta.fin(mdl)
                rg_delta.rdy("scalar")
                yi = rg_my.acq("tensor")
                py = MY[yi]
                for n in range(NST):
                    idx = dt * NST + n
                    ai = rg_da.acq("scalar")
                    da = da_t[:, ai * 512:ai * 512 + 512]
                    mda = mk.op("scalar", lambda s, da=da, dl=dl, idx=idx:
                                s.activation(da, dl, AF.Exp,
                                             scale=alog_sb[:, idx:idx + 1]))
                    rg_da.fin(mda)
                    if n == NST - 1:
                        rg_delta.rel(mda)
                    ci = rg_cm.acq("vector")
                    dbu = dbu_t[:, ci * 512:ci * 512 + 512]
                    hh = h_t[:, ci * 512:ci * 512 + 512]
                    cm = cm_t[:, ci * 512:ci * 512 + 512]
                    mk.op("vector", lambda v, dbu=dbu, du=du, n=n:
                          v.tensor_tensor(dbu, du,
                                          bball[:, n * 512:(n + 1) * 512],
                                          ALU.mult))
                    rg_da.rdy("vector")
                    msc = mk.op("vector",
                                lambda v, hh=hh, da=da, dbu=dbu, idx=idx:
                                v.tensor_tensor_scan(
                                    hh, da, dbu,
                                    hcarry[:, 2 * idx + 1:2 * idx + 2],
                                    ALU.mult, ALU.add))
                    rg_da.rel(msc)
                    mk.op("vector", lambda v, hh=hh, idx=idx:
                          v.tensor_copy(hcarry[:, 2 * idx:2 * idx + 2],
                                        hh[:, 510:512]))
                    mcm = mk.op("vector", lambda v, cm=cm, hh=hh, n=n:
                                v.tensor_tensor(
                                    cm, hh,
                                    cball[:, n * 512:(n + 1) * 512],
                                    ALU.mult))
                    rg_cm.fin(mcm)
                    rg_cm.rdy("tensor")
                    mns = mk.op("tensor", lambda t, cm=cm, py=py, n=n:
                                t.matmul(py[:, 0:512], eye_bf[:, :], cm,
                                         start=(n == 0), stop=(n == NST - 1)))
                    rg_cm.rel(mns)
                rg_my.fin(mns)
                si = rg_wslA.acq("sync")
                dstz = wslA[:, si * 1024:si * 1024 + 1024]
                md = mk.dma("sync", lambda s, dt=dt, dstz=dstz:
                            s.dma_start(out=dstz, in_=pk_inproj[16 + dt]),
                            sem_wslA[si])
                rg_wslA.fin(md)
                bi = rg_mb.acq("tensor")
                pz = MB[bi]
                rg_wslA.rdy("tensor")
                for kc in range(NKT):
                    mm = mk.op("tensor", lambda t, pz=pz, kc=kc, dstz=dstz:
                               t.matmul(pz[:, 0:512],
                                        dstz[:, kc * 128:(kc + 1) * 128],
                                        xln_c[:, kc * 512:(kc + 1) * 512],
                                        start=(kc == 0), stop=(kc == NKT - 1)))
                rg_mb.fin(mm)
                rg_wslA.rel(mm)
                zi = rg_z.acq("scalar")
                zz = z_t[:, zi * 512:zi * 512 + 512]
                rg_mb.rdy("scalar")
                mz = mk.op("scalar", lambda s, zz=zz, pz=pz:
                           s.activation(zz, pz[:, 0:512], AF.Silu))
                rg_mb.rel(mz)
                rg_z.fin(mz)
                yt = yt_t[:, 0:512]
                rg_z.rdy("vector")
                rg_my.rdy("vector")
                mk.op("vector", lambda v, yt=yt, dt=dt:
                      v.tensor_scalar(yt,
                                      xin_sil[:, dt * 512:(dt + 1) * 512],
                                      dvec_sb[:, dt:dt + 1], None, ALU.mult))
                mv = mk.op("vector", lambda v, yt=yt, py=py:
                           v.tensor_tensor(yt, py[:, 0:512], yt, ALU.add))
                rg_my.rel(mv)
                my5 = mk.op("vector", lambda v, yt=yt, zz=zz, dt=dt:
                            v.tensor_tensor(y_sb[:, dt * 512:(dt + 1) * 512],
                                            yt, zz, ALU.mult))
                rg_z.rel(my5)
            m_y = my5
            tap(f"my{c}", y_sb[:, :], m_y)

            mk.wait_mark("tensor", m_y)
            for ot in range(NKT):
                rg_wslO.acq("sync")
                dst = wslO[:, 0:2048]
                md = mk.dma("sync", lambda s, ot=ot, dst=dst:
                            s.dma_start(out=dst, in_=pk_outproj[ot]),
                            sem_wslO[0])
                rg_wslO.fin(md)
                bi = rg_mb.acq("tensor")
                pb = MB[bi]
                rg_wslO.rdy("tensor")
                for kc in range(NDT):
                    mm = mk.op("tensor", lambda t, pb=pb, kc=kc, dst=dst:
                               t.matmul(pb[:, 0:512],
                                        dst[:, kc * 128:(kc + 1) * 128],
                                        y_sb[:, kc * 512:(kc + 1) * 512],
                                        start=(kc == 0),
                                        stop=(kc == NDT - 1)))
                rg_mb.fin(mm)
                rg_wslO.rel(mm)
                xs = x_fm[:, ot * L + col0:ot * L + col0 + CH]
                rg_mb.rdy("vector")
                mv = mk.op("vector", lambda v, xs=xs, pb=pb:
                           v.tensor_tensor(xs, xs, pb[:, 0:512], ALU.add))
                rg_mb.rel(mv)
        ar.pop()
        barrier()

        if upto in ("moe", "attn", "ffn"):
            ar.push()
            xln_m = ar.alloc([128, NKT * 1024], BF16, "xln_m")
            sq_m = ar.alloc([128, 2 * 512], BF16, "sq_m")
            aa_m = ar.alloc([128, 512], F32, "aa_m")
            bb_m = ar.alloc([128, 512], F32, "bb_m")
            wr_sb = ar.alloc([128, 64], BF16, "wr_sb")
            wrf = ar.alloc([128, 64], F32, "wrf")
            srep = ar.alloc([128, 16], F32, "srep")
            rowt = ar.alloc([1, 16], F32, "rowt")
            acolm = ar.alloc([128, 8], F32, "acolm")
            bcolm = ar.alloc([128, 8], F32, "bcolm")
            br_sb = ar.alloc([8, 1], F32, "br_sb")
            logit_sb = ar.alloc([16, 2048], F32, "logit_sb")
            ltok = ar.alloc([128, 16 * 8], F32, "ltok")
            gtok = ar.alloc([128, 16 * 8], F32, "gtok")
            sel_sbuf = ar.alloc([128, 16], F32, "sel_sbuf")
            grow = ar.alloc([16, 2048], BF16, "grow")
            gcol = ar.alloc([128, 8], F32, "gcol")
            gb_sb = ar.alloc([128, 1024], BF16, "gb_sb")
            b1_sb = ar.alloc([128, 256], F32, "b1_sb")
            b2row = ar.alloc([1, 1024], BF16, "b2row")
            h_sb = ar.alloc([128, 32 * 1024], BF16, "h_sb")
            wsl1 = ar.alloc([128, 2 * 1024], BF16, "wsl1")
            wsl2 = ar.alloc([128, 2 * 1024], BF16, "wsl2")
            ybf = ar.alloc([128, 2 * 512], BF16, "ybf")
            mtm = ar.alloc([128, 2 * 512], BF16, "mtm")

            sem_moec = mk.dma_sem("moec")
            sem_moeg = mk.dma_sem("moeg")

            def g_dma_wrf(s, kc):
                return s.dma_start(out=wrf[:, kc * 8:(kc + 1) * 8],
                                   in_=e_wr[kc * 128:(kc + 1) * 128, :])
            sem_w1sl = [mk.dma_sem(f"w1sl{i}") for i in range(2)]
            sem_w2sl = [mk.dma_sem(f"w2sl{i}") for i in range(2)]
            sem_b2r = mk.dma_sem("b2r")
            rg_w1 = Ring(mk, 2)
            rg_w2 = Ring(mk, 2)
            rg_hp = Ring(mk, 2)
            rg_ybf = Ring(mk, 2)
            rg_b2 = Ring(mk, 1)

            for kc in range(NKT):
                mk.dma("gpsimd", lambda g, kc=kc:
                       g.dma_start(out=wr_sb[:, kc * 8:(kc + 1) * 8],
                                   in_=e_wr[kc * 128:(kc + 1) * 128, :]),
                       sem_moeg)
            mk.dma("sync", lambda s: s.dma_start(out=br_sb[:, :], in_=e_br),
                   sem_moec)
            for kc in range(NKT):
                mk.dma("sync", lambda s, kc=kc:
                       g_dma_wrf(s, kc), sem_moec)
            for e_ in range(8):
                for t_ in range(32):
                    mk.dma("sync", lambda s, e_=e_, t_=t_:
                           s.dma_start(
                               out=b1_sb[:, e_ * 32 + t_:e_ * 32 + t_ + 1],
                               in_=e_b1[e_ * 4096 + t_ * 128:
                                        e_ * 4096 + (t_ + 1) * 128, :]),
                           sem_moec)
            v_moec = (sem_moec, mk.dmaval[id(sem_moec)])
            v_moeg = (sem_moeg, mk.dmaval[id(sem_moeg)])
            m_gz = mk.op("vector", lambda v: v.memset(grow[:, :], 0.0))

            HP = [ps.bank(0, F32, 512, "hp0"), ps.bank(1, F32, 512, "hp1")]
            RT = ps.bank(2, F32, 512, "rt")
            GT = ps.bank(3, F32, 512, "gt")
            YP = [ps.bank(i, F32, 512, f"yp{i}") for i in range(8)]

            mk.wait_mark("tensor", v_moec)
            for kc in range(NKT):
                mm = mk.op("tensor", lambda t, kc=kc:
                           t.matmul(RT[0:1, 0:8], ones_colf[:, :],
                                    wrf[:, kc * 8:(kc + 1) * 8],
                                    start=(kc == 0), stop=(kc == NKT - 1)))
            mt = mk.op("tensor", lambda t:
                       t.transpose(RT[0:1, 8:16], br_sb[:, :],
                                   eye_f32[0:8, 0:8]))
            mk.wait_mark("vector", mt)
            mk.op("vector", lambda v: v.tensor_copy(rowt[0:1, 0:8],
                                                    RT[0:1, 0:8]))
            mvr = mk.op("vector", lambda v: v.tensor_copy(rowt[0:1, 8:16],
                                                          RT[0:1, 8:16]))
            mk.wait_mark("tensor", mvr)
            mm = mk.op("tensor", lambda t:
                       t.matmul(RT[:, 0:16], ones_rowf[:, :],
                                rowt[0:1, 0:16], start=True, stop=True))
            mk.wait_mark("vector", mm)
            msr = mk.op("vector", lambda v: v.tensor_copy(srep[:, 0:16],
                                                          RT[:, 0:16]))

            m_prev_y = (mk.prog["vector"], 0)
            mk.wait_mark("tensor", msr)
            for half in range(2):
                hb = half * 1024
                mk.wait_mark("tensor", m_prev_y)
                if half == 0:
                    mk.wait_mark("tensor", v_moec)
                    mk.wait_mark("tensor", v_moeg)
                mln1 = emit_ln(hb, xln_m, sq_m, aa_m, bb_m, rg_sq,
                               out_off=0, out_stride=1024)
                mln2 = mln1
                m_ab = (mk.prog["vector"], 0)
                for lc in range(2):
                    mk.wait_mark("tensor",
                                 mln1 if lc == 0 else mln2)
                    mk.wait_mark("tensor", m_ab)
                    for blk in range(4):
                        mta = mk.op("tensor", lambda t, blk=blk:
                                    t.transpose(
                                        GT[:, blk:blk + 1],
                                        rows[0:1, 1024 + blk * 128:
                                             1024 + (blk + 1) * 128],
                                        eye_f32[0:1, 0:1]))
                        mtb = mk.op("tensor", lambda t, blk=blk:
                                    t.transpose(
                                        TPS[:, 2 * blk:2 * blk + 1],
                                        rows_bf[0:1, 512 + blk * 128:
                                                512 + (blk + 1) * 128],
                                        eye_bf[0:1, 0:1]))
                    mk.wait_mark("vector", mtb)
                    for blk in range(4):
                        mva = mk.op("vector",
                                    lambda v, lc=lc, blk=blk:
                                    v.tensor_copy(
                                        acolm[:, lc * 4 + blk:
                                              lc * 4 + blk + 1],
                                        GT[:, blk:blk + 1]))
                        mvb = mk.op("vector",
                                    lambda v, lc=lc, blk=blk:
                                    v.tensor_copy(
                                        bcolm[:, lc * 4 + blk:
                                              lc * 4 + blk + 1],
                                        TPS[:, 2 * blk:2 * blk + 1]))
                    m_ab = mvb
                    if lc == 0:
                        mk.wait_mark("vector", mvb)
                        mln2 = emit_ln(hb + 512, xln_m, sq_m, aa_m, bb_m,
                                       rg_sq, out_off=512, out_stride=1024)
                m_gt = (mk.prog["vector"], 0)
                mlast_raw = (mk.prog["vector"], 0)
                for lc in range(2):
                    mk.wait_mark("tensor", m_prev_y)
                    mk.wait_mark("tensor", mlast_raw)
                    tok0 = hb + lc * 512
                    for kc in range(NKT):
                        mm = mk.op("tensor", lambda t, kc=kc, tok0=tok0:
                                   t.matmul(RT[0:8, 0:512],
                                            wrf[:, kc * 8:(kc + 1) * 8],
                                            x_fm[:, kc * L + tok0:
                                                 kc * L + tok0 + 512],
                                            start=(kc == 0),
                                            stop=(kc == NKT - 1)))
                    mk.wait_mark("vector", mm)
                    mlg = mk.op("vector", lambda v, lc=lc:
                                v.tensor_copy(
                                    logit_sb[0:8, hb + lc * 512:
                                             hb + (lc + 1) * 512],
                                    RT[0:8, 0:512]))
                    mlast_raw = mlg
                mk.wait_mark("tensor", mlast_raw)
                for tt in range(8):
                    mk.wait_mark("tensor", m_gt)
                    mt = mk.op("tensor", lambda t, tt=tt:
                               t.transpose(GT[:, 0:8],
                                           logit_sb[0:8, hb + tt * 128:
                                                    hb + (tt + 1) * 128],
                                           eye_f32[0:8, 0:8]))
                    mk.wait_mark("vector", mt)
                    lt_ = ltok[:, tt * 8:(tt + 1) * 8]
                    gt_ = gtok[:, tt * 8:(tt + 1) * 8]
                    mv = mk.op("vector", lambda v, lt_=lt_, tt=tt:
                               v.tensor_scalar(
                                   lt_, GT[:, 0:8],
                                   acolm[:, tt:tt + 1], None, ALU.mult))
                    mv = mk.op("vector", lambda v, lt_=lt_, tt=tt:
                               v.scalar_tensor_tensor(
                                   lt_, srep[:, 0:8],
                                   bcolm[:, tt:tt + 1], lt_,
                                   ALU.mult, ALU.add))
                    mv = mk.op("vector", lambda v, lt_=lt_:
                               v.tensor_tensor(lt_, lt_, srep[:, 8:16],
                                               ALU.add))
                    mk.op("vector", lambda v, lt_=lt_: v.tensor_reduce(
                        gcol[:, 0:1], lt_, mybir.AxisListType.X, ALU.max))
                    mk.op("vector", lambda v: v.tensor_scalar(
                        gcol[:, 1:2], gcol[:, 0:1], -1.0, None, ALU.mult))
                    mk.wait_mark("scalar", mk.mark("vector"))
                    msc = mk.op("scalar", lambda s, lt_=lt_, gt_=gt_:
                                s.activation(gt_, lt_, AF.Exp,
                                             bias=gcol[:, 1:2]))
                    mk.wait_mark("vector", msc)
                    mk.op("vector", lambda v, lt_=lt_: v.tensor_scalar(
                        sel_sbuf[:, 0:8], lt_, gcol[:, 0:1], None, ALU.is_ge))
                    mk.op("vector", lambda v, lt_=lt_:
                          v.scalar_tensor_tensor(
                              sel_sbuf[:, 8:16], sel_sbuf[:, 0:8], -1e30,
                              lt_, ALU.mult, ALU.add))
                    mk.op("vector", lambda v: v.tensor_reduce(
                        gcol[:, 2:3], sel_sbuf[:, 8:16],
                        mybir.AxisListType.X, ALU.max))
                    mk.op("vector", lambda v: v.tensor_tensor(
                        gcol[:, 3:4], gcol[:, 2:3], gcol[:, 1:2], ALU.add))
                    mk.wait_mark("scalar", mk.mark("vector"))
                    msc = mk.op("scalar", lambda s: s.activation(
                        gcol[:, 4:5], gcol[:, 3:4], AF.Exp))
                    mk.wait_mark("vector", msc)
                    mk.op("vector", lambda v: v.tensor_scalar(
                        gcol[:, 5:6], gcol[:, 4:5], 1.0, None, ALU.add))
                    mk.op("vector", lambda v: v.reciprocal(
                        gcol[:, 6:7], gcol[:, 5:6]))
                    mk.op("vector", lambda v, lt_=lt_: v.tensor_scalar(
                        sel_sbuf[:, 0:8], lt_, gcol[:, 2:3], None, ALU.is_ge))
                    mk.op("vector", lambda v, gt_=gt_: v.tensor_tensor(
                        gt_, gt_, sel_sbuf[:, 0:8], ALU.mult))
                    mg = mk.op("vector", lambda v, gt_=gt_: v.tensor_scalar(
                        gt_, gt_, gcol[:, 6:7], None, ALU.mult))
                    mk.wait_mark("tensor", mg)
                    mt2 = mk.op("tensor", lambda t, gt_=gt_:
                                t.transpose(GT[0:8, 8:136], gt_,
                                            eye_f32[:, :]))
                    mk.wait_mark("vector", mt2)
                    mgr = mk.op("vector", lambda v, tt=tt:
                                v.tensor_copy(
                                    grow[0:8, hb + tt * 128:
                                         hb + (tt + 1) * 128],
                                    GT[0:8, 8:136]))
                    m_gt = mgr
                mk.wait_mark("tensor", mln2)
                for e_ in range(8):
                    mk.wait_mark("tensor", mgr if e_ == 0 else m_prev_y)
                    if e_ == 0:
                        mk.wait_mark("tensor", m_gz)
                    for lc in range(2):
                        mm = mk.op("tensor", lambda t, e_=e_, lc=lc:
                                   t.matmul(GT[:, 0:512],
                                            sel16[:, e_ * 128:(e_ + 1) * 128],
                                            grow[0:16, hb + lc * 512:
                                                 hb + (lc + 1) * 512],
                                            start=True, stop=True))
                        mk.wait_mark("scalar", mm)
                        msg = mk.op("scalar", lambda s, lc=lc:
                                    s.copy(gb_sb[:, lc * 512:(lc + 1) * 512],
                                           GT[:, 0:512]))
                        mk.wait_mark("tensor", msg)
                    rg_b2.acq("sync")
                    mb2 = mk.dma("gpsimd", lambda g, e_=e_:
                                 g.dma_start(out=b2row[:, :],
                                             in_=e_b2[e_:e_ + 1, :]),
                                 sem_b2r)
                    rg_b2.fin(mb2)
                    for ht in range(32):
                        wi = rg_w1.acq("gpsimd")
                        dst1 = wsl1[:, wi * 1024:(wi + 1) * 1024]
                        md = mk.dma(
                            "gpsimd", lambda g, e_=e_, ht=ht, dst1=dst1:
                            g.dma_start(
                                out=dst1.rearrange(
                                    "p (kc hh) -> p kc hh", hh=128),
                                in_=e_w1[e_][:, ht * 128:(ht + 1) * 128]
                                .rearrange("(kc p) hh -> p kc hh", p=128)),
                            sem_w1sl[wi])
                        rg_w1.fin(md)
                        rg_w1.rdy("tensor")
                        for lc in range(2):
                            hi = rg_hp.acq("tensor")
                            hp = HP[hi]
                            if half == 0 and e_ == 0 and ht == 0:
                                mk.wait_mark("tensor", m_prev_y)
                            for kc in range(NKT):
                                mm = mk.op(
                                    "tensor",
                                    lambda t, hp=hp, kc=kc, lc=lc, dst1=dst1:
                                    t.matmul(
                                        hp[:, 0:512],
                                        dst1[:, kc * 128:(kc + 1) * 128],
                                        xln_m[:, kc * 1024 + lc * 512:
                                              kc * 1024 + (lc + 1) * 512],
                                        start=(kc == 0),
                                        stop=(kc == NKT - 1)))
                            rg_hp.fin(mm)
                            if lc == 1:
                                rg_w1.rel(mm)
                            rg_hp.rdy("vector")
                            mh = mk.op(
                                "vector",
                                lambda v, hp=hp, e_=e_, ht=ht, lc=lc:
                                v.tensor_scalar(
                                    h_sb[:, ht * 1024 + lc * 512:
                                         ht * 1024 + (lc + 1) * 512],
                                    hp[:, 0:512],
                                    b1_sb[:, e_ * 32 + ht:e_ * 32 + ht + 1],
                                    0.0, ALU.add, ALU.max))
                            rg_hp.rel(mh)
                    m_h_done = mh
                    rg_b2.rdy("tensor")
                    for lc in range(2):
                        mk.wait_mark("tensor", m_h_done)
                        mk.wait_mark("tensor", m_prev_y)
                        for dt in range(8):
                            mm = mk.op("tensor", lambda t, dt=dt:
                                       t.matmul(YP[dt][:, 0:512],
                                                b2row[0:1,
                                                      dt * 128:(dt + 1) * 128],
                                                ones512[0:1, 0:512],
                                                start=True, stop=False))
                        for hc in range(32):
                            wi = rg_w2.acq("gpsimd")
                            dst2 = wsl2[:, wi * 1024:(wi + 1) * 1024]
                            md = mk.dma(
                                "gpsimd", lambda g, e_=e_, hc=hc, dst2=dst2:
                                g.dma_start(
                                    out=dst2,
                                    in_=e_w2[e_][hc * 128:(hc + 1) * 128, :]),
                                sem_w2sl[wi])
                            rg_w2.fin(md)
                            rg_w2.rdy("tensor")
                            for dt in range(8):
                                mm = mk.op(
                                    "tensor",
                                    lambda t, dt=dt, hc=hc, lc=lc, dst2=dst2:
                                    t.matmul(
                                        YP[dt][:, 0:512],
                                        dst2[:, dt * 128:(dt + 1) * 128],
                                        h_sb[:, hc * 1024 + lc * 512:
                                             hc * 1024 + (lc + 1) * 512],
                                        start=False, stop=(hc == 31)))
                                if dt == 7:
                                    rg_w2.rel(mm)
                        m_ymm = mm
                        mk.wait_mark("scalar", m_ymm)
                        for dt in range(8):
                            yi = rg_ybf.acq("scalar")
                            yb = ybf[:, yi * 512:(yi + 1) * 512]
                            msy = mk.op("scalar", lambda s, yb=yb, dt=dt:
                                        s.copy(yb, YP[dt][:, 0:512]))
                            rg_ybf.fin(msy)
                            rg_ybf.rdy("vector")
                            mt_ = mtm[:, (dt % 2) * 512:
                                      (dt % 2) * 512 + 512]
                            mk.op("vector", lambda v, yb=yb, mt_=mt_, lc=lc:
                                  v.tensor_tensor(
                                      mt_, yb,
                                      gb_sb[:, lc * 512:(lc + 1) * 512],
                                      ALU.mult))
                            xs = x_fm[:, dt * L + hb + lc * 512:
                                      dt * L + hb + (lc + 1) * 512]
                            mv = mk.op("vector", lambda v, xs=xs, mt_=mt_:
                                       v.tensor_tensor(xs, xs, mt_, ALU.add))
                            rg_ybf.rel(mv)
                        m_prev_y = mv
                    rg_b2.rel(m_prev_y)
            ar.pop()
            barrier()

        if upto in ("attn", "ffn"):
            ar.push()
            cs1 = ar.alloc([128, L], BF16, "cs1")
            cs2 = ar.alloc([128, L], BF16, "cs2")
            lamv = ar.alloc([64, 4], F32, "lamv")
            alph = ar.alloc([1, 4], F32, "alph")
            srow = ar.alloc([1, 1024], F32, "srow")
            srow_bf = ar.alloc([1, 1024], BF16, "srow_bf")
            ccol = ar.alloc([128, 16], F32, "ccol")
            hmask_sb = ar.alloc([128, 4], BF16, "hmask")

            sem_ac = mk.dma_sem("ac")
            mk.dma("sync", lambda s: s.dma_start(out=hmask_sb[:, :],
                                                 in_=c_hmask), sem_ac)
            mk.dma("sync", lambda s: s.dma_start(out=cs1[:, :], in_=c_cs1),
                   sem_ac)
            mk.dma("sync", lambda s: s.dma_start(out=cs2[:, :], in_=c_cs2),
                   sem_ac)
            for i, src in enumerate((a_lq1, a_lk1, a_lq2, a_lk2)):
                mk.dma("sync", lambda s, i=i, src=src:
                       s.dma_start(out=lamv[:, i:i + 1], in_=src), sem_ac)
            mk.dma("sync", lambda s: s.dma_start(out=alph[0:1, 0:1],
                                                 in_=a_alpha), sem_ac)
            v_ac = (sem_ac, mk.dmaval[id(sem_ac)])

            ar.push()
            sq_a = ar.alloc([128, 2 * 512], BF16, "sq_a")
            aa_a = ar.alloc([128, 512], F32, "aa_a")
            bb_a = ar.alloc([128, 512], F32, "bb_a")
            xln_a = ar.alloc([128, NKT * 512], BF16, "xln_a")
            wslq = ar.alloc([128, 2 * 1024], BF16, "wslq")
            qkev = ar.alloc([128, 4 * 512], BF16, "qkev")
            shf_t = ar.alloc([128, 512], BF16, "shf_t")
            t1_t = ar.alloc([128, 512], BF16, "t1_t")
            sqb_a = ar.alloc([128, 512], BF16, "sqb_a")

            AP0 = ps.bank(0, F32, 512, "ap0")
            AP1 = ps.bank(1, F32, 512, "ap1")
            sem_aw = [mk.dma_sem(f"aw{i}") for i in range(2)]
            sem_qv = [mk.dma_sem(f"qv{i}") for i in range(4)]
            rg_aw = Ring(mk, 2)
            rg_qe = Ring(mk, 4)
            PERM32 = [(i + 16) % 32 for i in range(32)]

            def park_dst(ot, lc):
                c0, c1 = lc * 512, (lc + 1) * 512
                if ot < 4:
                    return pk_q1[ot * 128:(ot + 1) * 128, c0:c1]
                if ot < 8:
                    return pk_q2[(ot - 4) * 128:(ot - 3) * 128, c0:c1]
                if ot < 12:
                    return pk_k1[(ot - 8) * 128:(ot - 7) * 128, c0:c1]
                if ot < 16:
                    return pk_k2[(ot - 12) * 128:(ot - 11) * 128, c0:c1]
                return pk_v[ot - 16][:, c0:c1]

            mk.wait_mark("vector", v_ac)
            mk.op("vector", lambda v: v.memset(ccol[:, 0:4], 0.0))
            m_ev = [(mk.prog["vector"], 0)] * 2
            m_nrmm = (mk.prog["tensor"], 0)
            m_nrd = (mk.prog["vector"], 0)
            for lc in range(4):
                mln = emit_ln(lc * 512, xln_a, sq_a, aa_a, bb_a, rg_sq)
                mk.wait_mark("tensor", mln)
                for ot in range(32):
                    wi = rg_aw.acq("sync")
                    dsl = wslq[:, wi * 1024:(wi + 1) * 1024]
                    pkw = pk_wq[ot] if ot < 8 else (
                        pk_wk[ot - 8] if ot < 16 else pk_wv[ot - 16])
                    md = mk.dma("sync", lambda s, pkw=pkw, dsl=dsl:
                                s.dma_start(out=dsl, in_=pkw), sem_aw[wi])
                    rg_aw.fin(md)
                    rg_aw.rdy("tensor")
                    pb = AP0 if ot % 2 == 0 else AP1
                    mk.wait_mark("tensor", m_ev[ot % 2])
                    for kc in range(NKT):
                        mm = mk.op(
                            "tensor", lambda t, pb=pb, kc=kc, dsl=dsl:
                            t.matmul(pb[:, 0:512],
                                     dsl[:, kc * 128:(kc + 1) * 128],
                                     xln_a[:, kc * 512:(kc + 1) * 512],
                                     start=(kc == 0), stop=(kc == NKT - 1)))
                    rg_aw.rel(mm)
                    qi = rg_qe.acq("vector")
                    buf = qkev[:, qi * 512:(qi + 1) * 512]
                    mk.wait_mark("vector", mm)
                    mcv = mk.op("vector", lambda v, buf=buf, pb=pb:
                                v.tensor_copy(buf, pb[:, 0:512]))
                    m_ev[ot % 2] = mcv
                    mfin = mcv
                    if ot < 16:
                        c1s = cs1[:, lc * 512:(lc + 1) * 512]
                        c2s = cs2[:, lc * 512:(lc + 1) * 512]
                        mk.op("vector", lambda v, buf=buf:
                              v.stream_shuffle(shf_t[:, :], buf, PERM32))
                        mk.op("vector", lambda v, buf=buf, c1s=c1s:
                              v.tensor_tensor(t1_t[:, :], buf, c1s,
                                              ALU.mult))
                        mk.op("vector", lambda v, c2s=c2s:
                              v.tensor_tensor(shf_t[:, :], shf_t[:, :],
                                              c2s, ALU.mult))
                        m_rp = mk.op("vector", lambda v, buf=buf:
                                     v.tensor_tensor(buf, t1_t[:, :],
                                                     shf_t[:, :], ALU.add))
                        vi = ot // 4
                        mk.wait_mark("scalar", m_nrmm)
                        mk.wait_mark("scalar", m_rp)
                        m_sq = mk.op("scalar", lambda s, buf=buf:
                                     s.activation(sqb_a[:, :], buf,
                                                  AF.Square))
                        mk.wait_mark("tensor", m_sq)
                        mk.wait_mark("tensor", m_nrd)
                        m_nrmm = mk.op("tensor", lambda t: t.matmul(
                            P_LNS[0:4, 0:512], hmask_sb[:, :],
                            sqb_a[:, :], start=True, stop=True))
                        mk.wait_mark("vector", m_nrmm)
                        mk.op("vector", lambda v: v.tensor_reduce(
                            ccol[0:4, 8:9], P_LNS[0:4, 0:512],
                            mybir.AxisListType.X, ALU.max))
                        m_nrd = mfin = mk.op(
                            "vector", lambda v, vi=vi:
                            v.tensor_tensor(ccol[0:4, vi:vi + 1],
                                            ccol[0:4, vi:vi + 1],
                                            ccol[0:4, 8:9], ALU.max))
                    rg_qe.fin(mfin)
                    rg_qe.rdy("sync")
                    mpk = mk.dma("sync",
                                 lambda s, buf=buf, d=park_dst(ot, lc):
                                 s.dma_start(out=d, in_=buf), sem_qv[qi])
                    rg_qe.rel(mpk)
            ar.pop()
            barrier()

            mk.op("vector", lambda v: v.tensor_tensor(
                lamv[:, 0:1], lamv[:, 0:1], lamv[:, 1:2], ALU.mult))
            mvl = mk.op("vector", lambda v: v.tensor_tensor(
                lamv[:, 2:3], lamv[:, 2:3], lamv[:, 3:4], ALU.mult))
            mk.wait_mark("tensor", mvl)
            mk.op("tensor", lambda t: t.matmul(
                AP0[0:1, 256:257], ones_colf[0:64, :], lamv[:, 0:1],
                start=True, stop=True))
            mtl = mk.op("tensor", lambda t: t.matmul(
                AP0[0:1, 257:258], ones_colf[0:64, :], lamv[:, 2:3],
                start=True, stop=True))
            mk.wait_mark("scalar", mtl)
            msl = mk.op("scalar", lambda s: s.activation(
                alph[0:1, 1:3], AP0[0:1, 256:258], AF.Exp))
            mk.wait_mark("vector", msl)
            mk.op("vector", lambda v: v.tensor_tensor(
                alph[0:1, 1:2], alph[0:1, 1:2], alph[0:1, 2:3],
                ALU.subtract))
            mk.op("vector", lambda v: v.tensor_scalar(
                alph[0:1, 1:2], alph[0:1, 1:2], 0.2, None, ALU.add))
            mk.op("vector", lambda v: v.tensor_scalar(
                alph[0:1, 3:4], alph[0:1, 0:1], 2.0, 1.0, ALU.mult,
                ALU.add))
            mk.op("vector", lambda v: v.tensor_scalar(
                alph[0:1, 3:4], alph[0:1, 3:4], 1.0 / 3.14159265358979,
                None, ALU.mult))
            mtb = mk.op("tensor", lambda t: t.transpose(
                AP0[0:4, 0:4], ccol[0:4, 0:4], eye_f32[0:4, 0:4]))
            mk.wait_mark("vector", mtb)
            mvb = mk.op("vector", lambda v: v.tensor_reduce(
                ccol[0:4, 8:9], AP0[0:4, 0:4], mybir.AxisListType.X,
                ALU.max))
            mk.wait_mark("tensor", mvb)
            mtb2 = mk.op("tensor", lambda t: t.transpose(
                AP0[0:1, 128:132], ccol[0:4, 8:9], eye_f32[0:4, 0:4]))
            mk.wait_mark("vector", mtb2)
            mk.op("vector", lambda v: v.tensor_copy(ccol[0:1, 12:16],
                                                    AP0[0:1, 128:132]))
            mk.op("vector", lambda v: v.tensor_tensor(
                ccol[0:1, 9:10], ccol[0:1, 12:13], ccol[0:1, 14:15],
                ALU.mult))
            mvq = mk.op("vector", lambda v: v.tensor_tensor(
                ccol[0:1, 10:11], ccol[0:1, 13:14], ccol[0:1, 15:16],
                ALU.mult))
            mk.wait_mark("scalar", mvq)
            mk.op("scalar", lambda s: s.activation(
                ccol[0:1, 9:11], ccol[0:1, 9:11], AF.Ln))
            msq3 = mk.op("scalar", lambda s: s.activation(
                ccol[0:1, 9:11], ccol[0:1, 9:11], AF.Exp,
                scale=scal_sb[0:1, 3:4]))
            mk.wait_mark("vector", msq3)
            mk.op("vector", lambda v: v.tensor_scalar(
                ccol[0:1, 9:11], ccol[0:1, 9:11], -0.125, None, ALU.mult))
            mvb2 = mk.op("vector", lambda v: v.tensor_tensor(
                ccol[0:1, 9:10], ccol[0:1, 9:10], ccol[0:1, 10:11],
                ALU.min))
            mk.wait_mark("tensor", mvb2)
            mtb3 = mk.op("tensor", lambda t: t.matmul(
                AP0[:, 0:1], ones_rowf[:, :], ccol[0:1, 9:10],
                start=True, stop=True))
            mk.wait_mark("vector", mtb3)
            m_setup = mk.op("vector", lambda v: v.tensor_copy(
                ccol[:, 2:3], AP0[:, 0:1]))

            ar.push()
            qs1 = ar.alloc([32, L], BF16, "qs1")
            qs2 = ar.alloc([32, L], BF16, "qs2")
            ks1 = ar.alloc([32, L], BF16, "ks1")
            ks2 = ar.alloc([32, L], BF16, "ks2")
            vh = ar.alloc([128, L], BF16, "vh")
            vtok = ar.alloc([128, L], BF16, "vtok")
            ebuf = ar.alloc([128, 3 * 1024], BF16, "ebuf")
            rbc = ar.alloc([128, 1024], BF16, "rbc")
            cb1 = ar.alloc([128, 512], BF16, "cb1")
            cb2 = ar.alloc([128, 512], BF16, "cb2")
            oT = ar.alloc([128, L], BF16, "oT")
            o2h = ar.alloc([128, L], BF16, "o2h")
            sqa2 = ar.alloc([128, 512], BF16, "sqa2")
            gnc = ar.alloc([128, 16], F32, "gnc")

            SD = [ps.bank(0, F32, 1024, "sd0", nbanks=2),
                  ps.bank(2, F32, 1024, "sd1", nbanks=2)]
            AV1 = ps.bank(4, F32, 512, "av1")
            AV2 = ps.bank(5, F32, 512, "av2")
            SSD = ps.bank(6, F32, 1024, "ssd", nbanks=2)
            SB1 = SD[0]
            sem_stg = mk.dma_sem("stg")
            sem_vh = mk.dma_sem("vh")
            sem_o2 = mk.dma_sem("o2")
            rg_stg = Ring(mk, 1)
            rg_vh = Ring(mk, 1)
            rg_o2h = Ring(mk, 1)
            rg_eb = Ring(mk, 3)

            for sp_ in sem_qv:
                mk.wait_mark("sync", (sp_, mk.dmaval[id(sp_)]))
            m_comb = (mk.prog["vector"], 0)
            m_avlast = (mk.prog["tensor"], 0)
            m_stT = (mk.prog["tensor"], 0)
            m_o2dma = (sem_o2, 0)
            for h in range(16):
                rg_stg.acq("sync")
                for stg, src in ((qs1, pk_q1), (qs2, pk_q2),
                                 (ks1, pk_k1), (ks2, pk_k2)):
                    md = mk.dma("sync", lambda s, stg=stg, src=src, h=h:
                                s.dma_start(
                                    out=stg[:, :],
                                    in_=src[h * 32:(h + 1) * 32, :]),
                                sem_stg)
                rg_stg.fin(md)
                rg_vh.acq("sync")
                mvhd = mk.dma("sync", lambda s, h=h:
                              s.dma_start(out=vh[:, :], in_=pk_v[h]),
                              sem_vh)
                rg_vh.fin(mvhd)
                rg_vh.rdy("tensor")
                mk.wait_mark("vector", m_avlast)
                for b_ in range(16):
                    mtv = mk.op("tensor", lambda t, b_=b_:
                                t.transpose(
                                    TPS[:, (b_ % 8) * 128:
                                        (b_ % 8 + 1) * 128],
                                    vh[:, b_ * 128:(b_ + 1) * 128],
                                    eye_bf[:, :]))
                    if b_ % 8 == 7:
                        mk.wait_mark("vector", mtv)
                        mve = mk.op("vector", lambda v, b_=b_:
                                    v.tensor_copy(
                                        vtok[:, (b_ - 7) * 128:
                                             (b_ + 1) * 128],
                                        TPS[:, 0:1024]))
                        mk.wait_mark("tensor", mve)
                m_vtok = mve
                mvhlast = None
                rg_stg.rdy("tensor")
                if h == 0:
                    mk.wait_mark("tensor", m_setup)
                mk.wait_mark("vector", m_stT)
                for qc in range(4):
                    m_exp_r = [None, None]
                    eslots = []

                    def emit_av(kc):
                        ee, mex = eslots[kc]
                        mk.wait_mark("tensor", mex)
                        if kc == 0:
                            mk.wait_mark("tensor", m_vtok)
                        mk.op("tensor", lambda t, kc=kc, ee=ee:
                              t.matmul(
                                  AV1[:, 0:512],
                                  vtok[:, kc * 128:(kc + 1) * 128],
                                  ee[:, 0:512], start=(kc == 0),
                                  stop=(kc == 15)))
                        ma2 = mk.op("tensor", lambda t, kc=kc, ee=ee:
                                    t.matmul(
                                        AV2[:, 0:512],
                                        vtok[:, kc * 128:(kc + 1) * 128],
                                        ee[:, 512:1024],
                                        start=(kc == 0), stop=(kc == 15)))
                        mk.op("tensor", lambda t, kc=kc, ee=ee:
                              t.matmul(SSD[0:1, 0:512],
                                       ones_col[:, :], ee[:, 0:512],
                                       start=(kc == 0), stop=(kc == 15)))
                        mss = mk.op("tensor", lambda t, kc=kc, ee=ee:
                                    t.matmul(SSD[0:1, 512:1024],
                                             ones_col[:, :],
                                             ee[:, 512:1024],
                                             start=(kc == 0),
                                             stop=(kc == 15)))
                        rg_eb.rel(mss)
                        return ma2, mss

                    for kc in range(16):
                        r = kc % 2
                        if kc == 0:
                            mk.wait_mark("tensor", m_comb)
                        if m_exp_r[r] is not None:
                            mk.wait_mark("tensor", m_exp_r[r])
                        mm1 = mk.op("tensor", lambda t, r=r, qc=qc, kc=kc:
                                    t.matmul(
                                        SD[r][:, 0:512],
                                        ks1[:, kc * 128:(kc + 1) * 128],
                                        qs1[:, qc * 512:(qc + 1) * 512],
                                        start=True, stop=True))
                        mm2 = mk.op("tensor", lambda t, r=r, qc=qc, kc=kc:
                                    t.matmul(
                                        SD[r][:, 512:1024],
                                        ks2[:, kc * 128:(kc + 1) * 128],
                                        qs2[:, qc * 512:(qc + 1) * 512],
                                        start=True, stop=True))
                        mk.wait_mark("scalar", mm2)
                        ei = rg_eb.acq("scalar")
                        ee = ebuf[:, ei * 1024:(ei + 1) * 1024]
                        mex = mk.op("scalar", lambda s, ee=ee, r=r:
                                    s.activation(ee, SD[r][:, 0:1024],
                                                 AF.Exp,
                                                 bias=ccol[:, 2:3],
                                                 scale=scal_sb[:, 2:3]))
                        rg_eb.fin(mex)
                        m_exp_r[r] = mex
                        eslots.append((ee, mex))
                        if kc > 0:
                            ma2, mss = emit_av(kc - 1)
                    ma2, mss = emit_av(15)
                    mvhlast = ma2
                    m_avlast = ma2
                    mk.wait_mark("vector", mss)
                    mk.op("vector", lambda v: v.reciprocal(
                        srow[0:1, 0:512], SSD[0:1, 0:512]))
                    mk.op("vector", lambda v: v.reciprocal(
                        srow[0:1, 512:1024], SSD[0:1, 512:1024]))
                    mk.op("vector", lambda v: v.tensor_copy(
                        srow_bf[0:1, 0:512], srow[0:1, 0:512]))
                    mvr = mk.op("vector", lambda v: v.tensor_scalar(
                        srow_bf[0:1, 512:1024], srow[0:1, 512:1024],
                        alph[0:1, 1:2], None, ALU.mult))
                    mk.wait_mark("tensor", mvr)
                    mk.wait_mark("tensor", m_exp_r[0])
                    mk.wait_mark("tensor", m_exp_r[1])
                    mk.op("tensor", lambda t: t.matmul(
                        SD[0][:, 0:512], ones_row[:, :],
                        srow_bf[0:1, 0:512], start=True, stop=True))
                    mtr2 = mk.op("tensor", lambda t: t.matmul(
                        SD[0][:, 512:1024], ones_row[:, :],
                        srow_bf[0:1, 512:1024], start=True, stop=True))
                    mk.wait_mark("vector", mtr2)
                    mk.op("vector", lambda v: v.tensor_copy(
                        rbc[:, 0:1024], SD[0][:, 0:1024]))
                    mk.op("vector", lambda v: v.tensor_tensor(
                        cb1[:, :], AV1[:, 0:512], rbc[:, 0:512],
                        ALU.mult))
                    mk.op("vector", lambda v: v.tensor_tensor(
                        cb2[:, :], AV2[:, 0:512], rbc[:, 512:1024],
                        ALU.mult))
                    m_comb = mk.op("vector", lambda v, qc=qc:
                                   v.tensor_tensor(
                                       oT[:, qc * 512:(qc + 1) * 512],
                                       cb1[:, :], cb2[:, :],
                                       ALU.subtract))
                    mk.op("vector", lambda v, qc=qc: v.tensor_reduce(
                        gnc[:, qc:qc + 1],
                        oT[:, qc * 512:(qc + 1) * 512],
                        mybir.AxisListType.X, ALU.add))
                    mk.wait_mark("scalar", m_comb)
                    msq2 = mk.op("scalar", lambda s, qc=qc: s.activation(
                        sqa2[:, :], oT[:, qc * 512:(qc + 1) * 512],
                        AF.Square))
                    mk.wait_mark("vector", msq2)
                    mk.op("vector", lambda v, qc=qc: v.tensor_reduce(
                        gnc[:, 4 + qc:5 + qc], sqa2[:, :],
                        mybir.AxisListType.X, ALU.add))
                for a_, b_ in ((0, 1), (2, 3), (0, 2),
                               (4, 5), (6, 7), (4, 6)):
                    mgn = mk.op("vector", lambda v, a_=a_, b_=b_:
                                v.tensor_tensor(gnc[:, a_:a_ + 1],
                                                gnc[:, a_:a_ + 1],
                                                gnc[:, b_:b_ + 1],
                                                ALU.add))
                mk.wait_mark("tensor", mgn)
                mgt = mk.op("tensor", lambda t: t.transpose(
                    SB1[0:1, 0:128], gnc[:, 0:1], eye_f32[:, :]))
                mgt = mk.op("tensor", lambda t: t.transpose(
                    SB1[0:1, 128:256], gnc[:, 4:5], eye_f32[:, :]))
                mk.wait_mark("vector", mgt)
                mk.op("vector", lambda v: v.tensor_reduce(
                    srow[0:1, 0:1], SB1[0:1, 0:128],
                    mybir.AxisListType.X, ALU.add))
                mk.op("vector", lambda v: v.tensor_reduce(
                    srow[0:1, 1:2], SB1[0:1, 128:256],
                    mybir.AxisListType.X, ALU.add))
                mk.op("vector", lambda v: v.tensor_scalar(
                    srow[0:1, 0:2], srow[0:1, 0:2], 1.0 / 262144.0, None,
                    ALU.mult))
                mk.op("vector", lambda v: v.tensor_tensor(
                    srow[0:1, 2:3], srow[0:1, 0:1], srow[0:1, 0:1],
                    ALU.mult))
                mvv = mk.op("vector", lambda v: v.tensor_tensor(
                    srow[0:1, 1:2], srow[0:1, 1:2], srow[0:1, 2:3],
                    ALU.subtract))
                mk.wait_mark("scalar", mvv)
                mk.op("scalar", lambda s: s.activation(
                    srow[0:1, 1:2], srow[0:1, 1:2], AF.Ln,
                    bias=scal_sb[0:1, 0:1]))
                msv = mk.op("scalar", lambda s: s.activation(
                    srow[0:1, 1:2], srow[0:1, 1:2], AF.Exp,
                    scale=scal_sb[0:1, 1:2]))
                mk.wait_mark("vector", msv)
                mk.op("vector", lambda v: v.tensor_scalar(
                    srow[0:1, 1:2], srow[0:1, 1:2], 0.8, None, ALU.mult))
                mgn2 = mk.op("vector", lambda v: v.scalar_tensor_tensor(
                    srow[0:1, 2:3], srow[0:1, 0:1], -1.0, srow[0:1, 1:2],
                    ALU.mult, ALU.mult))
                mk.wait_mark("tensor", mgn2)
                mgt2 = mk.op("tensor", lambda t: t.matmul(
                    SB1[:, 0:2], ones_rowf[:, :], srow[0:1, 1:3],
                    start=True, stop=True))
                mk.wait_mark("vector", mgt2)
                mk.op("vector", lambda v: v.tensor_copy(ccol[:, 8:10],
                                                        SB1[:, 0:2]))
                mnrm = mk.op("vector", lambda v: v.tensor_scalar(
                    oT[:, :], oT[:, :], ccol[:, 8:9], ccol[:, 9:10],
                    ALU.mult, ALU.add))
                mk.wait_mark("tensor", mnrm)
                rg_o2h.acq("tensor")
                oT3 = oT.rearrange("p (x s) -> p x s", s=16)
                for qq in range(16):
                    mtq = mk.op("tensor", lambda t, qq=qq:
                                t.transpose(
                                    TPS[:, (qq % 8) * 128:
                                        (qq % 8 + 1) * 128],
                                    oT3[:, :, qq:qq + 1].rearrange(
                                        "p x s -> p (x s)"),
                                    eye_bf[:, :]))
                    if qq % 8 == 7:
                        mk.wait_mark("vector", mtq)
                        mo2 = mk.op("vector", lambda v, qq=qq:
                                    v.tensor_copy(
                                        o2h[:, (qq - 7) * 128:
                                            (qq + 1) * 128],
                                        TPS[:, 0:1024]))
                        mk.wait_mark("tensor", mo2)
                m_stT = mtq
                rg_o2h.fin(mo2)
                rg_o2h.rdy("sync")
                mo2d = mk.dma("sync", lambda s, h=h:
                              s.dma_start(out=pk_out2[h], in_=o2h[:, :]),
                              sem_o2)
                rg_o2h.rel(mo2d)
                m_o2dma = mo2d
                rg_stg.rel(mm2)
                rg_vh.rel(mvhlast if mvhlast else m_vtok)
            ar.pop()
            barrier()

            ar.push()
            olc = ar.alloc([128, 16 * 512], BF16, "olc")
            wslp = ar.alloc([128, 2 * 2048], BF16, "wslp")
            vpart = ar.alloc([128, 512], BF16, "vpart")
            abuf = ar.alloc([128, 512], F32, "abuf")
            rbufC = ar.alloc([128, 512], F32, "rbufC")
            tbufC = ar.alloc([128, 512], F32, "tbufC")
            gbuf = ar.alloc([128, 512], BF16, "gbuf")
            mbuf = ar.alloc([128, 512], BF16, "mbuf")
            c1col = ar.alloc([128, 1], F32, "c1col")
            sem_olc = mk.dma_sem("olc")
            sem_wp = [mk.dma_sem(f"wp{i}") for i in range(2)]
            rg_wp = Ring(mk, 2)
            rg_olc = Ring(mk, 1)
            BP0 = ps.bank(0, F32, 512, "ap0")
            BP1 = ps.bank(1, F32, 512, "ap1")

            mc1 = mk.op("tensor", lambda t: t.matmul(
                BP0[:, 0:1], ones_rowf[:, :], alph[0:1, 3:4],
                start=True, stop=True))
            mk.wait_mark("vector", mc1)
            mc1v = mk.op("vector", lambda v: v.tensor_copy(c1col[:, :],
                                                           BP0[:, 0:1]))
            m_ev0 = mc1v
            m_ev1 = mc1v
            m_gate = (mk.prog["vector"], 0)
            mk.wait_mark("sync", (sem_o2, mk.dmaval[id(sem_o2)]))
            for lc in range(4):
                rg_olc.acq("sync")
                for cc in range(16):
                    mol = mk.dma("sync", lambda s, cc=cc, lc=lc:
                                 s.dma_start(
                                     out=olc[:, cc * 512:(cc + 1) * 512],
                                     in_=pk_out2[cc][:, lc * 512:
                                                     (lc + 1) * 512]),
                                 sem_olc)
                rg_olc.fin(mol)
                rg_olc.rdy("tensor")
                for op_ in range(8):
                    for which in range(2):
                        ot = (8 + op_) if which == 0 else op_
                        wi = rg_wp.acq("sync")
                        dsl = wslp[:, wi * 2048:(wi + 1) * 2048]
                        md = mk.dma("sync", lambda s, ot=ot, dsl=dsl:
                                    s.dma_start(out=dsl,
                                                in_=pk_wproj[ot]),
                                    sem_wp[wi])
                        rg_wp.fin(md)
                        rg_wp.rdy("tensor")
                        pb = BP0 if which == 0 else BP1
                        mk.wait_mark("tensor",
                                     m_ev0 if which == 0 else m_ev1)
                        for cc in range(16):
                            mm = mk.op(
                                "tensor",
                                lambda t, pb=pb, cc=cc, dsl=dsl:
                                t.matmul(
                                    pb[:, 0:512],
                                    dsl[:, cc * 128:(cc + 1) * 128],
                                    olc[:, cc * 512:(cc + 1) * 512],
                                    start=(cc == 0), stop=(cc == 15)))
                        rg_wp.rel(mm)
                        if which == 0:
                            mk.wait_mark("scalar", mm)
                            mk.wait_mark("scalar", m_gate)
                            m_ev0 = mk.op("scalar", lambda s:
                                          s.copy(vpart[:, :],
                                                 BP0[:, 0:512]))
                        else:
                            mk.wait_mark("scalar", mm)
                            mar = mk.op("scalar", lambda s:
                                        s.activation(abuf[:, :],
                                                     BP1[:, 0:512],
                                                     AF.Abs))
                            mk.wait_mark("vector", mar)
                            mk.op("vector", lambda v: v.reciprocal(
                                rbufC[:, :], abuf[:, :]))
                            m_u = mk.op("vector", lambda v:
                                        v.tensor_tensor(
                                            rbufC[:, :], abuf[:, :],
                                            rbufC[:, :], ALU.min))
                            mk.wait_mark("scalar", m_u)
                            m_t = mk.op("scalar", lambda s:
                                        s.activation(tbufC[:, :],
                                                     rbufC[:, :],
                                                     AF.Arctan))
                            mk.op("vector", lambda v: v.tensor_scalar(
                                abuf[:, :], abuf[:, :], 1.0, None,
                                ALU.is_ge))
                            mk.wait_mark("vector", m_t)
                            mk.op("vector", lambda v: v.tensor_scalar(
                                rbufC[:, :], abuf[:, :], -2.0, 1.0,
                                ALU.mult, ALU.add))
                            mk.op("vector", lambda v: v.tensor_tensor(
                                rbufC[:, :], tbufC[:, :], rbufC[:, :],
                                ALU.mult))
                            mk.op("vector", lambda v:
                                  v.scalar_tensor_tensor(
                                      rbufC[:, :], abuf[:, :],
                                      1.5707963267948966, rbufC[:, :],
                                      ALU.mult, ALU.add))
                            m_ev1 = mk.op("vector", lambda v:
                                          v.tensor_scalar(
                                              abuf[:, :], BP1[:, 0:512],
                                              0.0, None, ALU.is_ge))
                            mk.op("vector", lambda v: v.tensor_scalar(
                                abuf[:, :], abuf[:, :], 2.0, -1.0,
                                ALU.mult, ALU.add))
                            mk.op("vector", lambda v: v.tensor_tensor(
                                abuf[:, :], rbufC[:, :], abuf[:, :],
                                ALU.mult))
                            mk.wait_mark("vector", m_ev0)
                            mk.op("vector", lambda v: v.tensor_scalar(
                                gbuf[:, :], abuf[:, :], c1col[:, :],
                                0.5, ALU.mult, ALU.add))
                            m_gate = mk.op("vector", lambda v:
                                           v.tensor_tensor(
                                               mbuf[:, :], gbuf[:, :],
                                               vpart[:, :], ALU.mult))
                            xs = x_fm[:, op_ * L + lc * 512:
                                      op_ * L + (lc + 1) * 512]
                            mk.op("vector", lambda v, xs=xs:
                                  v.tensor_tensor(xs, xs, mbuf[:, :],
                                                  ALU.add))
                rg_olc.rel(mm)
            ar.pop()
            barrier()

        if upto == "ffn":
            ar.push()
            sq_f = ar.alloc([128, 2 * 512], BF16, "sq_f")
            aa_f = ar.alloc([128, 512], F32, "aa_f")
            bb_f = ar.alloc([128, 512], F32, "bb_f")
            xln_f = ar.alloc([128, NKT * 512], BF16, "xln_f")
            hA = ar.alloc([128, 32 * 512], BF16, "hA")
            hB = ar.alloc([128, 32 * 512], BF16, "hB")
            wsh = ar.alloc([128, 2 * 4096], BF16, "wsh")
            ft1 = ar.alloc([128, 2 * 512], BF16, "ft1")
            ft2 = ar.alloc([128, 2 * 512], BF16, "ft2")
            w1s_sb = ar.alloc([128, 32], F32, "w1s")
            w2s_sb = ar.alloc([128, 32], F32, "w2s")
            h1s_sb = ar.alloc([128, 32], F32, "h1s")
            h2s_sb = ar.alloc([128, 32], F32, "h2s")
            w3s_sb = ar.alloc([128, 8], F32, "w3s")

            sem_ffc = mk.dma_sem("ffc")
            for dst, src, n in ((w1s_sb, f_w1s, 32), (w2s_sb, f_w2s, 32),
                                (h1s_sb, f_h1s, 32), (h2s_sb, f_h2s, 32),
                                (w3s_sb, f_w3s, 8)):
                for t_ in range(n):
                    mk.dma("sync", lambda s, dst=dst, src=src, t_=t_:
                           s.dma_start(out=dst[:, t_:t_ + 1],
                                       in_=src[t_ * 128:(t_ + 1) * 128,
                                               0:1]),
                           sem_ffc)
            v_ffc = (sem_ffc, mk.dmaval[id(sem_ffc)])
            mk.wait_mark("scalar", v_ffc)
            mk.wait_mark("vector", v_ffc)

            PB = [ps.bank(0, F32, 512, "ap0"), ps.bank(1, F32, 512, "ap1"),
                  ps.bank(3, F32, 512, "av1"), ps.bank(4, F32, 512, "av2")]
            m_pb = [(mk.prog["vector"], 0)] * 4
            sem_fh = [mk.dma_sem(f"fh{i}") for i in range(2)]
            rg_fh = Ring(mk, 2)
            m_s4mm = (mk.prog["tensor"], 0)
            m_s3mm = (mk.prog["tensor"], 0)

            for q_ in range(4):
                mln = emit_ln(q_ * 512, xln_f, sq_f, aa_f, bb_f, rg_sq)
                mk.wait_mark("tensor", mln)
                mk.wait_mark("vector", m_s4mm)
                for ot in range(32):
                    wi = rg_fh.acq("sync")
                    dsl = wsh[:, wi * 4096:wi * 4096 + 2048]
                    mk.dma("sync", lambda s, ot=ot, dsl=dsl:
                           s.dma_start(out=dsl[:, 0:1024],
                                       in_=pk_fw1[ot]), sem_fh[wi])
                    md = mk.dma("sync", lambda s, ot=ot, dsl=dsl:
                                s.dma_start(out=dsl[:, 1024:2048],
                                            in_=pk_fw2[ot]), sem_fh[wi])
                    rg_fh.fin(md)
                    rg_fh.rdy("tensor")
                    b1, b2 = (0, 1) if ot % 2 == 0 else (2, 3)
                    mk.wait_mark("tensor", m_pb[b1])
                    mk.wait_mark("tensor", m_pb[b2])
                    for kc in range(NKT):
                        mk.op("tensor", lambda t, b1=b1, kc=kc, dsl=dsl:
                              t.matmul(PB[b1][:, 0:512],
                                       dsl[:, kc * 128:(kc + 1) * 128],
                                       xln_f[:, kc * 512:(kc + 1) * 512],
                                       start=(kc == 0),
                                       stop=(kc == NKT - 1)))
                    for kc in range(NKT):
                        mm = mk.op(
                            "tensor", lambda t, b2=b2, kc=kc, dsl=dsl:
                            t.matmul(PB[b2][:, 0:512],
                                     dsl[:, 1024 + kc * 128:
                                         1024 + (kc + 1) * 128],
                                     xln_f[:, kc * 512:(kc + 1) * 512],
                                     start=(kc == 0),
                                     stop=(kc == NKT - 1)))
                    rg_fh.rel(mm)
                    mk.wait_mark("scalar", mm)
                    si = ot % 2
                    f1 = ft1[:, si * 512:(si + 1) * 512]
                    f2 = ft2[:, si * 512:(si + 1) * 512]
                    mk.op("scalar", lambda s, f1=f1, b1=b1, ot=ot:
                          s.activation(f1, PB[b1][:, 0:512], AF.Silu,
                                       scale=w1s_sb[:, ot:ot + 1]))
                    msc = mk.op("scalar", lambda s, f2=f2, b2=b2, ot=ot:
                                s.activation(f2, PB[b2][:, 0:512],
                                             AF.Identity,
                                             scale=w2s_sb[:, ot:ot + 1]))
                    mk.wait_mark("vector", msc)
                    mtt = mk.op("vector", lambda v, f1=f1, f2=f2, ot=ot:
                                v.tensor_tensor(
                                    hA[:, ot * 512:(ot + 1) * 512],
                                    f1, f2, ALU.mult))
                    m_pb[b1] = mtt
                    m_pb[b2] = mtt
                m_hA = mtt
                mk.wait_mark("tensor", m_hA)
                mk.wait_mark("scalar", m_s3mm)
                for ot in range(32):
                    wi = rg_fh.acq("sync")
                    dsl = wsh[:, wi * 4096:(wi + 1) * 4096]
                    md = mk.dma("sync", lambda s, ot=ot, dsl=dsl:
                                s.dma_start(out=dsl, in_=pk_fh1[ot]),
                                sem_fh[wi])
                    rg_fh.fin(md)
                    rg_fh.rdy("tensor")
                    bi = ot % 4
                    mk.wait_mark("tensor", m_pb[bi])
                    for kc in range(32):
                        mm = mk.op(
                            "tensor", lambda t, bi=bi, kc=kc, dsl=dsl:
                            t.matmul(PB[bi][:, 0:512],
                                     dsl[:, kc * 128:(kc + 1) * 128],
                                     hA[:, kc * 512:(kc + 1) * 512],
                                     start=(kc == 0), stop=(kc == 31)))
                    rg_fh.rel(mm)
                    mk.wait_mark("scalar", mm)
                    msg = mk.op("scalar", lambda s, bi=bi, ot=ot:
                                s.activation(
                                    hB[:, ot * 512:(ot + 1) * 512],
                                    PB[bi][:, 0:512], AF.Gelu,
                                    scale=h1s_sb[:, ot:ot + 1]))
                    m_pb[bi] = msg
                m_hB = msg
                mk.wait_mark("tensor", m_hB)
                for ot in range(32):
                    wi = rg_fh.acq("sync")
                    dsl = wsh[:, wi * 4096:(wi + 1) * 4096]
                    md = mk.dma("sync", lambda s, ot=ot, dsl=dsl:
                                s.dma_start(out=dsl, in_=pk_fh2[ot]),
                                sem_fh[wi])
                    rg_fh.fin(md)
                    rg_fh.rdy("tensor")
                    bi = ot % 4
                    mk.wait_mark("tensor", m_pb[bi])
                    for kc in range(32):
                        mm = mk.op(
                            "tensor", lambda t, bi=bi, kc=kc, dsl=dsl:
                            t.matmul(PB[bi][:, 0:512],
                                     dsl[:, kc * 128:(kc + 1) * 128],
                                     hB[:, kc * 512:(kc + 1) * 512],
                                     start=(kc == 0), stop=(kc == 31)))
                    m_s3mm = mm
                    rg_fh.rel(mm)
                    mk.wait_mark("scalar", mm)
                    msg = mk.op("scalar", lambda s, bi=bi, ot=ot:
                                s.activation(
                                    hA[:, ot * 512:(ot + 1) * 512],
                                    PB[bi][:, 0:512], AF.Gelu,
                                    scale=h2s_sb[:, ot:ot + 1]))
                    m_pb[bi] = msg
                m_hA2 = msg
                mk.wait_mark("tensor", m_hA2)
                for ot in range(NKT):
                    wi = rg_fh.acq("sync")
                    dsl = wsh[:, wi * 4096:(wi + 1) * 4096]
                    md = mk.dma("sync", lambda s, ot=ot, dsl=dsl:
                                s.dma_start(out=dsl, in_=pk_fw3[ot]),
                                sem_fh[wi])
                    rg_fh.fin(md)
                    rg_fh.rdy("tensor")
                    bi = ot % 4
                    mk.wait_mark("tensor", m_pb[bi])
                    for kc in range(32):
                        mm = mk.op(
                            "tensor", lambda t, bi=bi, kc=kc, dsl=dsl:
                            t.matmul(PB[bi][:, 0:512],
                                     dsl[:, kc * 128:(kc + 1) * 128],
                                     hA[:, kc * 512:(kc + 1) * 512],
                                     start=(kc == 0), stop=(kc == 31)))
                    m_s4mm = mm
                    rg_fh.rel(mm)
                    mk.wait_mark("vector", mm)
                    xs = x_fm[:, ot * L + q_ * 512:
                              ot * L + (q_ + 1) * 512]
                    mfr = mk.op("vector", lambda v, bi=bi, ot=ot, xs=xs:
                                v.scalar_tensor_tensor(
                                    xs, PB[bi][:, 0:512],
                                    w3s_sb[:, ot:ot + 1], xs,
                                    ALU.mult, ALU.add))
                    m_pb[bi] = mfr
            ar.pop()
            barrier()

        ar.push()
        otok = ar.alloc([128, 2 * D], F32, "otok")
        EP = [ps.bank(0, F32, 512, "EP0"), ps.bank(1, F32, 512, "EP1")]
        rg_ep = Ring(mk, 2)
        rg_otok = Ring(mk, 2)
        sem_ost = [mk.dma_sem(f"ost{i}") for i in range(2)]
        for lt in range(16):
            oi = rg_otok.acq("tensor")
            buf = otok[:, oi * D:oi * D + D]
            for kt in range(NKT):
                bi = rg_ep.acq("tensor")
                pb = EP[bi]
                mt = mk.op("tensor", lambda t, lt=lt, kt=kt, pb=pb:
                           t.transpose(pb[:, 0:128],
                                       x_fm[:, kt * L + lt * 128:
                                            kt * L + (lt + 1) * 128],
                                       eye_f32[:, :]))
                rg_ep.fin(mt)
                rg_ep.rdy("vector")
                mv = mk.op("vector", lambda v, buf=buf, kt=kt, pb=pb:
                           v.tensor_copy(buf[:, kt * 128:(kt + 1) * 128],
                                         pb[:, 0:128]))
                rg_ep.rel(mv)
            rg_otok.fin(mv)
            rg_otok.rdy("sync")
            mst = mk.dma("sync", lambda s, buf=buf, lt=lt:
                         s.dma_start(out=y_out[lt * 128:(lt + 1) * 128, :],
                                     in_=buf),
                         sem_ost[oi])
            rg_otok.rel(mst)
        ar.pop()

        for sp_ in sem_ost + [sem_store]:
            mk.wait_mark("sync", (sp_, mk.dmaval[id(sp_)]))
        mk.check_deadlock()
        nins = mk.ninstr
        peak = ar.peak
        with nc.Block() as block:
            mk.replay(block)

    meta = dict(ninstr=nins, sbuf_peak=peak, taps=taps_declared,
                nsem=mk.nsem)
    return nc, meta


def _constants():
    bf = ml_dtypes.bfloat16
    c = {}
    c["c_ones_col"] = np.ones((128, 1), dtype=bf)
    c["c_ones_colf"] = np.ones((128, 1), dtype=np.float32)
    c["c_ones_row"] = np.ones((1, 128), dtype=bf)
    c["c_eye"] = np.eye(128, dtype=bf)
    c["c_eye_f32"] = np.eye(128, dtype=np.float32)
    sel = np.zeros((16, 16 * 128), dtype=bf)
    for n in range(16):
        sel[n, n * 128:(n + 1) * 128] = 1
    c["c_sel16"] = sel
    scal = np.zeros((128, 8), dtype=np.float32)
    scal[:, 0] = 1e-5
    scal[:, 1] = -0.5
    scal[:, 2] = 0.125
    scal[:, 3] = 0.5
    c["c_scal"] = scal
    c["c_ones512"] = np.ones((1, 512), dtype=bf)
    c["c_ones_rowf"] = np.ones((1, 128), dtype=np.float32)
    i = np.arange(16, dtype=np.float64)
    inv_freq = 1.0 / 10000.0 ** (i / 256.0)
    t = np.arange(L, dtype=np.float64)
    freqs = np.outer(t, inv_freq)
    cosb = np.cos(freqs).astype(np.float32).astype(bf).astype(np.float32)
    sinb = np.sin(freqs).astype(np.float32).astype(bf).astype(np.float32)
    p = np.arange(128)
    cs1 = cosb[:, p % 16].T
    sgn = np.where((p % 32) < 16, -1.0, 1.0)[:, None]
    cs2 = sinb[:, p % 16].T * sgn
    c["c_cs1"] = np.ascontiguousarray(cs1.astype(bf))
    c["c_cs2"] = np.ascontiguousarray(cs2.astype(bf))
    hm = np.zeros((128, 4), dtype=bf)
    for j in range(4):
        hm[j * 32:(j + 1) * 32, j] = 1
    c["c_hmask"] = hm
    return c


def make_in_maps(inputs):
    consts = _constants()
    x_full = np.ascontiguousarray(inputs["L0_attn_x"])
    base = dict(
        m_inproj=inputs["L0_attn_in_proj_w"],
        m_convw=inputs["L0_attn_conv_w"].reshape(DI, 4),
        m_convb=inputs["L0_attn_conv_b"].reshape(DI, 1),
        m_xproj=inputs["L0_attn_x_proj_w"],
        m_dtw=inputs["L0_attn_dt_proj_w"],
        m_dtb=inputs["L0_attn_dt_proj_b"].reshape(DI, 1),
        m_alog=inputs["L0_attn_A_log"],
        m_dvec=inputs["L0_attn_D"].reshape(DI, 1),
        m_outproj=inputs["L0_attn_out_proj_w"],
        e_wr=inputs["L0_ffn_Wr"],
        e_br=inputs["L0_ffn_br"].reshape(8, 1),
        e_w1=inputs["L0_ffn_W1"],
        e_b1=inputs["L0_ffn_b1"].reshape(8 * 4096, 1),
        e_w2=inputs["L0_ffn_W2"],
        e_b2=inputs["L0_ffn_b2"],
        a_wq=inputs["L1_attn_Wq"],
        a_wk=inputs["L1_attn_Wk"],
        a_wv=inputs["L1_attn_Wv"],
        a_lq1=inputs["L1_attn_lambda_q1"].reshape(64, 1),
        a_lq2=inputs["L1_attn_lambda_q2"].reshape(64, 1),
        a_lk1=inputs["L1_attn_lambda_k1"].reshape(64, 1),
        a_lk2=inputs["L1_attn_lambda_k2"].reshape(64, 1),
        a_wproj=inputs["L1_attn_Wproj"],
        a_alpha=inputs["L1_attn_alpha"].reshape(1, 1),
        f_w1=inputs["L1_ffn_w1_w"],
        f_w1s=inputs["L1_ffn_w1_s"].reshape(4096, 1),
        f_w2=inputs["L1_ffn_w2_w"],
        f_w2s=inputs["L1_ffn_w2_s"].reshape(4096, 1),
        f_h1=inputs["L1_ffn_h1_w"],
        f_h1s=inputs["L1_ffn_h1_s"].reshape(4096, 1),
        f_h2=inputs["L1_ffn_h2_w"],
        f_h2s=inputs["L1_ffn_h2_s"].reshape(4096, 1),
        f_w3=inputs["L1_ffn_w3_w"],
        f_w3s=inputs["L1_ffn_w3_s"].reshape(D, 1),
        **consts,
    )
    base = {k: np.ascontiguousarray(v) for k, v in base.items()}
    in_maps = []
    for core in range(8):
        m = dict(base)
        m["x"] = np.ascontiguousarray(x_full[core])
        in_maps.append(m)
    return in_maps


_PROG_CACHE = {}


def _get_prog(key=()):
    if key not in _PROG_CACHE:
        _PROG_CACHE[key] = build_program(dbg_taps=key)
    return _PROG_CACHE[key]


def kernel(**inputs):
    nc, meta = _get_prog()
    in_maps = make_in_maps(inputs)
    res = run_bass_kernel_spmd(nc, in_maps, core_ids=list(range(8)))
    out = np.stack([res.results[i]["y"] for i in range(8)], axis=0)
    return out.astype(np.float32)


if __name__ == "__main__":
    import time
    t0 = time.time()
    nc, meta = build_program()
    print("built:", meta, f"{time.time() - t0:.1f}s")
